# version 32
# baseline (speedup 1.0000x reference)
import math
import numpy as np
import ml_dtypes
import concourse.bass as bass
import concourse.mybir as mybir
from concourse.bass_utils import run_bass_kernel_spmd
from contextlib import ExitStack

F32 = mybir.dt.float32
BF16 = mybir.dt.bfloat16
AF = mybir.ActivationFunctionType
ALU = mybir.AluOpType
AX = mybir.AxisListType
bfloat16 = ml_dtypes.bfloat16

NCORES = 8
D = 2048
S = 2048
B = 4
TOWN = 1024
NH = 8
DH = 128
INTER = 5632
NI = INTER // 128
EPS_LN = 1e-5
EPS_L1 = 1e-6

ENGINES = ("sync", "scalar", "vector", "tensor", "gpsimd")


class Sched:
    def __init__(self, nc, stack):
        self.nc = nc
        self.sems = {}
        self.cnt = {}
        self.queues = {e: [] for e in ENGINES}
        self.last_wait = {e: {} for e in ENGINES}
        self._stack = stack
        for e in ENGINES:
            self._mksem("e_" + e)
        for d in ("dW", "dX", "dO", "dC", "dS", "wA", "wB", "wC", "uA", "uB",
                  "xA", "xB", "sA", "sB", "dT", "dY", "dR", "cc2",
                  "cc1_0", "cc1_1", "cc1_2", "cc1_3",
                  "cc3_0", "cc3_1", "cc3_2", "cc3_3",
                  "cc2_0", "cc2_1", "cc2_2", "cc2_3"):
            self._mksem(d)

    def _mksem(self, name):
        self.sems[name] = self._stack.enter_context(self.nc.semaphore(name))
        self.cnt[name] = 0

    def now(self, eng):
        return ("e_" + eng, self.cnt["e_" + eng])

    def _wait(self, eng, deps, strict=False):
        for dep in deps:
            if dep is None:
                continue
            sem, val = dep
            if val <= 0 or (not strict and sem == "e_" + eng):
                continue
            lw = self.last_wait[eng]
            if lw.get(sem, 0) >= val:
                continue
            lw[sem] = val
            self.queues[eng].append(("wait", sem, val))

    def op(self, eng, fn, deps=(), strict=()):
        self._wait(eng, deps)
        self._wait(eng, strict, strict=True)
        sem = "e_" + eng
        self.cnt[sem] += 1
        self.queues[eng].append(("call", fn, sem, 1))
        return (sem, self.cnt[sem])

    def cc(self, fn, ccsem, deps=()):
        self._wait("gpsimd", deps)
        self.cnt[ccsem] += 1
        self.queues["gpsimd"].append(("call", fn, ccsem, None))
        return (ccsem, 1)

    def dma(self, eng, out, in_, deps=(), sem="dW", strict=()):
        self._wait(eng, deps, strict=True)
        self._wait(eng, strict, strict=True)
        self.cnt[sem] += 16
        self.queues[eng].append(
            ("call", lambda e, o=out, i=in_: e.dma_start(out=o, in_=i), sem, 16))
        return (sem, self.cnt[sem])

    def dyn_dma(self, eng, fn, deps=(), sem="dX"):
        self._wait(eng, deps, strict=True)
        self.cnt[sem] += 16
        self.queues[eng].append(("call", fn, sem, 16))
        return (sem, self.cnt[sem])

    def finalize(self):
        nc = self.nc
        with nc.Block() as block:
            for eng_name in ENGINES:
                items = self.queues[eng_name]

                def make(items=items):
                    def body(eng):
                        for it in items:
                            if it[0] == "wait":
                                eng.wait_ge(self.sems[it[1]], it[2])
                            else:
                                _, fn, sem, amt = it
                                ins = fn(eng)
                                if ins is None:
                                    ins = eng.nop()
                                if amt is None:
                                    ins.then_inc(self.sems[sem])
                                else:
                                    ins.then_inc(self.sems[sem], amt)
                    return body

                getattr(block, eng_name)(make())


class Pool:

    def __init__(self, nc, stack, n=8):
        self.free = []
        for i in range(n):
            t = stack.enter_context(nc.psum_tensor(f"pb{i}", [128, 512], F32))
            self.free.append((i, t, None))
        self.held = {}

    def acquire(self):
        assert self.free, "psum pool exhausted"
        i, t, ev = self.free.pop(0)
        self.held[i] = t
        return i, t, ev

    def release(self, idx, ev):
        t = self.held.pop(idx)
        self.free.append((idx, t, ev))


class Ctx:
    def __init__(self, nc, s, pool, stack):
        self.nc = nc
        self.s = s
        self.pool = pool
        self.stack = stack
        self.consts = {}

    def sbuf(self, name, shape, dtype):
        return self.stack.enter_context(self.nc.sbuf_tensor(name, shape, dtype))



def stage_consts(ctx, rope=False, masks=True):
    nc, s = ctx.nc, ctx.s
    names = [("c_ident", [128, 128], BF16, "ident"),
             ("c_onescol", [128, 1], BF16, "onescol"),
             ("c_onesrow", [1, 128], BF16, "onesrow")]
    if masks:
        names.append(("c_masks", [128, 4 * 512], BF16, "masks"))
    if rope:
        names.append(("c_cos", [128, S], F32, "cos"))
        names.append(("c_srot", [128, S], F32, "srot"))
        names.append(("c_prot", [128, 128], BF16, "prot"))
    for dn, shape, dt_, key in names:
        d = nc.dram_tensor(dn, shape, dt_, kind="ExternalInput")
        sb = ctx.sbuf(key + "_sb", shape, dt_)
        s.dma("sync", sb[:, :], d[:, :], sem="dC")
        ctx.consts[key] = sb
    ctx.consts["ready"] = ("dC", s.cnt["dC"])


def emit_ln_t(ctx, x_dram, ntiles, lnt, lnt_T, xt, lb, stat, sq,
              start_deps=(), t_off=0):
    nc, s, pool = ctx.nc, ctx.s, ctx.pool
    ident = ctx.consts["ident"]
    cready = ctx.consts["ready"]
    x_free = [None, None]
    lb_free = [None] * 4
    sq_free = [None, None]
    prevs = [{}, {}]
    ev_last = None
    apply_evs = []
    for g in range((ntiles + 3) // 4):
        n_in_g = min(4, ntiles - g * 4)
        for ii in range(n_in_g):
            i = g * 4 + ii
            xi = i % 2
            b = 8 * xi
            prev = prevs[xi]
            evx = s.dma("scalar" if xi == 0 else "sync",
                        xt[xi][:, :], x_dram[i * 128:(i + 1) * 128, :],
                        deps=[x_free[xi]] + list(start_deps),
                        sem="xA" if xi == 0 else "xB")
            ev_sum = s.op("vector", lambda e, xi=xi, b=b: e.reduce_sum(
                stat[:, b:b + 1], xt[xi][:, :], AX.X), deps=[evx],
                strict=[prev.get("e1")])
            ev_sq0 = s.op("scalar", lambda e, xi=xi: e.activation(
                sq[xi][:, :], xt[xi][:, :], AF.Square), deps=[evx, sq_free[xi]],
                strict=[prev.get("sq0")])
            ev_sq = s.op("vector", lambda e, xi=xi, b=b: e.reduce_sum(
                stat[:, b + 1:b + 2], sq[xi][:, :], AX.X), deps=[ev_sq0],
                strict=[ev_sum, prev.get("e2")])
            sq_free[xi] = ev_sq
            e1 = s.op("vector", lambda e, b=b: e.tensor_scalar_mul(
                stat[:, b + 2:b + 3], stat[:, b:b + 1], 1.0 / D),
                strict=[ev_sum, prev.get("e7")])
            e2 = s.op("vector", lambda e, b=b: e.tensor_scalar_mul(
                stat[:, b + 3:b + 4], stat[:, b + 1:b + 2], 1.0 / D),
                strict=[ev_sq, prev.get("e4")])
            e3 = s.op("vector", lambda e, b=b: e.scalar_tensor_tensor(
                stat[:, b + 7:b + 8], stat[:, b + 2:b + 3], 1.0,
                stat[:, b + 2:b + 3], ALU.bypass, ALU.mult),
                strict=[e1, prev.get("e4")])
            e4 = s.op("vector", lambda e, b=b: e.tensor_sub(
                stat[:, b + 4:b + 5], stat[:, b + 3:b + 4], stat[:, b + 7:b + 8]),
                deps=[prev.get("e5")], strict=[e2, e3])
            e5 = s.op("scalar", lambda e, b=b: e.activation(
                stat[:, b + 5:b + 6], stat[:, b + 4:b + 5], AF.Ln,
                bias=np.float32(EPS_LN)),
                deps=[e4], strict=[prev.get("e8s"), prev.get("e6")])
            e6 = s.op("scalar", lambda e, b=b: e.activation(
                stat[:, b + 5:b + 6], stat[:, b + 5:b + 6], AF.Exp,
                scale=np.float32(-0.5)), strict=[e5])
            e7 = s.op("vector", lambda e, b=b: e.scalar_tensor_tensor(
                stat[:, b + 6:b + 7], stat[:, b + 2:b + 3], -1.0,
                stat[:, b + 5:b + 6], ALU.mult, ALU.mult),
                deps=[e6, prev.get("e8v")], strict=[e1])
            if xi == 0:
                e8 = s.op("scalar", lambda e, xi=xi, ii=ii, b=b: e.activation(
                    lb[ii][:, :], xt[xi][:, :], AF.Identity,
                    bias=stat[:, b + 6:b + 7], scale=stat[:, b + 5:b + 6]),
                    deps=[e7, lb_free[ii]], strict=[e6])
                prev_e8 = dict(e8s=e8)
            else:
                e8 = s.op("vector", lambda e, xi=xi, ii=ii, b=b: e.tensor_scalar(
                    lb[ii][:, :], xt[xi][:, :], stat[:, b + 5:b + 6],
                    stat[:, b + 6:b + 7], ALU.mult, ALU.add),
                    deps=[lb_free[ii], e6], strict=[e7])
                prev_e8 = dict(e8v=e8)
            prevs[xi] = dict(e1=e1, e2=e2, e4=e4, e5=e5, e6=e6, e7=e7,
                             sq0=ev_sq0, **prev_e8)
            x_free[xi] = e8
            apply_evs.append(e8)
        grp_last = None
        for f in range(16):
            bidx, bank, bfree = pool.acquire()
            mm = None
            for ii in range(n_in_g):
                mm = s.op("tensor", lambda e, ii=ii, f=f, bank=bank: e.matmul(
                    bank[:, ii * 128:(ii + 1) * 128],
                    lb[ii][:, f * 128:(f + 1) * 128], ident[:, :],
                    start=True, stop=True),
                    deps=[apply_evs[-n_in_g + ii], bfree, cready])
            eng = "vector" if (f % 2 == 0) else "scalar"
            col0 = f * lnt_T + t_off + g * 512
            w = n_in_g * 128
            if eng == "vector":
                ev = s.op(eng, lambda e, bank=bank, col0=col0, w=w: e.tensor_copy(
                    lnt[:, col0:col0 + w], bank[:, 0:w]), deps=[mm])
            else:
                ev = s.op(eng, lambda e, bank=bank, col0=col0, w=w: e.copy(
                    lnt[:, col0:col0 + w], bank[:, 0:w]), deps=[mm])
            pool.release(bidx, ev)
            grp_last = ev
            ev_last = ev
        for ii in range(n_in_g):
            lb_free[ii] = grp_last
    return ev_last


def gather_w(ctx, w_dram, col0, ncols, wf, deps, sem="wA"):
    nk = w_dram.shape[0] // 128
    view = w_dram[:, :].rearrange("(k p) t -> p k t", p=128)
    return ctx.s.dma("sync", wf[:, 0:nk * ncols].rearrange("p (k c) -> p k c", k=nk),
                     view[:, :, col0:col0 + ncols], deps=deps, sem=sem)


def conv_w(ctx, wf, wb, n, eng, deps):
    if eng == "vector":
        return ctx.s.op("vector", lambda e: e.tensor_copy(wb[:, 0:n], wf[:, 0:n]),
                        deps=deps)
    return ctx.s.op("scalar", lambda e: e.copy(wb[:, 0:n], wf[:, 0:n]), deps=deps)


def emit_proj_fm(ctx, lnt, lnt_T, Tq, w_dram, O, out, out_T, wfbufs, wbbufs,
                 bias_cols=None, rope=None, ready=(), tag="q"):
    nc, s, pool = ctx.nc, ctx.s, ctx.pool
    nk = D // 128
    noc = O // 128
    wfree = [None, None]
    if rope is not None:
        qc, qx, qr, qe = rope["bufs"]
        cos, srot = ctx.consts["cos"], ctx.consts["srot"]
        prot = ctx.consts["prot"]
        rfree = None
    for oc in range(noc):
        wi = oc % 2
        wf, wb = wfbufs[wi], wbbufs[wi]
        ev_d = gather_w(ctx, w_dram, oc * 128, 128, wf,
                        [wfree[wi]] + (list(ready) if oc < 2 else []),
                        sem="wA" if wi == 0 else "wB")
        ev_w = conv_w(ctx, wf, wb, nk * 128, "vector" if wi else "scalar",
                      [ev_d] + (list(ready) if oc < 2 else []))
        last_mm = None
        for ts in range(Tq // 512):
            bidx, bank, bfree = pool.acquire()
            mm = None
            for k in range(nk):
                mm = s.op("tensor", lambda e, k=k, ts=ts, bank=bank, wb=wb: e.matmul(
                    bank[:, :], wb[:, k * 128:(k + 1) * 128],
                    lnt[:, k * lnt_T + ts * 512: k * lnt_T + ts * 512 + 512],
                    start=(k == 0), stop=(k == nk - 1)),
                    deps=([ev_w, bfree] + list(ready)) if k == 0 else [])
            last_mm = mm
            col0 = oc * out_T + ts * 512
            if rope is None:
                if bias_cols is not None:
                    ev = s.op("scalar", lambda e, bank=bank, col0=col0, oc=oc: e.activation(
                        out[:, col0:col0 + 512], bank[:, :], AF.Identity,
                        bias=bias_cols[:, oc:oc + 1]), deps=[mm])
                else:
                    ev = s.op("scalar", lambda e, bank=bank, col0=col0: e.copy(
                        out[:, col0:col0 + 512], bank[:, :]), deps=[mm])
                pool.release(bidx, ev)
            else:
                t0 = ts * 512
                e1 = s.op("vector", lambda e, bank=bank, t0=t0: e.scalar_tensor_tensor(
                    qc[:, :], bank[:, :], 1.0, cos[:, t0:t0 + 512],
                    ALU.bypass, ALU.mult), deps=[mm, rfree])
                e2 = s.op("vector", lambda e, bank=bank, t0=t0: e.scalar_tensor_tensor(
                    qx[:, :], bank[:, :], 1.0, srot[:, t0:t0 + 512],
                    ALU.bypass, ALU.mult), strict=[e1])
                pool.release(bidx, e2)
                bR_idx, bR, bR_free = pool.acquire()
                mmrot = s.op("tensor", lambda e, bR=bR: e.matmul(
                    bR[:, :], prot[:, :], qx[:, :], start=True, stop=True),
                    deps=[e2, bR_free])
                e3 = s.op("vector", lambda e, bR=bR: e.tensor_add(
                    qr[:, :], qc[:, :], bR[:, :]), deps=[mmrot])
                pool.release(bR_idx, e3)
                e4 = s.op("vector", lambda e: e.tensor_scalar_min(
                    qc[:, :], qr[:, :], 0.0), strict=[e3])
                e5 = s.op("scalar", lambda e: e.activation(
                    qe[:, :], qc[:, :], AF.Exp), deps=[e4])
                ev = s.op("vector", lambda e, col0=col0: e.scalar_tensor_tensor(
                    out[:, col0:col0 + 512], qr[:, :], 0.0, qe[:, :],
                    ALU.max, ALU.add), deps=[e5], strict=[e4])
                rfree = ev
        wfree[wi] = last_mm
    return s.now("tensor")


def emit_proj_tm(ctx, lnt, lnt_T, Tq, w_dram, O, out, out_C, wf, wbig,
                 bias_row=None, evac=None, ready=(), ready_mm=None, tag="v"):
    nc, s, pool = ctx.nc, ctx.s, ctx.pool
    onesrow = ctx.consts["onesrow"]
    nk = D // 128
    ntc = Tq // 128
    nos = max(1, O // 512)
    ow = min(512, O)
    wf_free = [None, None]
    wbig_free = None
    ev_out = None
    prev_conv = None
    for os_ in range(nos):
        convs = []
        for half in range(2):
            wfh = wf[half % len(wf)]
            view = w_dram[:, :].rearrange("(k p) t -> p k t", p=128)
            ev_d = s.dma("sync", wfh[:, 0:8 * ow].rearrange("p (k c) -> p k c", k=8),
                         view[:, half * 8:half * 8 + 8, os_ * ow:os_ * ow + ow],
                         deps=[wf_free[half], prev_conv] +
                              (list(ready) if os_ == 0 else []),
                         sem="wA")
            ev_c = s.op("vector" if half else "scalar",
                        (lambda e, wfh=wfh, half=half: e.tensor_copy(
                            wbig[:, half * 8 * ow:(half + 1) * 8 * ow], wfh[:, 0:8 * ow]))
                        if half else
                        (lambda e, wfh=wfh, half=half: e.copy(
                            wbig[:, half * 8 * ow:(half + 1) * 8 * ow], wfh[:, 0:8 * ow])),
                        deps=[ev_d, wbig_free])
            convs.append(ev_c)
            prev_conv = ev_c
        for tc in range(ntc):
            bidx, bank, bfree = pool.acquire()
            mm = None
            for k in range(nk):
                mm = s.op("tensor", lambda e, k=k, tc=tc, bank=bank: e.matmul(
                    bank[:, 0:ow], lnt[:, k * lnt_T + tc * 128: k * lnt_T + tc * 128 + 128],
                    wbig[:, k * ow:(k + 1) * ow],
                    start=(k == 0), stop=(k == nk - 1 and bias_row is None)),
                    deps=(convs + [bfree] +
                          list(ready if ready_mm is None else ready_mm))
                    if k == 0 else [])
            if bias_row is not None:
                mm = s.op("tensor", lambda e, bank=bank, os_=os_: e.matmul(
                    bank[:, 0:ow], onesrow[0:1, :],
                    bias_row[0:1, os_ * ow:os_ * ow + ow],
                    start=False, stop=True), deps=[mm])
            col0 = tc * out_C + os_ * ow
            if evac is None:
                ev = s.op("scalar", lambda e, bank=bank, col0=col0: e.copy(
                    out[:, col0:col0 + ow], bank[:, 0:ow]), deps=[mm])
            else:
                ev = evac(tc, os_, bank, mm, col0)
            pool.release(bidx, ev)
            ev_out = ev
            if tc == ntc - 1:
                wbig_free = mm
                wf_free = [mm, mm]
    return ev_out


def emit_attention(ctx, QT, KT, Vtok, OT, kv_of_head, vcols, mode, esb, ready,
                   tag="a0", mid_cb=None):
    nc, s, pool = ctx.nc, ctx.s, ctx.pool
    onescol = ctx.consts["onescol"]
    onesrow = ctx.consts["onesrow"]
    masks = ctx.consts["masks"]
    ident = ctx.consts["ident"]
    den = [ctx.sbuf(f"{tag}_den{i}", [1, 512], F32) for i in range(2)]
    recb = [ctx.sbuf(f"{tag}_recb{i}", [1, 512], BF16) for i in range(2)]
    bcs = [ctx.sbuf(f"{tag}_bcs{i}", [128, 512], F32) for i in range(2)]
    esb_free = [None] * 32
    den_free = [None, None]
    recb_free = [None, None]
    bcs_free = [None, None]
    scale = np.float32(1.0 / math.sqrt(DH))
    pairs = [(h, Q) for Q in range(4) for h in range(NH)]
    state = {}

    def stage1(i):
        h, Q = pairs[i]
        par = i % 2
        kv = kv_of_head(h)
        jmax = 4 * Q + 4
        eoff = 16 * (i % 2)
        bU_idx, bU, bU_free = pool.acquire()
        bD_idx, bD, bD_free = pool.acquire()
        mmu = mmden = None

        def emit_S(j):
            bS_idx, bS, bS_free = pool.acquire()
            delta = j - 4 * Q
            mm = s.op("tensor", lambda e, h=h, kv=kv, Q=Q, j=j, bS=bS: e.matmul(
                bS[:, :], KT[:, kv * S + j * 128: kv * S + j * 128 + 128],
                QT[:, h * S + Q * 512: h * S + Q * 512 + 512],
                start=True, stop=(delta < 0)), deps=[bS_free] + ready)
            if delta >= 0:
                mm = s.op("tensor", lambda e, bS=bS, delta=delta: e.matmul(
                    bS[:, :], ident[:, :],
                    masks[:, delta * 512:delta * 512 + 512],
                    start=False, stop=True), deps=[mm])
            return (bS_idx, bS, mm)

        spend = [emit_S(jj) for jj in range(min(2, jmax))]
        for j in range(jmax):
            bS_idx, bS, mm = spend.pop(0)
            if j + 2 < jmax:
                spend.append(emit_S(j + 2))
            delta = j - 4 * Q
            je = j + eoff
            ef = esb_free[je] or []
            if mode == "softmax":
                ev_e = s.op("scalar", lambda e, bS=bS, je=je: e.activation(
                    esb[je][:, :], bS[:, :], AF.Exp, scale=scale),
                    deps=[mm], strict=ef)
            else:
                if delta >= 0:
                    ev_e = s.op("scalar", lambda e, bS=bS, je=je: e.activation(
                        esb[je][:, :], bS[:, :], AF.Relu),
                        deps=[mm], strict=ef)
                else:
                    ev_e = s.op("scalar", lambda e, bS=bS, je=je: e.copy(
                        esb[je][:, :], bS[:, :]), deps=[mm], strict=ef)
            pool.release(bS_idx, ev_e)
            first = j == 0
            mmden = s.op("tensor", lambda e, je=je, bD=bD, first=first, jmax=jmax, j=j: e.matmul(
                bD[0:1, :], onescol[:, 0:1], esb[je][:, :],
                start=first, stop=(j == jmax - 1)),
                deps=[ev_e] + ([bD_free] if first else []))
            mmu = s.op("tensor", lambda e, j=j, je=je, kv=kv, bU=bU, first=first, jmax=jmax, vcols=vcols: e.matmul(
                bU[:, :], Vtok[:, j * vcols + kv * 128: j * vcols + kv * 128 + 128],
                esb[je][:, :], start=first, stop=(j == jmax - 1)),
                deps=[ev_e] + ([bU_free] if first else []))
            esb_free[je] = [mmu]
        state[i] = (h, Q, par, bU_idx, bU, mmu, bD_idx, bD, mmden)

    def stage2(i):
        h, Q, par, bU_idx, bU, mmu, bD_idx, bD, mmden = state.pop(i)
        bias = np.float32(EPS_L1) if mode == "linear" else np.float32(0.0)
        e1 = s.op("scalar", lambda e, par=par, bD=bD, bias=bias: e.activation(
            den[par][0:1, :], bD[0:1, :], AF.Ln, bias=bias),
            deps=[mmden], strict=[den_free[par]])
        pool.release(bD_idx, e1)
        e3 = s.op("scalar", lambda e, par=par: e.activation(
            recb[par][0:1, :], den[par][0:1, :], AF.Exp, scale=np.float32(-1.0)),
            deps=[recb_free[par]], strict=[e1])
        den_free[par] = e3
        bB_idx, bB, bB_free = pool.acquire()
        e4 = s.op("tensor", lambda e, par=par, bB=bB: e.matmul(
            bB[:, :], onesrow[0:1, :], recb[par][0:1, :], start=True, stop=True),
            deps=[e3, bB_free])
        recb_free[par] = e4
        e5 = s.op("scalar", lambda e, par=par, bB=bB: e.copy(
            bcs[par][:, :], bB[:, :]), deps=[e4, bcs_free[par]])
        pool.release(bB_idx, e5)
        ev = s.op("vector", lambda e, h=h, Q=Q, par=par, bU=bU: e.scalar_tensor_tensor(
            OT[:, h * S + Q * 512: h * S + Q * 512 + 512], bU[:, :], 1.0,
            bcs[par][:, :], ALU.bypass, ALU.mult), deps=[mmu, e5])
        bcs_free[par] = ev
        pool.release(bU_idx, ev)

    gsz = len(pairs) // 4
    for g in range(4):
        for i in range(g * gsz, (g + 1) * gsz):
            stage1(i)
            if i > g * gsz:
                stage2(i - 1)
        stage2((g + 1) * gsz - 1)
        if mid_cb is not None and g < 3:
            mid_cb(s.now("vector"))
    return s.now("vector")



def build_l0attn():
    nc = bass.Bass("TRN2", target_bir_lowering=False)
    stack = ExitStack()
    xfull = nc.dram_tensor("xfull", [S, D], F32, kind="ExternalInput")
    wq = nc.dram_tensor("wq", [D, 1024], F32, kind="ExternalInput")
    wk = nc.dram_tensor("wk", [D, 1024], F32, kind="ExternalInput")
    wv = nc.dram_tensor("wv", [D, 1024], F32, kind="ExternalInput")
    bqc = nc.dram_tensor("bqc", [128, 8], F32, kind="ExternalInput")
    bkc = nc.dram_tensor("bkc", [128, 8], F32, kind="ExternalInput")
    bvr = nc.dram_tensor("bvr", [1, 1024], F32, kind="ExternalInput")
    oat = nc.dram_tensor("oat", [1024, S], BF16, kind="ExternalOutput")

    with stack:
        s = Sched(nc, stack)
        pool = Pool(nc, stack)
        ctx = Ctx(nc, s, pool, stack)
        stage_consts(ctx, masks=False)
        bq_sb = ctx.sbuf("bq_sb", [128, 8], F32)
        bk_sb = ctx.sbuf("bk_sb", [128, 8], F32)
        bvf = ctx.sbuf("bvf", [1, 1024], F32)
        bvb = ctx.sbuf("bvb", [1, 1024], BF16)
        s.dma("sync", bq_sb[:, :], bqc[:, :], sem="dC")
        s.dma("sync", bk_sb[:, :], bkc[:, :], sem="dC")
        evb = s.dma("sync", bvf[:, :], bvr[:, :], sem="dC")
        s.op("scalar", lambda e: e.copy(bvb[:, :], bvf[:, :]), deps=[evb])
        ctx.consts["ready"] = ("dC", s.cnt["dC"])

        QT = ctx.sbuf("QT", [128, NH * S], BF16)
        KT = ctx.sbuf("KT", [128, NH * S], BF16)
        VT = ctx.sbuf("VT", [128, (S // 128) * 1024], BF16)

        with (nc.sbuf_tensor("lnt1", [128, 16 * S], BF16) as lnt1,):
            with (nc.sbuf_tensor("xt0", [128, D], F32) as xt0,
                  nc.sbuf_tensor("xt1", [128, D], F32) as xt1,
                  nc.sbuf_tensor("lb0", [128, D], BF16) as lb0,
                  nc.sbuf_tensor("lb1", [128, D], BF16) as lb1,
                  nc.sbuf_tensor("lb2", [128, D], BF16) as lb2,
                  nc.sbuf_tensor("lb3", [128, D], BF16) as lb3,
                  nc.sbuf_tensor("stat", [128, 16], F32) as stat,
                  nc.sbuf_tensor("sqb0", [128, D], BF16) as sqb0,
                  nc.sbuf_tensor("sqb1", [128, D], BF16) as sqb1):
                emit_ln_t(ctx, xfull, S // 128, lnt1, S, [xt0, xt1],
                          [lb0, lb1, lb2, lb3], stat, [sqb0, sqb1])
            ln_done = [s.now("tensor"), s.now("vector"), s.now("scalar")]
            with (nc.sbuf_tensor("vwf0", [128, 8 * 512], F32) as vwf0,
                  nc.sbuf_tensor("vwb", [128, 16 * 512], BF16) as vwb):
                ev_v = emit_proj_tm(ctx, lnt1, S, S, wv, 1024, VT, 1024,
                                    [vwf0], vwb, bias_row=bvb,
                                    ready=ln_done, tag="v")
            v_done = [s.now("tensor")]
            with (nc.sbuf_tensor("qwf0", [128, 16 * 128], F32) as qwf0,
                  nc.sbuf_tensor("qwf1", [128, 16 * 128], F32) as qwf1,
                  nc.sbuf_tensor("qwb0", [128, 16 * 128], BF16) as qwb0,
                  nc.sbuf_tensor("qwb1", [128, 16 * 128], BF16) as qwb1):
                emit_proj_fm(ctx, lnt1, S, S, wq, 1024, QT, S,
                             [qwf0, qwf1], [qwb0, qwb1], bias_cols=bq_sb,
                             ready=ln_done + v_done, tag="q")
                emit_proj_fm(ctx, lnt1, S, S, wk, 1024, KT, S,
                             [qwf0, qwf1], [qwb0, qwb1], bias_cols=bk_sb,
                             ready=ln_done + [s.now("tensor")], tag="k")

        md = nc.dram_tensor("c_masks", [128, 4 * 512], BF16, kind="ExternalInput")
        masks_sb = ctx.sbuf("masks_sb", [128, 4 * 512], BF16)
        evm = s.dma("sync", masks_sb[:, :], md[:, :],
                    deps=[s.now("tensor")], sem="dC")
        ctx.consts["masks"] = masks_sb
        OT = ctx.sbuf("OT", [128, NH * S], BF16)
        esb = [ctx.sbuf(f"esb{j}", [128, 512], BF16) for j in range(32)]
        ready = [s.now("scalar"), s.now("vector"), evm]
        ev_a = emit_attention(ctx, QT, KT, VT, OT, lambda h: h, 1024, "softmax",
                              esb, ready)
        ev = s.dma("scalar", oat[:, :].rearrange("(h p) t -> p h t", p=128),
                   OT[:, :].rearrange("p (h t) -> p h t", h=NH),
                   deps=[ev_a], sem="dO")
        s.op("sync", lambda e: None, deps=[ev])
        s.finalize()
    return nc


def build_wo(layer):
    nc = bass.Bass("TRN2", target_bir_lowering=False)
    stack = ExitStack()
    otf = nc.dram_tensor("otf", [D, TOWN], BF16, kind="ExternalInput")
    wo = nc.dram_tensor("wo", [D, D], F32, kind="ExternalInput")
    xres = nc.dram_tensor("xres", [TOWN, D], F32, kind="ExternalInput")
    xout = nc.dram_tensor("xout", [TOWN, D], F32, kind="ExternalOutput")
    use_bias = layer == 0
    if use_bias:
        bor = nc.dram_tensor("bor", [1, D], F32, kind="ExternalInput")

    with stack:
        s = Sched(nc, stack)
        pool = Pool(nc, stack)
        ctx = Ctx(nc, s, pool, stack)
        stage_consts(ctx, masks=False)
        bias_row = None
        if use_bias:
            brf = ctx.sbuf("brf", [1, D], F32)
            brb = ctx.sbuf("brb", [1, D], BF16)
            e1 = s.dma("sync", brf[:, :], bor[:, :], sem="dC")
            s.op("scalar", lambda e: e.copy(brb[:, :], brf[:, :]), deps=[e1])
            bias_row = brb
        ctx.consts["ready"] = ("dC", s.cnt["dC"])
        bias_ev = s.now("scalar")
        onesrow = ctx.consts["onesrow"]

        otf_sb = ctx.sbuf("otf_sb", [128, 16 * TOWN], BF16)
        ev_o = s.dma("sync", otf_sb[:, :].rearrange("p (f t) -> p f t", f=16),
                     otf[:, :].rearrange("(f p) t -> p f t", p=128), sem="dX")
        wob = ctx.sbuf("wob", [128, 16 * D], BF16)
        wf0 = ctx.sbuf("wf0", [128, D], F32)
        wf1 = ctx.sbuf("wf1", [128, D], F32)
        wfs = [wf0, wf1]
        wfree = [None, None]
        ev_conv = []
        for k in range(16):
            wi = k % 2
            evd = s.dma("sync", wfs[wi][:, :], wo[k * 128:(k + 1) * 128, :],
                        deps=[wfree[wi]], sem="dW")
            if wi:
                evc = s.op("vector", lambda e, k=k, wi=wi: e.tensor_copy(
                    wob[:, k * D:(k + 1) * D], wfs[wi][:, :]), deps=[evd])
            else:
                evc = s.op("scalar", lambda e, k=k, wi=wi: e.copy(
                    wob[:, k * D:(k + 1) * D], wfs[wi][:, :]), deps=[evd])
            wfree[wi] = evc
            ev_conv.append(evc)
        w_ready = [("dX", s.cnt["dX"])] + ev_conv[-2:]

        xr0 = ctx.sbuf("xr0", [128, D], F32)
        xr1 = ctx.sbuf("xr1", [128, D], F32)
        xrs = [xr0, xr1]
        xo0 = ctx.sbuf("xo0", [128, D], F32)
        xo1 = ctx.sbuf("xo1", [128, D], F32)
        xos = [xo0, xo1]
        xr_free = [None, None]
        xo_free = [None, None]
        onesrow = ctx.consts["onesrow"]
        for tc in range(TOWN // 128):
            xi = tc % 2
            evx = s.dma("scalar", xrs[xi][:, :], xres[tc * 128:(tc + 1) * 128, :],
                        deps=[xr_free[xi]], sem="dX")
            evs_os = []
            for os_ in range(4):
                bidx, bank, bfree = pool.acquire()
                mm = None
                for k in range(16):
                    mm = s.op("tensor", lambda e, k=k, tc=tc, os_=os_, bank=bank: e.matmul(
                        bank[:, :], otf_sb[:, k * TOWN + tc * 128: k * TOWN + tc * 128 + 128],
                        wob[:, k * D + os_ * 512: k * D + os_ * 512 + 512],
                        start=(k == 0), stop=(k == 15 and bias_row is None)),
                        deps=(w_ready + [bfree]) if k == 0 else [])
                if bias_row is not None:
                    mm = s.op("tensor", lambda e, bank=bank, os_=os_: e.matmul(
                        bank[:, :], onesrow[0:1, :], bias_row[0:1, os_ * 512:os_ * 512 + 512],
                        start=False, stop=True), deps=[mm, bias_ev])
                ev = s.op("vector", lambda e, bank=bank, os_=os_, xi=xi: e.tensor_add(
                    xos[xi][:, os_ * 512:os_ * 512 + 512], bank[:, :],
                    xrs[xi][:, os_ * 512:os_ * 512 + 512]),
                    deps=[mm, evx, xo_free[xi]])
                pool.release(bidx, ev)
                evs_os.append(ev)
            evd = s.dma("scalar", xout[tc * 128:(tc + 1) * 128, :], xos[xi][:, :],
                        deps=evs_os, sem="dO")
            xo_free[xi] = evd
            xr_free[xi] = evd
        s.op("sync", lambda e: None, deps=[("dO", s.cnt["dO"])])
        s.finalize()
    return nc


def build_l0ffn():
    nc = bass.Bass("TRN2", target_bir_lowering=False)
    stack = ExitStack()
    x1d = nc.dram_tensor("x1d", [TOWN, D], F32, kind="ExternalInput")
    gwt = nc.dram_tensor("gwt", [D, INTER], F32, kind="ExternalInput")
    uwt = nc.dram_tensor("uwt", [D, INTER], F32, kind="ExternalInput")
    dwt = nc.dram_tensor("dwt", [INTER, D], F32, kind="ExternalInput")
    gsc = nc.dram_tensor("gsc", [128, NI], F32, kind="ExternalInput")
    usc = nc.dram_tensor("usc", [128, NI], F32, kind="ExternalInput")
    dsb = nc.dram_tensor("dsb", [128, D], F32, kind="ExternalInput")
    xout = nc.dram_tensor("xout", [TOWN, D], F32, kind="ExternalOutput")

    with stack:
        s = Sched(nc, stack)
        pool = Pool(nc, stack)
        ctx = Ctx(nc, s, pool, stack)
        stage_consts(ctx, masks=False)
        gs_sb = ctx.sbuf("gs_sb", [128, NI], F32)
        us_sb = ctx.sbuf("us_sb", [128, NI], F32)
        ds_sb = ctx.sbuf("ds_sb", [128, D], F32)
        s.dma("sync", gs_sb[:, :], gsc[:, :], sem="dC")
        s.dma("sync", us_sb[:, :], usc[:, :], sem="dC")
        s.dma("sync", ds_sb[:, :], dsb[:, :], sem="dC")
        ctx.consts["ready"] = ("dC", s.cnt["dC"])
        c_ready = ctx.consts["ready"]

        hdn = ctx.sbuf("hdn", [128, NI * TOWN], BF16)
        nk = 16
        with (nc.sbuf_tensor("lnt2", [128, 16 * TOWN], BF16) as lnt2,):
            with (nc.sbuf_tensor("xt0", [128, D], F32) as xt0,
                  nc.sbuf_tensor("xt1", [128, D], F32) as xt1,
                  nc.sbuf_tensor("lb0", [128, D], BF16) as lb0,
                  nc.sbuf_tensor("lb1", [128, D], BF16) as lb1,
                  nc.sbuf_tensor("lb2", [128, D], BF16) as lb2,
                  nc.sbuf_tensor("lb3", [128, D], BF16) as lb3,
                  nc.sbuf_tensor("stat", [128, 16], F32) as stat,
                  nc.sbuf_tensor("sqb0", [128, D], BF16) as sqb0,
                  nc.sbuf_tensor("sqb1", [128, D], BF16) as sqb1):
                emit_ln_t(ctx, x1d, TOWN // 128, lnt2, TOWN, [xt0, xt1],
                          [lb0, lb1, lb2, lb3], stat, [sqb0, sqb1])
            ln_done = [s.now("tensor"), s.now("vector"), s.now("scalar")]
            with (nc.sbuf_tensor("wg_f", [128, nk * 128], F32) as wg_f,
                  nc.sbuf_tensor("wg_b", [128, nk * 128], BF16) as wg_b,
                  nc.sbuf_tensor("wu_f", [128, nk * 128], F32) as wu_f,
                  nc.sbuf_tensor("wu_b", [128, nk * 128], BF16) as wu_b,
                  nc.sbuf_tensor("sg", [128, TOWN], F32) as sg):
                wg_free = wu_free = None
                sg_free = None
                ev_sg = None
                for ic in range(NI):
                    ev_dg = gather_w(ctx, gwt, ic * 128, 128, wg_f,
                                     [wg_free] + (ln_done if ic == 0 else []))
                    ev_wg = conv_w(ctx, wg_f, wg_b, nk * 128, "scalar", [ev_dg])
                    ev_du = gather_w(ctx, uwt, ic * 128, 128, wu_f, [wu_free])
                    ev_wu = conv_w(ctx, wu_f, wu_b, nk * 128, "vector", [ev_du])
                    sg_evs = []
                    for ts in range(TOWN // 512):
                        bidx, bank, bfree = pool.acquire()
                        mm = None
                        for k in range(nk):
                            mm = s.op("tensor", lambda e, k=k, ts=ts, bank=bank: e.matmul(
                                bank[:, :], wg_b[:, k * 128:(k + 1) * 128],
                                lnt2[:, k * TOWN + ts * 512: k * TOWN + ts * 512 + 512],
                                start=(k == 0), stop=(k == nk - 1)),
                                deps=([ev_wg, bfree] + ln_done) if k == 0 else [])
                        ev_sg = s.op("scalar", lambda e, ic=ic, ts=ts, bank=bank: e.activation(
                            sg[:, ts * 512:ts * 512 + 512], bank[:, :], AF.Silu,
                            scale=gs_sb[:, ic:ic + 1]), deps=[mm, sg_free, c_ready])
                        pool.release(bidx, ev_sg)
                        sg_evs.append(ev_sg)
                        wg_free = mm
                    hv = []
                    for ts in range(TOWN // 512):
                        bidx, bank, bfree = pool.acquire()
                        mm = None
                        for k in range(nk):
                            mm = s.op("tensor", lambda e, k=k, ts=ts, bank=bank: e.matmul(
                                bank[:, :], wu_b[:, k * 128:(k + 1) * 128],
                                lnt2[:, k * TOWN + ts * 512: k * TOWN + ts * 512 + 512],
                                start=(k == 0), stop=(k == nk - 1)),
                                deps=[ev_wu, bfree] if k == 0 else [])
                        col0 = ic * TOWN + ts * 512
                        ev_h = s.op("vector", lambda e, ic=ic, ts=ts, bank=bank, col0=col0: e.scalar_tensor_tensor(
                            hdn[:, col0:col0 + 512], bank[:, :], us_sb[:, ic:ic + 1],
                            sg[:, ts * 512:ts * 512 + 512], ALU.mult, ALU.mult),
                            deps=[mm, sg_evs[ts], c_ready])
                        pool.release(bidx, ev_h)
                        hv.append(ev_h)
                        wu_free = mm
                    sg_free = hv[-1]
        hdn_ready = [s.now("vector")]

        with (nc.sbuf_tensor("dw_f0", [128, 8 * 512], F32) as dw_f0,
              nc.sbuf_tensor("dw_f1", [128, 8 * 512], F32) as dw_f1,
              nc.sbuf_tensor("dbig", [128, NI * 512], BF16) as dbig,
              nc.sbuf_tensor("xr0", [128, 512], F32) as xr0,
              nc.sbuf_tensor("xr1", [128, 512], F32) as xr1,
              nc.sbuf_tensor("xo0", [128, 512], F32) as xo0,
              nc.sbuf_tensor("xo1", [128, 512], F32) as xo1):
            dview = dwt[:, :].rearrange("(k p) t -> p k t", p=128)
            dwf = [dw_f0, dw_f1]
            xrs, xos = [xr0, xr1], [xo0, xo1]
            gu_done = s.now("tensor")
            dwf_free = [gu_done, gu_done]
            dbig_free = gu_done
            xr_free = [gu_done, gu_done]
            xo_free = [gu_done, gu_done]
            for os_ in range(4):
                convs = []
                for bi, kb0 in enumerate(range(0, NI, 8)):
                    nkb = min(8, NI - kb0)
                    wi = bi % 2
                    evd = s.dma("sync", dwf[wi][:, 0:nkb * 512].rearrange("p (k c) -> p k c", k=nkb),
                                dview[:, kb0:kb0 + nkb, os_ * 512:os_ * 512 + 512],
                                deps=[dwf_free[wi]], sem="dW")
                    evc = s.op("vector" if wi else "scalar",
                               (lambda e, wi=wi, kb0=kb0, nkb=nkb: e.tensor_copy(
                                   dbig[:, kb0 * 512:(kb0 + nkb) * 512],
                                   dwf[wi][:, 0:nkb * 512])) if wi else
                               (lambda e, wi=wi, kb0=kb0, nkb=nkb: e.copy(
                                   dbig[:, kb0 * 512:(kb0 + nkb) * 512],
                                   dwf[wi][:, 0:nkb * 512])),
                               deps=[evd, dbig_free])
                    convs.append(evc)
                    dwf_free[wi] = evc
                for tc in range(TOWN // 128):
                    xi = tc % 2
                    evx = s.dma("scalar", xrs[xi][:, :],
                                x1d[tc * 128:(tc + 1) * 128, os_ * 512:os_ * 512 + 512],
                                deps=[xr_free[xi]], sem="dX")
                    bidx, bank, bfree = pool.acquire()
                    mm = None
                    for k in range(NI):
                        mm = s.op("tensor", lambda e, k=k, tc=tc, bank=bank: e.matmul(
                            bank[:, :], hdn[:, k * TOWN + tc * 128: k * TOWN + tc * 128 + 128],
                            dbig[:, k * 512:(k + 1) * 512],
                            start=(k == 0), stop=(k == NI - 1)),
                            deps=(convs + hdn_ready + [bfree]) if k == 0 else [])
                    if tc == TOWN // 128 - 1:
                        dbig_free = mm
                    e1 = s.op("vector", lambda e, bank=bank, os_=os_, xi=xi: e.scalar_tensor_tensor(
                        xos[xi][:, :], bank[:, :], 1.0,
                        ds_sb[:, os_ * 512:os_ * 512 + 512], ALU.bypass, ALU.mult),
                        deps=[mm, xo_free[xi], c_ready])
                    e2 = s.op("vector", lambda e, xi=xi: e.tensor_add(
                        xos[xi][:, :], xos[xi][:, :], xrs[xi][:, :]),
                        deps=[evx], strict=[e1])
                    pool.release(bidx, e2)
                    evd2 = s.dma("scalar", xout[tc * 128:(tc + 1) * 128, os_ * 512:os_ * 512 + 512],
                                 xos[xi][:, :], deps=[e2], sem="dO")
                    xo_free[xi] = evd2
                    xr_free[xi] = evd2
        s.op("sync", lambda e: None, deps=[("dO", s.cnt["dO"])])
        s.finalize()
    return nc


def build_l1attn():
    nc = bass.Bass("TRN2", target_bir_lowering=False)
    stack = ExitStack()
    x2f = nc.dram_tensor("x2f", [S, D], F32, kind="ExternalInput")
    wq1 = nc.dram_tensor("wq1", [D, 1024], F32, kind="ExternalInput")
    wk1 = nc.dram_tensor("wk1", [D, 256], F32, kind="ExternalInput")
    wv1 = nc.dram_tensor("wv1", [D, 256], F32, kind="ExternalInput")
    o1t = nc.dram_tensor("o1t", [1024, S], BF16, kind="ExternalOutput")

    with stack:
        s = Sched(nc, stack)
        pool = Pool(nc, stack)
        ctx = Ctx(nc, s, pool, stack)
        stage_consts(ctx, rope=True, masks=False)

        QT = ctx.sbuf("QT1", [128, NH * S], BF16)
        KT = ctx.sbuf("KT1", [128, 2 * S], BF16)
        VT = ctx.sbuf("VT1", [128, (S // 128) * 256], BF16)
        with (nc.sbuf_tensor("lnt3", [128, 16 * S], BF16) as lnt3,):
            with (nc.sbuf_tensor("xt0", [128, D], F32) as xt0,
                  nc.sbuf_tensor("xt1", [128, D], F32) as xt1,
                  nc.sbuf_tensor("lb0", [128, D], BF16) as lb0,
                  nc.sbuf_tensor("lb1", [128, D], BF16) as lb1,
                  nc.sbuf_tensor("lb2", [128, D], BF16) as lb2,
                  nc.sbuf_tensor("lb3", [128, D], BF16) as lb3,
                  nc.sbuf_tensor("stat", [128, 16], F32) as stat,
                  nc.sbuf_tensor("sqb0", [128, D], BF16) as sqb0,
                  nc.sbuf_tensor("sqb1", [128, D], BF16) as sqb1):
                emit_ln_t(ctx, x2f, S // 128, lnt3, S, [xt0, xt1],
                          [lb0, lb1, lb2, lb3], stat, [sqb0, sqb1])
            ln_done = [s.now("tensor"), s.now("vector"), s.now("scalar")]
            with (nc.sbuf_tensor("vwf0", [128, 8 * 256], F32) as vwf0,
                  nc.sbuf_tensor("vwb", [128, 16 * 256], BF16) as vwb):
                emit_proj_tm(ctx, lnt3, S, S, wv1, 256, VT, 256,
                             [vwf0], vwb, ready=ln_done, tag="v1")
            with (nc.sbuf_tensor("qwf0", [128, 16 * 128], F32) as qwf0,
                  nc.sbuf_tensor("qwf1", [128, 16 * 128], F32) as qwf1,
                  nc.sbuf_tensor("qwb0", [128, 16 * 128], BF16) as qwb0,
                  nc.sbuf_tensor("qwb1", [128, 16 * 128], BF16) as qwb1,
                  nc.sbuf_tensor("rp0", [128, 512], F32) as rp0,
                  nc.sbuf_tensor("rp1", [128, 512], BF16) as rp1,
                  nc.sbuf_tensor("rp2", [128, 512], F32) as rp2,
                  nc.sbuf_tensor("rp3", [128, 512], F32) as rp3):
                rope = {"bufs": (rp0, rp1, rp2, rp3)}
                v_done = [s.now("tensor")]
                emit_proj_fm(ctx, lnt3, S, S, wq1, 1024, QT, S,
                             [qwf0, qwf1], [qwb0, qwb1], rope=rope,
                             ready=ln_done + v_done, tag="q1")
                emit_proj_fm(ctx, lnt3, S, S, wk1, 256, KT, S,
                             [qwf0, qwf1], [qwb0, qwb1], rope=rope,
                             ready=ln_done + [s.now("tensor")], tag="k1")

        md = nc.dram_tensor("c_masks", [128, 4 * 512], BF16, kind="ExternalInput")
        masks_sb = ctx.sbuf("masks_sb", [128, 4 * 512], BF16)
        evm = s.dma("sync", masks_sb[:, :], md[:, :],
                    deps=[s.now("tensor")], sem="dC")
        ctx.consts["masks"] = masks_sb
        OT = ctx.sbuf("OT1", [128, NH * S], BF16)
        esb = [ctx.sbuf(f"esb{j}", [128, 512], BF16) for j in range(32)]
        ready = [s.now("scalar"), s.now("vector"), evm]
        ev_a = emit_attention(ctx, QT, KT, VT, OT, lambda h: h // 4, 256,
                              "linear", esb, ready, tag="a1")
        ev = s.dma("scalar", o1t[:, :].rearrange("(h p) t -> p h t", p=128),
                   OT[:, :].rearrange("p (h t) -> p h t", h=NH),
                   deps=[ev_a], sem="dO")
        s.op("sync", lambda e: None, deps=[ev])
        s.finalize()
    return nc


def build_l1ffn():
    nc = bass.Bass("TRN2", target_bir_lowering=False)
    stack = ExitStack()
    x3d = nc.dram_tensor("x3d", [TOWN, D], F32, kind="ExternalInput")
    w1t = nc.dram_tensor("w1t", [D, D], F32, kind="ExternalInput")
    b1b = nc.dram_tensor("b1b", [128, D], F32, kind="ExternalInput")
    xout = nc.dram_tensor("xout", [TOWN, D], F32, kind="ExternalOutput")

    with stack:
        s = Sched(nc, stack)
        pool = Pool(nc, stack)
        ctx = Ctx(nc, s, pool, stack)
        stage_consts(ctx, masks=False)
        onescol = ctx.consts["onescol"]
        onesrow = ctx.consts["onesrow"]
        b1_sb = ctx.sbuf("b1_sb", [128, D], F32)
        ev_b1 = s.dma("sync", b1_sb[:, :], b1b[:, :], sem="dC")
        ctx.consts["ready"] = ("dC", s.cnt["dC"])
        cready = ctx.consts["ready"]

        signT = ctx.sbuf("signT", [128, 16 * D], BF16)
        alphab = ctx.sbuf("alphab", [128, D], F32)
        lnt4 = ctx.sbuf("lnt4", [128, 16 * TOWN], BF16)

        with (nc.sbuf_tensor("xt0", [128, D], F32) as xt0,
              nc.sbuf_tensor("xt1", [128, D], F32) as xt1,
              nc.sbuf_tensor("lb0", [128, D], BF16) as lb0,
              nc.sbuf_tensor("lb1", [128, D], BF16) as lb1,
              nc.sbuf_tensor("lb2", [128, D], BF16) as lb2,
              nc.sbuf_tensor("lb3", [128, D], BF16) as lb3,
              nc.sbuf_tensor("stat", [128, 16], F32) as stat,
                  nc.sbuf_tensor("sqb0", [128, D], BF16) as sqb0,
                  nc.sbuf_tensor("sqb1", [128, D], BF16) as sqb1):
            emit_ln_t(ctx, x3d, TOWN // 128, lnt4, TOWN, [xt0, xt1],
                      [lb0, lb1, lb2, lb3], stat, [sqb0, sqb1])

        with (nc.sbuf_tensor("wsl", [128, D], F32) as wsl,
              nc.sbuf_tensor("absl", [128, D], BF16) as absl,
              nc.sbuf_tensor("alr_f", [1, D], F32) as alr_f,
              nc.sbuf_tensor("alr_b", [1, D], BF16) as alr_b):
            ln_guard = s.op("sync", lambda e: None, deps=[
                s.now("tensor"), s.now("vector"), s.now("scalar")])
            wsl_free = ln_guard
            absl_free = None
            apb = [pool.acquire() for _ in range(4)]
            mm_a = [None] * 4
            for k in range(16):
                evw = s.dma("sync", wsl[:, :], w1t[k * 128:(k + 1) * 128, :],
                            deps=[wsl_free], sem="dW")
                evs = s.op("scalar", lambda e, k=k: e.activation(
                    signT[:, k * D:(k + 1) * D], wsl[:, :], AF.Sign), deps=[evw])
                eva0 = s.op("scalar", lambda e: e.activation(
                    absl[:, 0:1024], wsl[:, 0:1024], AF.Abs), deps=[evw, absl_free])
                eva = s.op("scalar", lambda e: e.activation(
                    absl[:, 1024:2048], wsl[:, 1024:2048], AF.Abs), deps=[evw])
                wsl_free = eva
                for q4 in range(4):
                    bidx, bank, bfree = apb[q4]
                    deps = [eva, cready]
                    if k == 0:
                        deps.append(bfree)
                    mm_a[q4] = s.op("tensor", lambda e, q4=q4, bank=bank, k=k: e.matmul(
                        bank[0:1, :], onescol[:, 0:1],
                        absl[:, q4 * 512:(q4 + 1) * 512],
                        start=(k == 0), stop=(k == 15)), deps=deps)
                absl_free = mm_a[3]
            for q4 in range(4):
                bidx, bank, bfree = apb[q4]
                e1 = s.op("vector", lambda e, q4=q4, bank=bank: e.tensor_scalar_mul(
                    alr_f[0:1, q4 * 512:(q4 + 1) * 512], bank[0:1, :], 1.0 / D),
                    deps=[mm_a[q4]])
                pool.release(bidx, e1)
            e2 = s.op("scalar", lambda e: e.copy(alr_b[0:1, :], alr_f[0:1, :]),
                      deps=[s.now("vector")])
            for q4 in range(4):
                bidx, bank, bfree = pool.acquire()
                e3 = s.op("tensor", lambda e, q4=q4, bank=bank: e.matmul(
                    bank[:, :], onesrow[0:1, :], alr_b[0:1, q4 * 512:(q4 + 1) * 512],
                    start=True, stop=True), deps=[e2, bfree, cready])
                e4 = s.op("scalar", lambda e, q4=q4, bank=bank: e.copy(
                    alphab[:, q4 * 512:(q4 + 1) * 512], bank[:, :]), deps=[e3])
                pool.release(bidx, e4)
        alpha_ready = s.now("scalar")
        sign_ready = s.now("scalar")
        ready_mm = [sign_ready, s.now("vector")]

        with (nc.sbuf_tensor("xo0", [128, 512], F32) as xo0,
              nc.sbuf_tensor("xo1", [128, 512], F32) as xo1,
              nc.sbuf_tensor("xr0", [128, 512], F32) as xr0,
              nc.sbuf_tensor("xr1", [128, 512], F32) as xr1):
            xos = [xo0, xo1]
            xrs = [xr0, xr1]
            xo_free = [None, None]
            xr_free = [ready_mm[1], ready_mm[1]]
            nn = 0
            for os_ in range(4):
                for tc in range(TOWN // 128):
                    bidx, bank, bfree = pool.acquire()
                    mm = None
                    for k in range(16):
                        mm = s.op("tensor", lambda e, k=k, tc=tc, os_=os_, bank=bank: e.matmul(
                            bank[:, :], lnt4[:, k * TOWN + tc * 128: k * TOWN + tc * 128 + 128],
                            signT[:, k * D + os_ * 512: k * D + os_ * 512 + 512],
                            start=(k == 0), stop=(k == 15)),
                            deps=(ready_mm + [bfree]) if k == 0 else [])
                    xi = nn % 2
                    nn += 1
                    evx = s.dma("sync", xrs[xi][:, :],
                                x3d[tc * 128:(tc + 1) * 128, os_ * 512:os_ * 512 + 512],
                                deps=[xr_free[xi]], sem="dX")
                    e1 = s.op("vector", lambda e, bank=bank, os_=os_, xi=xi: e.scalar_tensor_tensor(
                        xos[xi][:, :], bank[:, :], 1.0,
                        alphab[:, os_ * 512:os_ * 512 + 512], ALU.bypass, ALU.mult),
                        deps=[mm, xo_free[xi], alpha_ready])
                    e2 = s.op("vector", lambda e, xi=xi: e.tensor_add(
                        xos[xi][:, :], xos[xi][:, :], xrs[xi][:, :]),
                        deps=[evx], strict=[e1])
                    e3 = s.op("vector", lambda e, os_=os_, xi=xi: e.tensor_add(
                        xos[xi][:, :], xos[xi][:, :],
                        b1_sb[:, os_ * 512:os_ * 512 + 512]),
                        deps=[e2, ev_b1])
                    pool.release(bidx, e3)
                    evd = s.dma("scalar", xout[tc * 128:(tc + 1) * 128, os_ * 512:os_ * 512 + 512],
                                xos[xi][:, :], deps=[e3], sem="dO")
                    xo_free[xi] = evd
                    xr_free[xi] = e2
        s.op("sync", lambda e: None, deps=[("dO", s.cnt["dO"])])
        s.finalize()
    return nc



def _consts(rope=False):
    ident = np.eye(128, dtype=bfloat16)
    masks = np.zeros((128, 4 * 512), dtype=np.float32)
    kk = np.arange(128)[:, None]
    qq = np.arange(512)[None, :]
    for d in range(4):
        masks[:, d * 512:(d + 1) * 512] = (kk <= qq - 128 * d).astype(np.float32)
    maskneg = np.where(masks > 0, np.float32(0.0), np.float32(-1e30))
    out = dict(
        c_ident=ident,
        c_masks=masks.astype(bfloat16),
        c_maskneg=maskneg.astype(bfloat16),
        c_onescol=np.ones((128, 1), dtype=bfloat16),
        c_onesrow=np.ones((1, 128), dtype=bfloat16),
    )
    if rope:
        inv = 1.0 / (10000.0 ** (np.arange(0, DH, 2, dtype=np.float64) / DH))
        t = np.arange(S, dtype=np.float64)
        fr = np.outer(t, inv)
        out["c_prot"] = np.ascontiguousarray(np.roll(np.eye(128), 64, axis=0).astype(bfloat16))
        out["c_cos"] = np.ascontiguousarray(
            np.concatenate([np.cos(fr), np.cos(fr)], 1).T.astype(np.float32))
        out["c_srot"] = np.ascontiguousarray(
            np.concatenate([np.sin(fr), -np.sin(fr)], 1).T.astype(np.float32))
    return out


def _run(nc, in_maps):
    res = run_bass_kernel_spmd(nc, in_maps, core_ids=list(range(NCORES)))
    return res.results


def _pc(fn):
    return [fn(c // 2, c % 2) for c in range(NCORES)]


def kernel_unfused(**inputs):
    ii = {k: np.asarray(v) for k, v in inputs.items()}
    X = np.ascontiguousarray(ii["L0_attn_X"])
    cA = _consts()
    cR = _consts(rope=True)
    ca_l = {k: v for k, v in cA.items() if k != "c_masks"}

    nc1 = build_l0attn()
    maps = _pc(lambda b, r: dict(
        xfull=X[b],
        wq=np.ascontiguousarray(ii["L0_attn_Wq"][:, r * 1024:(r + 1) * 1024]),
        wk=np.ascontiguousarray(ii["L0_attn_Wk"][:, r * 1024:(r + 1) * 1024]),
        wv=np.ascontiguousarray(ii["L0_attn_Wv"][:, r * 1024:(r + 1) * 1024]),
        bqc=np.ascontiguousarray(ii["L0_attn_bq"][r * 1024:(r + 1) * 1024].reshape(8, 128).T),
        bkc=np.ascontiguousarray(ii["L0_attn_bk"][r * 1024:(r + 1) * 1024].reshape(8, 128).T),
        bvr=np.ascontiguousarray(ii["L0_attn_bv"][r * 1024:(r + 1) * 1024].reshape(1, 1024)),
        **cA))
    r1 = _run(nc1, maps)

    nc2 = build_wo(0)
    maps = _pc(lambda b, r: dict(
        otf=np.ascontiguousarray(np.concatenate(
            [r1[2 * b]["oat"], r1[2 * b + 1]["oat"]], 0)[:, r * 1024:(r + 1) * 1024]),
        wo=ii["L0_attn_Wo"],
        xres=np.ascontiguousarray(X[b, r * 1024:(r + 1) * 1024]),
        bor=np.ascontiguousarray(ii["L0_attn_bo"].reshape(1, D)),
        **ca_l))
    r2 = _run(nc2, maps)

    nc3 = build_l0ffn()
    gwt = np.ascontiguousarray(ii["L0_ffn_gate_w"].T)
    uwt = np.ascontiguousarray(ii["L0_ffn_up_w"].T)
    dwt = np.ascontiguousarray(ii["L0_ffn_down_w"].T)
    gsc = np.ascontiguousarray(ii["L0_ffn_gate_s"].reshape(NI, 128).T)
    usc = np.ascontiguousarray(ii["L0_ffn_up_s"].reshape(NI, 128).T)
    dsb = np.ascontiguousarray(np.broadcast_to(ii["L0_ffn_down_s"][None, :], (128, D)))
    maps = _pc(lambda b, r: dict(
        x1d=r2[2 * b + r]["xout"], gwt=gwt, uwt=uwt, dwt=dwt,
        gsc=gsc, usc=usc, dsb=dsb, **ca_l))
    r3 = _run(nc3, maps)

    nc4 = build_l1attn()
    wq1 = np.ascontiguousarray(ii["L1_attn_Wq"].T)
    wk1 = np.ascontiguousarray(ii["L1_attn_Wk"].T)
    wv1 = np.ascontiguousarray(ii["L1_attn_Wv"].T)
    maps = _pc(lambda b, r: dict(
        x2f=np.ascontiguousarray(np.concatenate(
            [r3[2 * b]["xout"], r3[2 * b + 1]["xout"]], 0)),
        wq1=np.ascontiguousarray(wq1[:, r * 1024:(r + 1) * 1024]),
        wk1=np.ascontiguousarray(wk1[:, r * 256:(r + 1) * 256]),
        wv1=np.ascontiguousarray(wv1[:, r * 256:(r + 1) * 256]),
        **cR))
    r4 = _run(nc4, maps)

    nc5 = build_wo(1)
    wo1t = np.ascontiguousarray(ii["L1_attn_Wo"].T)
    maps = _pc(lambda b, r: dict(
        otf=np.ascontiguousarray(np.concatenate(
            [r4[2 * b]["o1t"], r4[2 * b + 1]["o1t"]], 0)[:, r * 1024:(r + 1) * 1024]),
        wo=wo1t,
        xres=r3[2 * b + r]["xout"],
        **ca_l))
    r5 = _run(nc5, maps)

    nc6 = build_l1ffn()
    w1t = np.ascontiguousarray(ii["L1_ffn_weight"].T)
    b1b = np.ascontiguousarray(np.broadcast_to(ii["L1_ffn_bias"][None, :], (128, D)))
    maps = _pc(lambda b, r: dict(
        x3d=r5[2 * b + r]["xout"], w1t=w1t, b1b=b1b, **ca_l))
    r6 = _run(nc6, maps)

    out = np.empty((B, S, D), dtype=np.float32)
    for b in range(B):
        out[b, 0:1024] = r6[2 * b]["xout"]
        out[b, 1024:2048] = r6[2 * b + 1]["xout"]
    return out



def build_fused(dbg=False):
    nc = bass.Bass("TRN2", target_bir_lowering=False)
    stack = ExitStack()
    xfull = nc.dram_tensor("xfull", [S, D], F32, kind="ExternalInput")
    xres = nc.dram_tensor("xres", [TOWN, D], F32, kind="ExternalInput")
    toff = nc.dram_tensor("toff", [1, 1], mybir.dt.int32, kind="ExternalInput")
    wq = nc.dram_tensor("wq", [D, 1024], F32, kind="ExternalInput")
    wk = nc.dram_tensor("wk", [D, 1024], F32, kind="ExternalInput")
    wv = nc.dram_tensor("wv", [D, 1024], F32, kind="ExternalInput")
    bqc = nc.dram_tensor("bqc", [128, 8], F32, kind="ExternalInput")
    bkc = nc.dram_tensor("bkc", [128, 8], F32, kind="ExternalInput")
    bvr = nc.dram_tensor("bvr", [1, 1024], F32, kind="ExternalInput")
    wo = nc.dram_tensor("wo", [D, D], F32, kind="ExternalInput")
    bor = nc.dram_tensor("bor", [1, D], F32, kind="ExternalInput")
    gwt = nc.dram_tensor("gwt", [D, INTER], F32, kind="ExternalInput")
    uwt = nc.dram_tensor("uwt", [D, INTER], F32, kind="ExternalInput")
    dwt = nc.dram_tensor("dwt", [INTER, D], F32, kind="ExternalInput")
    gsc = nc.dram_tensor("gsc", [128, NI], F32, kind="ExternalInput")
    usc = nc.dram_tensor("usc", [128, NI], F32, kind="ExternalInput")
    dsb = nc.dram_tensor("dsb", [128, D], F32, kind="ExternalInput")
    wq1 = nc.dram_tensor("wq1", [D, 1024], F32, kind="ExternalInput")
    wk1 = nc.dram_tensor("wk1", [D, 256], F32, kind="ExternalInput")
    wv1 = nc.dram_tensor("wv1", [D, 256], F32, kind="ExternalInput")
    wo1 = nc.dram_tensor("wo1", [D, D], F32, kind="ExternalInput")
    w1t = nc.dram_tensor("w1t", [D, D], F32, kind="ExternalInput")
    b1b = nc.dram_tensor("b1b", [128, D], F32, kind="ExternalInput")
    c_ident = nc.dram_tensor("c_ident", [128, 128], BF16, kind="ExternalInput")
    c_onescol = nc.dram_tensor("c_onescol", [128, 1], BF16, kind="ExternalInput")
    c_onesrow = nc.dram_tensor("c_onesrow", [1, 128], BF16, kind="ExternalInput")
    c_masks = nc.dram_tensor("c_masks", [128, 4 * 512], BF16, kind="ExternalInput")
    c_maskneg = nc.dram_tensor("c_maskneg", [128, 4 * 512], BF16, kind="ExternalInput")
    c_cos = nc.dram_tensor("c_cos", [128, S], F32, kind="ExternalInput")
    c_srot = nc.dram_tensor("c_srot", [128, S], F32, kind="ExternalInput")
    c_prot = nc.dram_tensor("c_prot", [128, 128], BF16, kind="ExternalInput")
    xout = nc.dram_tensor("xout", [TOWN, D], F32, kind="ExternalOutput")
    KD = "ExternalOutput" if dbg else "Internal"
    x1_d = nc.dram_tensor("x1_d", [TOWN, D], F32, kind=KD)
    x2_d = nc.dram_tensor("x2_d", [TOWN, D], F32, kind=KD)
    x3_d = nc.dram_tensor("x3_d", [TOWN, D], F32, kind=KD)
    e1_in = nc.dram_tensor("e1_in", [4, 1024, 512], BF16)
    e1_out = nc.dram_tensor("e1_out", [4, 2048, 512], BF16)
    e2_in = nc.dram_tensor("e2_in", [4, D, 256], BF16)
    e2_out = nc.dram_tensor("e2_out", [4, 2 * D, 256], BF16)
    e3_in = nc.dram_tensor("e3_in", [4, 1024, 512], BF16)
    e3_out = nc.dram_tensor("e3_out", [4, 2048, 512], BF16)
    if dbg:
        dbg_e1i = nc.dram_tensor("dbg_e1i", [1024, S], BF16, kind="ExternalOutput")
        dbg_e1o = nc.dram_tensor("dbg_e1o", [2048, S], BF16, kind="ExternalOutput")
        dbg_e2i = nc.dram_tensor("dbg_e2i", [D, 1024], BF16, kind="ExternalOutput")
        dbg_e3i = nc.dram_tensor("dbg_e3i", [1024, S], BF16, kind="ExternalOutput")
        dbg_lnt2 = nc.dram_tensor("dbg_lnt2", [128, 16 * TOWN], BF16, kind="ExternalOutput")
        dbg_hdn = nc.dram_tensor("dbg_hdn", [128, 4 * TOWN], BF16, kind="ExternalOutput")
        dbg_lnt4 = nc.dram_tensor("dbg_lnt4", [128, 16 * TOWN], BF16, kind="ExternalOutput")
        dbg_sgn = nc.dram_tensor("dbg_sgn", [128, 2 * D], BF16, kind="ExternalOutput")
        dbg_alp = nc.dram_tensor("dbg_alp", [128, D], F32, kind="ExternalOutput")
        dbg_absl = nc.dram_tensor("dbg_absl", [128, D], BF16, kind="ExternalOutput")
        dbg_alrf = nc.dram_tensor("dbg_alrf", [1, D], F32, kind="ExternalOutput")
        dbg_alrb = nc.dram_tensor("dbg_alrb", [1, D], BF16, kind="ExternalOutput")
        dbg_otf = nc.dram_tensor("dbg_otf", [128, 16 * TOWN], BF16, kind="ExternalOutput")
    RG = [[0, 1], [2, 3], [4, 5], [6, 7]]

    with stack:
        s = Sched(nc, stack)
        pool = Pool(nc, stack)
        ctx = Ctx(nc, s, pool, stack)
        ident_sb = ctx.sbuf("ident_sb", [128, 128], BF16)
        onescol_sb = ctx.sbuf("onescol_sb", [128, 1], BF16)
        onesrow_sb = ctx.sbuf("onesrow_sb", [1, 128], BF16)
        toff_sb = ctx.sbuf("toff_sb", [1, 1], mybir.dt.int32)
        s.dma("sync", ident_sb[:, :], c_ident[:, :], sem="dC")
        s.dma("sync", onescol_sb[:, :], c_onescol[:, :], sem="dC")
        s.dma("sync", onesrow_sb[:, :], c_onesrow[:, :], sem="dC")
        ev_toff = s.dma("sync", toff_sb[:, :], toff[:, :], sem="dT")
        ctx.consts.update(ident=ident_sb, onescol=onescol_sb, onesrow=onesrow_sb)
        ctx.consts["ready"] = ("dC", s.cnt["dC"])
        onesrow = onesrow_sb

        def wo_phase(otf_src_dram, cc_ev, w_dram, bias_dram, res_dram, out_dram,
                     px, guard=(), dbg_otf=None):
            bias_row = None
            if bias_dram is not None:
                brf = ctx.sbuf(f"{px}brf", [1, D], F32)
                brb = ctx.sbuf(f"{px}brb", [1, D], BF16)
                e1b = s.dma("sync", brf[:, :], bias_dram[:, :],
                            deps=list(guard), sem="dC")
                ebb = s.op("scalar", lambda e: e.copy(brb[:, :], brf[:, :]), deps=[e1b])
                bias_row = brb
                bias_ev = ebb
            else:
                bias_ev = None
            otf_sb = ctx.sbuf(f"{px}otf", [128, 16 * TOWN], BF16)

            def dynload(q):
                def fn(e):
                    with e.register(f"{px}ro{q}") as ro:
                        e.reg_load(ro, toff_sb[0:1, 0:1])
                        off = e.snap(ro, min_val=0, max_val=1)
                        return e.dma_start(
                            out=otf_sb[:, :].rearrange(
                                "p (f q t) -> p q f t", q=2, t=512)[:, q],
                            in_=otf_src_dram[:, :, :].rearrange(
                                "s (f p) t -> s p f t", p=128)
                                [bass.ds(off * 2 + q, 1), :, :, :].squeeze(0))
                return fn

            ev_o0 = s.dyn_dma("gpsimd", dynload(0),
                              deps=[cc_ev[0], cc_ev[2], ev_toff], sem="dY")
            ev_o = s.dyn_dma("gpsimd", dynload(1),
                             deps=[cc_ev[1], cc_ev[3]], sem="dY")
            ev_os = [ev_o0, ev_o]
            if dbg_otf is not None:
                s.dma("scalar", dbg_otf[:, :], otf_sb[:, :], deps=[ev_o], sem="dO")
            wob = ctx.sbuf(f"{px}wob", [128, 16 * D], BF16)
            wf0 = ctx.sbuf(f"{px}wf0", [128, 2 * D], F32)
            wf1 = ctx.sbuf(f"{px}wf1", [128, 2 * D], F32)
            wfs = [wf0, wf1]
            wfree = [None, None]
            ev_conv = []
            for kb in range(8):
                wi = kb % 2
                evd = s.dma("sync" if wi == 0 else "scalar",
                            wfs[wi][:, :].rearrange("p (a c) -> p a c", a=2),
                            w_dram[kb * 256:(kb + 1) * 256, :]
                            .rearrange("(a p) c -> p a c", p=128),
                            deps=[wfree[wi]] + (list(guard) if kb < 2 else []),
                            sem="wA" if wi == 0 else "wB")
                if wi:
                    evc = s.op("vector", lambda e, kb=kb, wi=wi: e.tensor_copy(
                        wob[:, kb * 2 * D:(kb + 1) * 2 * D], wfs[wi][:, :]),
                        deps=[evd] + (list(guard) if kb < 2 else []))
                else:
                    evc = s.op("scalar", lambda e, kb=kb, wi=wi: e.copy(
                        wob[:, kb * 2 * D:(kb + 1) * 2 * D], wfs[wi][:, :]),
                        deps=[evd] + (list(guard) if kb < 2 else []))
                wfree[wi] = evc
                ev_conv.append(evc)
                ev_conv.append(evc)
            w_ready0 = [bias_ev] if bias_ev else []
            xr0 = ctx.sbuf(f"{px}xr0", [128, D], F32)
            xr1 = ctx.sbuf(f"{px}xr1", [128, D], F32)
            xo0 = ctx.sbuf(f"{px}xo0", [128, D], F32)
            xo1 = ctx.sbuf(f"{px}xo1", [128, D], F32)
            xrs, xos = [xr0, xr1], [xo0, xo1]
            xr_free = [None, None]
            xo_free = [None, None]
            for tc in range(TOWN // 128):
                xi = tc % 2
                evx = s.dma("scalar", xrs[xi][:, :], res_dram[tc * 128:(tc + 1) * 128, :],
                            deps=[xr_free[xi]] + (list(guard) if tc < 2 else []),
                            sem="xA" if xi == 0 else "xB")
                evs_os = []
                for os_ in range(4):
                    bidx, bank, bfree = pool.acquire()
                    mm = None
                    for k in range(16):
                        kdeps = [ev_conv[k]]
                        if k == 0:
                            kdeps += w_ready0 + [ev_os[tc // 4], bfree]
                        mm = s.op("tensor", lambda e, k=k, tc=tc, os_=os_, bank=bank: e.matmul(
                            bank[:, :], otf_sb[:, k * TOWN + tc * 128: k * TOWN + tc * 128 + 128],
                            wob[:, k * D + os_ * 512: k * D + os_ * 512 + 512],
                            start=(k == 0), stop=(k == 15 and bias_row is None)),
                            deps=kdeps)
                    if bias_row is not None:
                        mm = s.op("tensor", lambda e, bank=bank, os_=os_: e.matmul(
                            bank[:, :], onesrow[0:1, :],
                            bias_row[0:1, os_ * 512:os_ * 512 + 512],
                            start=False, stop=True), deps=[mm])
                    ev = s.op("vector", lambda e, bank=bank, os_=os_, xi=xi: e.tensor_add(
                        xos[xi][:, os_ * 512:os_ * 512 + 512], bank[:, :],
                        xrs[xi][:, os_ * 512:os_ * 512 + 512]),
                        deps=[mm, evx, xo_free[xi]])
                    pool.release(bidx, ev)
                    evs_os.append(ev)
                evd = s.dma("scalar", out_dram[tc * 128:(tc + 1) * 128, :],
                            xos[xi][:, :], deps=evs_os,
                            sem="sA" if xi == 0 else "sB")
                xo_free[xi] = evd
                xr_free[xi] = evd
            return [("sA", s.cnt["sA"]), ("sB", s.cnt["sB"])]

        stA = ExitStack()
        ctx.stack = stA
        bq_sb = ctx.sbuf("bq_sb", [128, 8], F32)
        bk_sb = ctx.sbuf("bk_sb", [128, 8], F32)
        bvf = ctx.sbuf("bvf", [1, 1024], F32)
        bvb = ctx.sbuf("bvb", [1, 1024], BF16)
        s.dma("sync", bq_sb[:, :], bqc[:, :], sem="dC")
        s.dma("sync", bk_sb[:, :], bkc[:, :], sem="dC")
        evb = s.dma("sync", bvf[:, :], bvr[:, :], sem="dC")
        s.op("scalar", lambda e: e.copy(bvb[:, :], bvf[:, :]), deps=[evb])
        ctx.consts["ready"] = ("dC", s.cnt["dC"])
        QT = ctx.sbuf("QT", [128, NH * S], BF16)
        KT = ctx.sbuf("KT", [128, NH * S], BF16)
        VT = ctx.sbuf("VT", [128, (S // 128) * 1024], BF16)
        stA1 = ExitStack()
        ctx.stack = stA1
        lnt1 = ctx.sbuf("lnt1", [128, 16 * S], BF16)
        stA1a = ExitStack()
        ctx.stack = stA1a
        xt0 = ctx.sbuf("xt0", [128, D], F32)
        xt1 = ctx.sbuf("xt1", [128, D], F32)
        lbs = [ctx.sbuf(f"lb{i}", [128, D], BF16) for i in range(4)]
        statb = ctx.sbuf("statb", [128, 16], F32)
        sqb = [ctx.sbuf("sqbAa", [128, D], BF16), ctx.sbuf("sqbAb", [128, D], BF16)]
        emit_ln_t(ctx, xfull, S // 128, lnt1, S, [xt0, xt1], lbs, statb, sqb)
        stA1a.close()
        ln_done = [s.now("tensor"), s.now("vector"), s.now("scalar")]
        stA1b = ExitStack()
        ctx.stack = stA1b
        vwf0 = ctx.sbuf("vwf0", [128, 8 * 512], F32)
        vwb = ctx.sbuf("vwb", [128, 16 * 512], BF16)
        emit_proj_tm(ctx, lnt1, S, S, wv, 1024, VT, 1024, [vwf0], vwb,
                     bias_row=bvb, ready=ln_done, tag="v")
        stA1b.close()
        v_done = [s.now("tensor")]
        stA1c = ExitStack()
        ctx.stack = stA1c
        qwf0 = ctx.sbuf("qwf0", [128, 16 * 128], F32)
        qwf1 = ctx.sbuf("qwf1", [128, 16 * 128], F32)
        qwb0 = ctx.sbuf("qwb0", [128, 16 * 128], BF16)
        qwb1 = ctx.sbuf("qwb1", [128, 16 * 128], BF16)
        emit_proj_fm(ctx, lnt1, S, S, wq, 1024, QT, S, [qwf0, qwf1], [qwb0, qwb1],
                     bias_cols=bq_sb, ready=ln_done + v_done, tag="q")
        emit_proj_fm(ctx, lnt1, S, S, wk, 1024, KT, S, [qwf0, qwf1], [qwb0, qwb1],
                     bias_cols=bk_sb, ready=ln_done + [s.now("tensor")], tag="k")
        stA1c.close()
        stA1.close()
        sched_dS_prev = [None]

        def dS_dma(eng, out, in_, deps=(), sem="dS"):
            ev = s.dma(eng, out, in_, deps=list(deps) + [sched_dS_prev[0]],
                       sem=sem)
            sched_dS_prev[0] = ev
            return ev

        stA2 = ExitStack()
        ctx.stack = stA2
        masks_sb = ctx.sbuf("masksA", [128, 4 * 512], BF16)
        evm = s.dma("sync", masks_sb[:, :], c_maskneg[:, :], deps=[s.now("tensor")],
                    sem="dC")
        ctx.consts["masks"] = masks_sb
        OT = ctx.sbuf("OT", [128, NH * S], BF16)
        esb = [ctx.sbuf(f"esb{j}", [128, 512], BF16) for j in range(32)]
        ready = [s.now("scalar"), s.now("vector"), evm]
        ccs1 = []
        sts1 = []

        def mid1(ev_half):
            g = len(ccs1)
            st = dS_dma("scalar",
                        e1_in[g, :, :].rearrange("(h p) t -> p h t", p=128),
                        OT[:, :].rearrange("p (h t) -> p h t", h=NH)
                        [:, :, g * 512:(g + 1) * 512],
                        deps=[ev_half])
            sts1.append(st)
            ccs1.append(s.cc(lambda e, g=g: e.collective_compute(
                "AllGather", ALU.bypass, replica_groups=RG,
                ins=[e1_in[g, :, :]], outs=[e1_out[g, :, :]]),
                f"cc1_{g}", deps=[st]))

        ev_a = emit_attention(ctx, QT, KT, VT, OT, lambda h: h, 1024, "softmax",
                              esb, ready, mid_cb=mid1)
        mid1(ev_a)
        ev_st1, cc1 = sts1[-1], ccs1[-1]
        stA2.close()
        stA.close()
        guardA = [ev_st1, s.now("tensor"), s.now("vector"), s.now("scalar")]

        stB_out = ExitStack()
        ctx.stack = stB_out
        lnt2 = ctx.sbuf("lnt2", [128, 16 * TOWN], BF16)
        stB = ExitStack()
        ctx.stack = stB
        sp1 = wo_phase(e1_out, ccs1, wo, bor, xres, x1_d, "b", guard=guardA,
                       dbg_otf=dbg_otf if dbg else None)
        stB.close()
        guardB = sp1 + [s.now("tensor"), s.now("vector"), s.now("scalar")]
        ctx.stack = stB_out
        hdn = ctx.sbuf("hdn", [128, NI * TOWN], BF16)
        gs_sb = ctx.sbuf("gs_sb", [128, NI], F32)
        us_sb = ctx.sbuf("us_sb", [128, NI], F32)
        ds_sb = ctx.sbuf("ds_sb", [128, D], F32)
        s.dma("sync", gs_sb[:, :], gsc[:, :], deps=guardB, sem="dC")
        s.dma("sync", us_sb[:, :], usc[:, :], sem="dC")
        s.dma("sync", ds_sb[:, :], dsb[:, :], sem="dC")
        c_ready_f = ("dC", s.cnt["dC"])
        stB2 = ExitStack()
        ctx.stack = stB2
        xt0 = ctx.sbuf("xtB0", [128, D], F32)
        xt1 = ctx.sbuf("xtB1", [128, D], F32)
        lbs = [ctx.sbuf(f"lbB{i}", [128, D], BF16) for i in range(4)]
        statb = ctx.sbuf("statB", [128, 16], F32)
        sqb = [ctx.sbuf("sqbBa", [128, D], BF16), ctx.sbuf("sqbBb", [128, D], BF16)]
        emit_ln_t(ctx, x1_d, TOWN // 128, lnt2, TOWN, [xt0, xt1], lbs, statb, sqb,
                  start_deps=sp1 + [s.now("tensor")])
        stB2.close()

        ln2_done = [s.now("tensor"), s.now("vector"), s.now("scalar")]
        if dbg:
            s.dma("scalar", dbg_lnt2[:, :], lnt2[:, :], deps=ln2_done, sem="dO")
        stC1 = ExitStack()
        ctx.stack = stC1
        nk = 16
        wg_f = ctx.sbuf("wg_f", [128, nk * 128], F32)
        wg_b = ctx.sbuf("wg_b", [128, nk * 128], BF16)
        wu_f = ctx.sbuf("wu_f", [128, nk * 128], F32)
        wu_b = ctx.sbuf("wu_b", [128, nk * 128], BF16)
        sg = ctx.sbuf("sg", [128, TOWN], F32)
        wg_free = wu_free = None
        sg_free = None
        for ic in range(NI):
            ev_dg = gather_w(ctx, gwt, ic * 128, 128, wg_f,
                             [wg_free] + (ln2_done if ic == 0 else []), sem="wA")
            ev_wg = conv_w(ctx, wg_f, wg_b, nk * 128, "scalar",
                           [ev_dg] + (ln2_done if ic == 0 else []))
            ev_du = gather_w(ctx, uwt, ic * 128, 128, wu_f,
                             [wu_free] + (ln2_done if ic == 0 else []), sem="uA")
            ev_wu = conv_w(ctx, wu_f, wu_b, nk * 128, "vector",
                           [ev_du] + (ln2_done if ic == 0 else []))
            sg_evs = []
            for ts_ in range(TOWN // 512):
                bidx, bank, bfree = pool.acquire()
                mm = None
                for k in range(nk):
                    mm = s.op("tensor", lambda e, k=k, ts_=ts_, bank=bank: e.matmul(
                        bank[:, :], wg_b[:, k * 128:(k + 1) * 128],
                        lnt2[:, k * TOWN + ts_ * 512: k * TOWN + ts_ * 512 + 512],
                        start=(k == 0), stop=(k == nk - 1)),
                        deps=([ev_wg, bfree] + ln2_done) if k == 0 else [])
                ev_sg = s.op("scalar", lambda e, ic=ic, ts_=ts_, bank=bank: e.activation(
                    sg[:, ts_ * 512:ts_ * 512 + 512], bank[:, :], AF.Silu,
                    scale=gs_sb[:, ic:ic + 1]), deps=[mm, sg_free, c_ready_f])
                pool.release(bidx, ev_sg)
                sg_evs.append(ev_sg)
                wg_free = mm
            hv = []
            for ts_ in range(TOWN // 512):
                bidx, bank, bfree = pool.acquire()
                mm = None
                for k in range(nk):
                    mm = s.op("tensor", lambda e, k=k, ts_=ts_, bank=bank: e.matmul(
                        bank[:, :], wu_b[:, k * 128:(k + 1) * 128],
                        lnt2[:, k * TOWN + ts_ * 512: k * TOWN + ts_ * 512 + 512],
                        start=(k == 0), stop=(k == nk - 1)),
                        deps=[ev_wu, bfree] if k == 0 else [])
                col0 = ic * TOWN + ts_ * 512
                ev_h = s.op("vector", lambda e, ic=ic, ts_=ts_, bank=bank, col0=col0: e.scalar_tensor_tensor(
                    hdn[:, col0:col0 + 512], bank[:, :], us_sb[:, ic:ic + 1],
                    sg[:, ts_ * 512:ts_ * 512 + 512], ALU.mult, ALU.mult),
                    deps=[mm, sg_evs[ts_], c_ready_f] + (guardB if ic == 0 else []))
                pool.release(bidx, ev_h)
                hv.append(ev_h)
                wu_free = mm
            sg_free = hv[-1]
        stC1.close()
        hdn_ready = [s.now("vector")]
        if dbg:
            s.dma("scalar", dbg_hdn[:, :], hdn[:, 0:4 * TOWN], deps=hdn_ready, sem="dO")
        stC2 = ExitStack()
        ctx.stack = stC2
        dw_f0 = ctx.sbuf("dw_f0", [128, 4 * 512], F32)
        dw_f1 = ctx.sbuf("dw_f1", [128, 4 * 512], F32)
        dbig = ctx.sbuf("dbig", [128, NI * 512], BF16)
        xr0 = ctx.sbuf("xrC0", [128, 512], F32)
        xr1 = ctx.sbuf("xrC1", [128, 512], F32)
        xo0 = ctx.sbuf("xoC0", [128, 512], F32)
        xo1 = ctx.sbuf("xoC1", [128, 512], F32)
        dview = dwt[:, :].rearrange("(k p) t -> p k t", p=128)
        dwf = [dw_f0, dw_f1]
        xrs, xos = [xr0, xr1], [xo0, xo1]
        gu_done = s.op("sync", lambda e: None, deps=[
            s.now("tensor"), s.now("vector"), s.now("scalar")])
        dwf_free = [gu_done, gu_done]
        dbig_free = gu_done
        xr_free = [gu_done, gu_done]
        xo_free = [gu_done, gu_done]
        for os_ in range(4):
            convs = []
            for bi, kb0 in enumerate(range(0, NI, 4)):
                nkb = min(4, NI - kb0)
                wi = bi % 2
                evd = s.dma("sync", dwf[wi][:, 0:nkb * 512].rearrange("p (k c) -> p k c", k=nkb),
                            dview[:, kb0:kb0 + nkb, os_ * 512:os_ * 512 + 512],
                            deps=[dwf_free[wi]], sem="wA" if wi == 0 else "wB")
                evc = s.op("vector" if wi else "scalar",
                           (lambda e, wi=wi, kb0=kb0, nkb=nkb: e.tensor_copy(
                               dbig[:, kb0 * 512:(kb0 + nkb) * 512],
                               dwf[wi][:, 0:nkb * 512])) if wi else
                           (lambda e, wi=wi, kb0=kb0, nkb=nkb: e.copy(
                               dbig[:, kb0 * 512:(kb0 + nkb) * 512],
                               dwf[wi][:, 0:nkb * 512])),
                           deps=[evd, dbig_free])
                convs.append(evc)
                dwf_free[wi] = evc
            for tc in range(TOWN // 128):
                xi = tc % 2
                evx = s.dma("scalar", xrs[xi][:, :],
                            x1_d[tc * 128:(tc + 1) * 128, os_ * 512:os_ * 512 + 512],
                            deps=[xr_free[xi]], sem="xA" if xi == 0 else "xB")
                bidx, bank, bfree = pool.acquire()
                mm = None
                for k in range(NI):
                    kdeps = [convs[k // 4]]
                    if k == 0:
                        kdeps += hdn_ready + [bfree]
                    mm = s.op("tensor", lambda e, k=k, tc=tc, bank=bank: e.matmul(
                        bank[:, :], hdn[:, k * TOWN + tc * 128: k * TOWN + tc * 128 + 128],
                        dbig[:, k * 512:(k + 1) * 512],
                        start=(k == 0), stop=(k == NI - 1)), deps=kdeps)
                if tc == TOWN // 128 - 1:
                    dbig_free = mm
                e1_ = s.op("vector", lambda e, bank=bank, os_=os_, xi=xi, xos=xos: e.scalar_tensor_tensor(
                    xos[xi][:, :], bank[:, :], 1.0,
                    ds_sb[:, os_ * 512:os_ * 512 + 512], ALU.bypass, ALU.mult),
                    deps=[mm, xo_free[xi], c_ready_f])
                e2_ = s.op("vector", lambda e, xi=xi, xos=xos, xrs=xrs: e.tensor_add(
                    xos[xi][:, :], xos[xi][:, :], xrs[xi][:, :]),
                    deps=[evx], strict=[e1_])
                pool.release(bidx, e2_)
                evd2 = s.dma("scalar", x2_d[tc * 128:(tc + 1) * 128, os_ * 512:os_ * 512 + 512],
                             xos[xi][:, :], deps=[e2_],
                             sem="sA" if xi == 0 else "sB")
                xo_free[xi] = evd2
                xr_free[xi] = evd2
        stC2.close()
        stB_out.close()
        guardC = [("sA", s.cnt["sA"]), ("sB", s.cnt["sB"]),
                  s.now("tensor"), s.now("vector"), s.now("scalar")]

        stD = ExitStack()
        ctx.stack = stD
        cos_sb = ctx.sbuf("cos_sb", [128, S], F32)
        srot_sb = ctx.sbuf("srot_sb", [128, S], F32)
        prot_sb = ctx.sbuf("prot_sb", [128, 128], BF16)
        s.dma("sync", cos_sb[:, :], c_cos[:, :], deps=guardC, sem="dC")
        s.dma("sync", srot_sb[:, :], c_srot[:, :], sem="dC")
        s.dma("sync", prot_sb[:, :], c_prot[:, :], sem="dC")
        ctx.consts.update(cos=cos_sb, srot=srot_sb, prot=prot_sb)
        ctx.consts["ready"] = ("dC", s.cnt["dC"])
        QT1 = ctx.sbuf("QT1", [128, NH * S], BF16)
        KT1 = ctx.sbuf("KT1", [128, 2 * S], BF16)
        VT1 = ctx.sbuf("VT1", [128, (S // 128) * 256], BF16)
        stD1 = ExitStack()
        ctx.stack = stD1
        lnt3 = ctx.sbuf("lnt3", [128, 16 * S], BF16)
        stD1a = ExitStack()
        ctx.stack = stD1a
        lnt3hs = [ctx.sbuf(f"lnt3h{g}", [128, 16 * 256], BF16) for g in range(4)]
        xt0 = ctx.sbuf("xtD0", [128, D], F32)
        xt1 = ctx.sbuf("xtD1", [128, D], F32)
        lbs = [ctx.sbuf(f"lbD{i}", [128, D], BF16) for i in range(4)]
        statb = ctx.sbuf("statD", [128, 16], F32)
        sqb = [ctx.sbuf("sqbDa", [128, D], BF16), ctx.sbuf("sqbDb", [128, D], BF16)]
        ccs2 = []
        for g in range(4):
            lnow = guardC if g == 0 else [s.now("vector"), s.now("scalar")]
            emit_ln_t(ctx, x2_d[g * 256:(g + 1) * 256, :], 2, lnt3hs[g], 256,
                      [xt0, xt1], lbs, statb, sqb,
                      start_deps=lnow)
            lng = [s.now("tensor"), s.now("vector"), s.now("scalar")]
            stg = dS_dma("scalar",
                         e2_in[g, :, :].rearrange("(f p) t -> p f t", p=128),
                         lnt3hs[g][:, :].rearrange("p (f t) -> p f t", f=16),
                         deps=lng)
            ccs2.append(s.cc(lambda e, g=g: e.collective_compute(
                "AllGather", ALU.bypass, replica_groups=RG,
                ins=[e2_in[g, :, :]], outs=[e2_out[g, :, :]]),
                f"cc2_{g}", deps=[stg]))
        ev_st2 = s.now("scalar")
        cc2 = ccs2[-1]
        stD1a.close()
        rbs = []
        for g in range(4):
            rbs.append(s.dma(
                "sync",
                lnt3[:, :].rearrange("p (f t) -> p f t", f=16)
                [:, :, g * 256:(g + 1) * 256],
                e2_out[g, 0:D, :].rearrange("(f p) t -> p f t", p=128),
                deps=[ccs2[g], s.now("scalar"), s.now("vector"),
                      rbs[-1] if rbs else None], sem="dR"))
            rbs.append(s.dma(
                "sync",
                lnt3[:, :].rearrange("p (f t) -> p f t", f=16)
                [:, :, 1024 + g * 256:1024 + (g + 1) * 256],
                e2_out[g, D:2 * D, :].rearrange("(f p) t -> p f t", p=128),
                deps=[rbs[-1]], sem="dR"))
        ln3f_done = rbs[-2:]
        rb1, rb2 = rbs[0], rbs[1]
        stD1b = ExitStack()
        ctx.stack = stD1b
        vwf0 = ctx.sbuf("vwfD", [128, 8 * 256], F32)
        vwb = ctx.sbuf("vwbD", [128, 16 * 256], BF16)
        emit_proj_tm(ctx, lnt3, S, S, wv1, 256, VT1, 256, [vwf0], vwb,
                     ready=[ev_st2, s.now("tensor"), s.now("vector")],
                     ready_mm=ln3f_done, tag="v1")
        stD1b.close()
        stD1c = ExitStack()
        ctx.stack = stD1c
        qwf0 = ctx.sbuf("qwfD0", [128, 16 * 128], F32)
        qwf1 = ctx.sbuf("qwfD1", [128, 16 * 128], F32)
        qwb0 = ctx.sbuf("qwbD0", [128, 16 * 128], BF16)
        qwb1 = ctx.sbuf("qwbD1", [128, 16 * 128], BF16)
        rp0 = ctx.sbuf("rp0", [128, 512], F32)
        rp1 = ctx.sbuf("rp1", [128, 512], BF16)
        rp2 = ctx.sbuf("rp2", [128, 512], F32)
        rp3 = ctx.sbuf("rp3", [128, 512], F32)
        rope = {"bufs": (rp0, rp1, rp2, rp3)}
        v1_done = [s.now("tensor")]
        emit_proj_fm(ctx, lnt3, S, S, wq1, 1024, QT1, S, [qwf0, qwf1], [qwb0, qwb1],
                     rope=rope, ready=ln3f_done + v1_done, tag="q1")
        emit_proj_fm(ctx, lnt3, S, S, wk1, 256, KT1, S, [qwf0, qwf1], [qwb0, qwb1],
                     rope=rope, ready=ln3f_done + [s.now("tensor")], tag="k1")
        stD1c.close()
        stD1.close()
        stD2 = ExitStack()
        ctx.stack = stD2
        masksD = ctx.sbuf("masksD", [128, 4 * 512], BF16)
        evmD = s.dma("sync", masksD[:, :], c_maskneg[:, :], deps=[s.now("tensor")],
                     sem="dC")
        ctx.consts["masks"] = masksD
        OT1 = ctx.sbuf("OT1", [128, NH * S], BF16)
        esb1 = [ctx.sbuf(f"esbD{j}", [128, 512], BF16) for j in range(32)]
        ready1 = [s.now("scalar"), s.now("vector"), evmD]
        ccs3 = []
        sts3 = []

        def mid3(ev_half):
            g = len(ccs3)
            st = dS_dma("scalar",
                        e3_in[g, :, :].rearrange("(h p) t -> p h t", p=128),
                        OT1[:, :].rearrange("p (h t) -> p h t", h=NH)
                        [:, :, g * 512:(g + 1) * 512],
                        deps=[ev_half])
            sts3.append(st)
            ccs3.append(s.cc(lambda e, g=g: e.collective_compute(
                "AllGather", ALU.bypass, replica_groups=RG,
                ins=[e3_in[g, :, :]], outs=[e3_out[g, :, :]]),
                f"cc3_{g}", deps=[st]))

        ev_a1 = emit_attention(ctx, QT1, KT1, VT1, OT1, lambda h: h // 4, 256,
                               "linear", esb1, ready1, tag="a1", mid_cb=mid3)
        mid3(ev_a1)
        ev_st3, cc3 = sts3[-1], ccs3[-1]
        stD2.close()
        stD.close()

        guardD = [ev_st3, s.now("tensor"), s.now("vector"), s.now("scalar")]
        stE = ExitStack()
        ctx.stack = stE
        sp3 = wo_phase(e3_out, ccs3, wo1, None, x2_d, x3_d, "e", guard=guardD)
        stE.close()
        guardE = sp3 + [s.now("tensor"), s.now("vector"), s.now("scalar")]

        stF = ExitStack()
        ctx.stack = stF
        b1_sb = ctx.sbuf("b1_sb", [128, D], F32)
        ev_b1 = s.dma("sync", b1_sb[:, :], b1b[:, :], deps=guardE, sem="dC")
        lnt4 = ctx.sbuf("lnt4", [128, 16 * TOWN], BF16)
        signT = ctx.sbuf("signT", [128, 16 * D], BF16)
        alphab = ctx.sbuf("alphab", [128, D], F32)
        stF1 = ExitStack()
        ctx.stack = stF1
        xt0 = ctx.sbuf("xtF0", [128, D], F32)
        xt1 = ctx.sbuf("xtF1", [128, D], F32)
        lbs = [ctx.sbuf(f"lbF{i}", [128, D], BF16) for i in range(4)]
        statb = ctx.sbuf("statF", [128, 16], F32)
        sqb = [ctx.sbuf("sqbFa", [128, D], BF16), ctx.sbuf("sqbFb", [128, D], BF16)]
        wsl = [ctx.sbuf("wsl0", [128, D], F32), ctx.sbuf("wsl1", [128, D], F32)]
        absl = [ctx.sbuf("absl0", [128, D], BF16), ctx.sbuf("absl1", [128, D], BF16)]
        alr_f = ctx.sbuf("alr_f", [1, D], F32)
        alr_b = ctx.sbuf("alr_b", [1, D], BF16)
        onescol = ctx.consts["onescol"]
        emit_ln_t(ctx, x3_d, TOWN // 128, lnt4, TOWN, [xt0, xt1], lbs, statb, sqb,
                  start_deps=guardE)
        ln_done = [s.now("vector"), s.now("scalar")]
        ev_zero = s.op("vector", lambda e: e.memset(alr_f[0:1, :], 0.0),
                       deps=guardE)
        acc_prev = ev_zero
        sign_evs = []
        wsl_free = [[], []]
        absl_free = [None, None]
        for k in range(16):
            sl = k % 2
            evw = s.dma("sync" if sl == 0 else "gpsimd",
                        wsl[sl][:, :], w1t[k * 128:(k + 1) * 128, :],
                        deps=wsl_free[sl] + (guardE if k < 2 else []),
                        sem="wA" if sl == 0 else "wC")
            evsg = s.op("scalar", lambda e, k=k, sl=sl: e.activation(
                signT[:, k * D:(k + 1) * D], wsl[sl][:, :], AF.Sign), deps=[evw])
            sign_evs.append(evsg)
            eva = s.op("vector", lambda e, k=k, sl=sl: e.tensor_tensor(
                absl[sl][:, :], wsl[sl][:, :], signT[:, k * D:(k + 1) * D],
                ALU.mult), deps=[evsg, absl_free[sl]])
            wsl_free[sl] = [evsg, eva]
            mm_a = None
            for q4 in range(4):
                bidx, bank, bfree = pool.acquire()
                mm_a = s.op("tensor", lambda e, bank=bank, sl=sl, q4=q4: e.matmul(
                    bank[0:1, :], onescol[:, 0:1],
                    absl[sl][:, q4 * 512:(q4 + 1) * 512],
                    start=True, stop=True), deps=[eva, bfree])
                e_acc = s.op("vector", lambda e, bank=bank, q4=q4: e.tensor_add(
                    alr_f[0:1, q4 * 512:(q4 + 1) * 512],
                    alr_f[0:1, q4 * 512:(q4 + 1) * 512], bank[0:1, :]),
                    deps=[mm_a], strict=[acc_prev])
                acc_prev = e_acc
                pool.release(bidx, e_acc)
            absl_free[sl] = mm_a
        e1a = s.op("vector", lambda e: e.tensor_scalar_mul(
            alr_f[0:1, :], alr_f[0:1, :], 1.0 / D), strict=[acc_prev])
        e2a = s.op("scalar", lambda e: e.copy(alr_b[0:1, :], alr_f[0:1, :]),
                   deps=[e1a])
        for q4 in range(4):
            bidx, bank, bfree = pool.acquire()
            e3a = s.op("tensor", lambda e, q4=q4, bank=bank: e.matmul(
                bank[:, :], onesrow[0:1, :], alr_b[0:1, q4 * 512:(q4 + 1) * 512],
                start=True, stop=True), deps=[e2a, bfree])
            e4a = s.op("scalar", lambda e, q4=q4, bank=bank: e.copy(
                alphab[:, q4 * 512:(q4 + 1) * 512], bank[:, :]), deps=[e3a])
            pool.release(bidx, e4a)
        f2_done = s.op("sync", lambda e: None, deps=[
            s.now("tensor"), s.now("vector"), s.now("scalar"), s.now("gpsimd")])
        stF1.close()
        alpha_ready = s.now("scalar")
        ready_mm = ln_done
        stF3 = ExitStack()
        ctx.stack = stF3
        xo0 = ctx.sbuf("xoF0", [128, 512], F32)
        xo1 = ctx.sbuf("xoF1", [128, 512], F32)
        xr0 = ctx.sbuf("xrF0", [128, 512], F32)
        xr1 = ctx.sbuf("xrF1", [128, 512], F32)
        xosF, xrsF = [xo0, xo1], [xr0, xr1]
        xo_free = [None, None]
        xr_free = [f2_done, f2_done]
        nn = 0
        for os_ in range(4):
            for tc in range(TOWN // 128):
                bidx, bank, bfree = pool.acquire()
                mm = None
                for k in range(16):
                    kdeps = [sign_evs[k]]
                    if k == 0:
                        kdeps += ready_mm + [bfree]
                    mm = s.op("tensor", lambda e, k=k, tc=tc, os_=os_, bank=bank: e.matmul(
                        bank[:, :], lnt4[:, k * TOWN + tc * 128: k * TOWN + tc * 128 + 128],
                        signT[:, k * D + os_ * 512: k * D + os_ * 512 + 512],
                        start=(k == 0), stop=(k == 15)), deps=kdeps)
                xi = nn % 2
                nn += 1
                evx = s.dma("sync", xrsF[xi][:, :],
                            x3_d[tc * 128:(tc + 1) * 128, os_ * 512:os_ * 512 + 512],
                            deps=[xr_free[xi]], sem="xA" if xi == 0 else "xB")
                e1_ = s.op("vector", lambda e, bank=bank, os_=os_, xi=xi, xosF=xosF: e.scalar_tensor_tensor(
                    xosF[xi][:, :], bank[:, :], 1.0,
                    alphab[:, os_ * 512:os_ * 512 + 512], ALU.bypass, ALU.mult),
                    deps=[mm, xo_free[xi], alpha_ready])
                e2_ = s.op("vector", lambda e, xi=xi, xosF=xosF, xrsF=xrsF: e.tensor_add(
                    xosF[xi][:, :], xosF[xi][:, :], xrsF[xi][:, :]),
                    deps=[evx], strict=[e1_])
                e3_ = s.op("vector", lambda e, os_=os_, xi=xi, xosF=xosF: e.tensor_add(
                    xosF[xi][:, :], xosF[xi][:, :],
                    b1_sb[:, os_ * 512:os_ * 512 + 512]),
                    deps=[ev_b1], strict=[e2_])
                pool.release(bidx, e3_)
                evd = s.dma("scalar", xout[tc * 128:(tc + 1) * 128, os_ * 512:os_ * 512 + 512],
                            xosF[xi][:, :], deps=[e3_],
                            sem="sA" if xi == 0 else "sB")
                xo_free[xi] = evd
                xr_free[xi] = e2_
        stF3.close()
        stF.close()
        if dbg:
            pass
        s.op("sync", lambda e: None, deps=[
            ("dO", s.cnt["dO"]), ("sA", s.cnt["sA"]), ("sB", s.cnt["sB"])])
        s.finalize()
    return nc


def kernel(**inputs):
    ii = {k: np.asarray(v) for k, v in inputs.items()}
    X = np.ascontiguousarray(ii["L0_attn_X"])
    cR = _consts(rope=True)
    nc = build_fused()
    gwt = np.ascontiguousarray(ii["L0_ffn_gate_w"].T)
    uwt = np.ascontiguousarray(ii["L0_ffn_up_w"].T)
    dwt = np.ascontiguousarray(ii["L0_ffn_down_w"].T)
    gsc = np.ascontiguousarray(ii["L0_ffn_gate_s"].reshape(NI, 128).T)
    usc = np.ascontiguousarray(ii["L0_ffn_up_s"].reshape(NI, 128).T)
    dsb = np.ascontiguousarray(np.broadcast_to(ii["L0_ffn_down_s"][None, :], (128, D)))
    wq1 = np.ascontiguousarray(ii["L1_attn_Wq"].T)
    wk1 = np.ascontiguousarray(ii["L1_attn_Wk"].T)
    wv1 = np.ascontiguousarray(ii["L1_attn_Wv"].T)
    wo1 = np.ascontiguousarray(ii["L1_attn_Wo"].T)
    w1t = np.ascontiguousarray(ii["L1_ffn_weight"].T)
    b1b = np.ascontiguousarray(np.broadcast_to(ii["L1_ffn_bias"][None, :], (128, D)))
    bor = np.ascontiguousarray(ii["L0_attn_bo"].reshape(1, D))
    maps = _pc(lambda b, r: dict(
        xfull=X[b],
        xres=np.ascontiguousarray(X[b, r * 1024:(r + 1) * 1024]),
        toff=np.array([[r]], dtype=np.int32),
        wq=np.ascontiguousarray(ii["L0_attn_Wq"][:, r * 1024:(r + 1) * 1024]),
        wk=np.ascontiguousarray(ii["L0_attn_Wk"][:, r * 1024:(r + 1) * 1024]),
        wv=np.ascontiguousarray(ii["L0_attn_Wv"][:, r * 1024:(r + 1) * 1024]),
        bqc=np.ascontiguousarray(ii["L0_attn_bq"][r * 1024:(r + 1) * 1024].reshape(8, 128).T),
        bkc=np.ascontiguousarray(ii["L0_attn_bk"][r * 1024:(r + 1) * 1024].reshape(8, 128).T),
        bvr=np.ascontiguousarray(ii["L0_attn_bv"][r * 1024:(r + 1) * 1024].reshape(1, 1024)),
        wo=ii["L0_attn_Wo"], bor=bor,
        gwt=gwt, uwt=uwt, dwt=dwt, gsc=gsc, usc=usc, dsb=dsb,
        wq1=np.ascontiguousarray(wq1[:, r * 1024:(r + 1) * 1024]),
        wk1=np.ascontiguousarray(wk1[:, r * 256:(r + 1) * 256]),
        wv1=np.ascontiguousarray(wv1[:, r * 256:(r + 1) * 256]),
        wo1=wo1, w1t=w1t, b1b=b1b,
        **cR))
    res = _run(nc, maps)
    out = np.empty((B, S, D), dtype=np.float32)
    for b in range(B):
        out[b, 0:1024] = res[2 * b]["xout"]
        out[b, 1024:2048] = res[2 * b + 1]["xout"]
    return out


# version 33
# speedup vs baseline: 1.1455x; 1.1455x over previous
import math
import numpy as np
import ml_dtypes
import concourse.bass as bass
import concourse.mybir as mybir
from concourse.bass_utils import run_bass_kernel_spmd
from contextlib import ExitStack

F32 = mybir.dt.float32
BF16 = mybir.dt.bfloat16
AF = mybir.ActivationFunctionType
ALU = mybir.AluOpType
AX = mybir.AxisListType
bfloat16 = ml_dtypes.bfloat16

NCORES = 8
D = 2048
S = 2048
B = 4
TOWN = 1024
NH = 8
DH = 128
INTER = 5632
NI = INTER // 128
EPS_LN = 1e-5
EPS_L1 = 1e-6

ENGINES = ("sync", "scalar", "vector", "tensor", "gpsimd")


class Sched:
    def __init__(self, nc, stack):
        self.nc = nc
        self.sems = {}
        self.cnt = {}
        self.queues = {e: [] for e in ENGINES}
        self.last_wait = {e: {} for e in ENGINES}
        self._stack = stack
        for e in ENGINES:
            self._mksem("e_" + e)
        for d in ("dW", "dX", "dO", "dC", "dS", "wA", "wB", "wC", "uA", "uB",
                  "xA", "xB", "sA", "sB", "dT", "dY", "dR", "cc2",
                  "cc1_0", "cc1_1", "cc1_2", "cc1_3",
                  "cc3_0", "cc3_1", "cc3_2", "cc3_3",
                  "cc2_0", "cc2_1", "cc2_2", "cc2_3"):
            self._mksem(d)

    def _mksem(self, name):
        self.sems[name] = self._stack.enter_context(self.nc.semaphore(name))
        self.cnt[name] = 0

    def now(self, eng):
        return ("e_" + eng, self.cnt["e_" + eng])

    def _wait(self, eng, deps, strict=False):
        for dep in deps:
            if dep is None:
                continue
            sem, val = dep
            if val <= 0 or (not strict and sem == "e_" + eng):
                continue
            lw = self.last_wait[eng]
            if lw.get(sem, 0) >= val:
                continue
            lw[sem] = val
            self.queues[eng].append(("wait", sem, val))

    def op(self, eng, fn, deps=(), strict=()):
        self._wait(eng, deps)
        self._wait(eng, strict, strict=True)
        sem = "e_" + eng
        self.cnt[sem] += 1
        self.queues[eng].append(("call", fn, sem, 1))
        return (sem, self.cnt[sem])

    def cc(self, fn, ccsem, deps=()):
        self._wait("gpsimd", deps)
        self.cnt[ccsem] += 1
        self.queues["gpsimd"].append(("call", fn, ccsem, None))
        return (ccsem, 1)

    def dma(self, eng, out, in_, deps=(), sem="dW", strict=()):
        self._wait(eng, deps, strict=True)
        self._wait(eng, strict, strict=True)
        self.cnt[sem] += 16
        self.queues[eng].append(
            ("call", lambda e, o=out, i=in_: e.dma_start(out=o, in_=i), sem, 16))
        return (sem, self.cnt[sem])

    def dyn_dma(self, eng, fn, deps=(), sem="dX"):
        self._wait(eng, deps, strict=True)
        self.cnt[sem] += 16
        self.queues[eng].append(("call", fn, sem, 16))
        return (sem, self.cnt[sem])

    def finalize(self):
        nc = self.nc
        with nc.Block() as block:
            for eng_name in ENGINES:
                items = self.queues[eng_name]

                def make(items=items):
                    def body(eng):
                        for it in items:
                            if it[0] == "wait":
                                eng.wait_ge(self.sems[it[1]], it[2])
                            else:
                                _, fn, sem, amt = it
                                ins = fn(eng)
                                if ins is None:
                                    ins = eng.nop()
                                if amt is None:
                                    ins.then_inc(self.sems[sem])
                                else:
                                    ins.then_inc(self.sems[sem], amt)
                    return body

                getattr(block, eng_name)(make())


class Pool:

    def __init__(self, nc, stack, n=8):
        self.free = []
        for i in range(n):
            t = stack.enter_context(nc.psum_tensor(f"pb{i}", [128, 512], F32))
            self.free.append((i, t, None))
        self.held = {}

    def acquire(self):
        assert self.free, "psum pool exhausted"
        i, t, ev = self.free.pop(0)
        self.held[i] = t
        return i, t, ev

    def release(self, idx, ev):
        t = self.held.pop(idx)
        self.free.append((idx, t, ev))


class Ctx:
    def __init__(self, nc, s, pool, stack):
        self.nc = nc
        self.s = s
        self.pool = pool
        self.stack = stack
        self.consts = {}

    def sbuf(self, name, shape, dtype):
        return self.stack.enter_context(self.nc.sbuf_tensor(name, shape, dtype))



def stage_consts(ctx, rope=False, masks=True):
    nc, s = ctx.nc, ctx.s
    names = [("c_ident", [128, 128], BF16, "ident"),
             ("c_onescol", [128, 1], BF16, "onescol"),
             ("c_onesrow", [1, 128], BF16, "onesrow")]
    if masks:
        names.append(("c_masks", [128, 4 * 512], BF16, "masks"))
    if rope:
        names.append(("c_cos", [128, S], F32, "cos"))
        names.append(("c_srot", [128, S], F32, "srot"))
        names.append(("c_prot", [128, 128], BF16, "prot"))
    for dn, shape, dt_, key in names:
        d = nc.dram_tensor(dn, shape, dt_, kind="ExternalInput")
        sb = ctx.sbuf(key + "_sb", shape, dt_)
        s.dma("sync", sb[:, :], d[:, :], sem="dC")
        ctx.consts[key] = sb
    ctx.consts["ready"] = ("dC", s.cnt["dC"])


def emit_ln_t(ctx, x_dram, ntiles, lnt, lnt_T, xt, lb, stat, sq,
              start_deps=(), t_off=0):
    nc, s, pool = ctx.nc, ctx.s, ctx.pool
    ident = ctx.consts["ident"]
    cready = ctx.consts["ready"]
    x_free = [None, None]
    lb_free = [None] * 4
    sq_free = [None, None]
    prevs = [{}, {}]
    ev_last = None
    apply_evs = []
    for g in range((ntiles + 3) // 4):
        n_in_g = min(4, ntiles - g * 4)
        for ii in range(n_in_g):
            i = g * 4 + ii
            xi = i % 2
            b = 8 * xi
            prev = prevs[xi]
            evx = s.dma("scalar" if xi == 0 else "sync",
                        xt[xi][:, :], x_dram[i * 128:(i + 1) * 128, :],
                        deps=[x_free[xi]] + list(start_deps),
                        sem="xA" if xi == 0 else "xB")
            ev_sum = s.op("vector", lambda e, xi=xi, b=b: e.reduce_sum(
                stat[:, b:b + 1], xt[xi][:, :], AX.X), deps=[evx],
                strict=[prev.get("e1")])
            ev_sq0 = s.op("scalar", lambda e, xi=xi: e.activation(
                sq[xi][:, :], xt[xi][:, :], AF.Square), deps=[evx, sq_free[xi]],
                strict=[prev.get("sq0")])
            ev_sq = s.op("vector", lambda e, xi=xi, b=b: e.reduce_sum(
                stat[:, b + 1:b + 2], sq[xi][:, :], AX.X), deps=[ev_sq0],
                strict=[ev_sum, prev.get("e2")])
            sq_free[xi] = ev_sq
            e1 = s.op("vector", lambda e, b=b: e.tensor_scalar_mul(
                stat[:, b + 2:b + 3], stat[:, b:b + 1], 1.0 / D),
                strict=[ev_sum, prev.get("e7")])
            e2 = s.op("vector", lambda e, b=b: e.tensor_scalar_mul(
                stat[:, b + 3:b + 4], stat[:, b + 1:b + 2], 1.0 / D),
                strict=[ev_sq, prev.get("e4")])
            e3 = s.op("vector", lambda e, b=b: e.scalar_tensor_tensor(
                stat[:, b + 7:b + 8], stat[:, b + 2:b + 3], 1.0,
                stat[:, b + 2:b + 3], ALU.bypass, ALU.mult),
                strict=[e1, prev.get("e4")])
            e4 = s.op("vector", lambda e, b=b: e.tensor_sub(
                stat[:, b + 4:b + 5], stat[:, b + 3:b + 4], stat[:, b + 7:b + 8]),
                deps=[prev.get("e5")], strict=[e2, e3])
            e5 = s.op("scalar", lambda e, b=b: e.activation(
                stat[:, b + 5:b + 6], stat[:, b + 4:b + 5], AF.Ln,
                bias=np.float32(EPS_LN)),
                deps=[e4], strict=[prev.get("e8s"), prev.get("e6")])
            e6 = s.op("scalar", lambda e, b=b: e.activation(
                stat[:, b + 5:b + 6], stat[:, b + 5:b + 6], AF.Exp,
                scale=np.float32(-0.5)), strict=[e5])
            e7 = s.op("vector", lambda e, b=b: e.scalar_tensor_tensor(
                stat[:, b + 6:b + 7], stat[:, b + 2:b + 3], -1.0,
                stat[:, b + 5:b + 6], ALU.mult, ALU.mult),
                deps=[e6, prev.get("e8v")], strict=[e1])
            if xi == 0:
                e8 = s.op("scalar", lambda e, xi=xi, ii=ii, b=b: e.activation(
                    lb[ii][:, :], xt[xi][:, :], AF.Identity,
                    bias=stat[:, b + 6:b + 7], scale=stat[:, b + 5:b + 6]),
                    deps=[e7, lb_free[ii]], strict=[e6])
                prev_e8 = dict(e8s=e8)
            else:
                e8 = s.op("vector", lambda e, xi=xi, ii=ii, b=b: e.tensor_scalar(
                    lb[ii][:, :], xt[xi][:, :], stat[:, b + 5:b + 6],
                    stat[:, b + 6:b + 7], ALU.mult, ALU.add),
                    deps=[lb_free[ii], e6], strict=[e7])
                prev_e8 = dict(e8v=e8)
            prevs[xi] = dict(e1=e1, e2=e2, e4=e4, e5=e5, e6=e6, e7=e7,
                             sq0=ev_sq0, **prev_e8)
            x_free[xi] = e8
            apply_evs.append(e8)
        grp_last = None
        for f in range(16):
            bidx, bank, bfree = pool.acquire()
            mm = None
            for ii in range(n_in_g):
                mm = s.op("tensor", lambda e, ii=ii, f=f, bank=bank: e.matmul(
                    bank[:, ii * 128:(ii + 1) * 128],
                    lb[ii][:, f * 128:(f + 1) * 128], ident[:, :],
                    start=True, stop=True),
                    deps=[apply_evs[-n_in_g + ii], bfree, cready])
            eng = "vector" if (f % 2 == 0) else "scalar"
            col0 = f * lnt_T + t_off + g * 512
            w = n_in_g * 128
            if eng == "vector":
                ev = s.op(eng, lambda e, bank=bank, col0=col0, w=w: e.tensor_copy(
                    lnt[:, col0:col0 + w], bank[:, 0:w]), deps=[mm])
            else:
                ev = s.op(eng, lambda e, bank=bank, col0=col0, w=w: e.copy(
                    lnt[:, col0:col0 + w], bank[:, 0:w]), deps=[mm])
            pool.release(bidx, ev)
            grp_last = ev
            ev_last = ev
        for ii in range(n_in_g):
            lb_free[ii] = grp_last
    return ev_last


def gather_w(ctx, w_dram, col0, ncols, wf, deps, sem="wA"):
    nk = w_dram.shape[0] // 128
    view = w_dram[:, :].rearrange("(k p) t -> p k t", p=128)
    return ctx.s.dma("sync", wf[:, 0:nk * ncols].rearrange("p (k c) -> p k c", k=nk),
                     view[:, :, col0:col0 + ncols], deps=deps, sem=sem)


def conv_w(ctx, wf, wb, n, eng, deps):
    if eng == "vector":
        return ctx.s.op("vector", lambda e: e.tensor_copy(wb[:, 0:n], wf[:, 0:n]),
                        deps=deps)
    return ctx.s.op("scalar", lambda e: e.copy(wb[:, 0:n], wf[:, 0:n]), deps=deps)


def emit_proj_fm(ctx, lnt, lnt_T, Tq, w_dram, O, out, out_T, wfbufs, wbbufs,
                 bias_cols=None, rope=None, ready=(), tag="q"):
    nc, s, pool = ctx.nc, ctx.s, ctx.pool
    nk = D // 128
    noc = O // 128
    wfree = [None, None]
    if rope is not None:
        qc, qx, qr, qe = rope["bufs"]
        cos, srot = ctx.consts["cos"], ctx.consts["srot"]
        prot = ctx.consts["prot"]
        rfree = None
    for oc in range(noc):
        wi = oc % 2
        wf, wb = wfbufs[wi], wbbufs[wi]
        ev_d = gather_w(ctx, w_dram, oc * 128, 128, wf,
                        [wfree[wi]] + (list(ready) if oc < 2 else []),
                        sem="wA" if wi == 0 else "wB")
        ev_w = conv_w(ctx, wf, wb, nk * 128, "vector" if wi else "scalar",
                      [ev_d] + (list(ready) if oc < 2 else []))
        last_mm = None
        for ts in range(Tq // 512):
            bidx, bank, bfree = pool.acquire()
            mm = None
            for k in range(nk):
                mm = s.op("tensor", lambda e, k=k, ts=ts, bank=bank, wb=wb: e.matmul(
                    bank[:, :], wb[:, k * 128:(k + 1) * 128],
                    lnt[:, k * lnt_T + ts * 512: k * lnt_T + ts * 512 + 512],
                    start=(k == 0), stop=(k == nk - 1)),
                    deps=([ev_w, bfree] + list(ready)) if k == 0 else [])
            last_mm = mm
            col0 = oc * out_T + ts * 512
            if rope is None:
                if bias_cols is not None:
                    ev = s.op("scalar", lambda e, bank=bank, col0=col0, oc=oc: e.activation(
                        out[:, col0:col0 + 512], bank[:, :], AF.Identity,
                        bias=bias_cols[:, oc:oc + 1]), deps=[mm])
                else:
                    ev = s.op("scalar", lambda e, bank=bank, col0=col0: e.copy(
                        out[:, col0:col0 + 512], bank[:, :]), deps=[mm])
                pool.release(bidx, ev)
            else:
                t0 = ts * 512
                e1 = s.op("vector", lambda e, bank=bank, t0=t0: e.scalar_tensor_tensor(
                    qc[:, :], bank[:, :], 1.0, cos[:, t0:t0 + 512],
                    ALU.bypass, ALU.mult), deps=[mm, rfree])
                e2 = s.op("vector", lambda e, bank=bank, t0=t0: e.scalar_tensor_tensor(
                    qx[:, :], bank[:, :], 1.0, srot[:, t0:t0 + 512],
                    ALU.bypass, ALU.mult), strict=[e1])
                pool.release(bidx, e2)
                bR_idx, bR, bR_free = pool.acquire()
                mmrot = s.op("tensor", lambda e, bR=bR: e.matmul(
                    bR[:, :], prot[:, :], qx[:, :], start=True, stop=True),
                    deps=[e2, bR_free])
                e3 = s.op("vector", lambda e, bR=bR: e.tensor_add(
                    qr[:, :], qc[:, :], bR[:, :]), deps=[mmrot])
                pool.release(bR_idx, e3)
                e4 = s.op("vector", lambda e: e.tensor_scalar_min(
                    qc[:, :], qr[:, :], 0.0), strict=[e3])
                e5 = s.op("scalar", lambda e: e.activation(
                    qe[:, :], qc[:, :], AF.Exp), deps=[e4])
                ev = s.op("vector", lambda e, col0=col0: e.scalar_tensor_tensor(
                    out[:, col0:col0 + 512], qr[:, :], 0.0, qe[:, :],
                    ALU.max, ALU.add), deps=[e5], strict=[e4])
                rfree = ev
        wfree[wi] = last_mm
    return s.now("tensor")


def emit_proj_tm(ctx, lnt, lnt_T, Tq, w_dram, O, out, out_C, wf, wbig,
                 bias_row=None, evac=None, ready=(), ready_mm=None, tag="v"):
    nc, s, pool = ctx.nc, ctx.s, ctx.pool
    onesrow = ctx.consts["onesrow"]
    nk = D // 128
    ntc = Tq // 128
    nos = max(1, O // 512)
    ow = min(512, O)
    wf_free = [None, None]
    wbig_free = None
    ev_out = None
    prev_conv = None
    for os_ in range(nos):
        convs = []
        for half in range(2):
            wfh = wf[half % len(wf)]
            view = w_dram[:, :].rearrange("(k p) t -> p k t", p=128)
            ev_d = s.dma("sync", wfh[:, 0:8 * ow].rearrange("p (k c) -> p k c", k=8),
                         view[:, half * 8:half * 8 + 8, os_ * ow:os_ * ow + ow],
                         deps=[wf_free[half], prev_conv] +
                              (list(ready) if os_ == 0 else []),
                         sem="wA")
            ev_c = s.op("vector" if half else "scalar",
                        (lambda e, wfh=wfh, half=half: e.tensor_copy(
                            wbig[:, half * 8 * ow:(half + 1) * 8 * ow], wfh[:, 0:8 * ow]))
                        if half else
                        (lambda e, wfh=wfh, half=half: e.copy(
                            wbig[:, half * 8 * ow:(half + 1) * 8 * ow], wfh[:, 0:8 * ow])),
                        deps=[ev_d, wbig_free])
            convs.append(ev_c)
            prev_conv = ev_c
        for tc in range(ntc):
            bidx, bank, bfree = pool.acquire()
            mm = None
            for k in range(nk):
                mm = s.op("tensor", lambda e, k=k, tc=tc, bank=bank: e.matmul(
                    bank[:, 0:ow], lnt[:, k * lnt_T + tc * 128: k * lnt_T + tc * 128 + 128],
                    wbig[:, k * ow:(k + 1) * ow],
                    start=(k == 0), stop=(k == nk - 1 and bias_row is None)),
                    deps=(convs + [bfree] +
                          list(ready if ready_mm is None else ready_mm))
                    if k == 0 else [])
            if bias_row is not None:
                mm = s.op("tensor", lambda e, bank=bank, os_=os_: e.matmul(
                    bank[:, 0:ow], onesrow[0:1, :],
                    bias_row[0:1, os_ * ow:os_ * ow + ow],
                    start=False, stop=True), deps=[mm])
            col0 = tc * out_C + os_ * ow
            if evac is None:
                ev = s.op("scalar", lambda e, bank=bank, col0=col0: e.copy(
                    out[:, col0:col0 + ow], bank[:, 0:ow]), deps=[mm])
            else:
                ev = evac(tc, os_, bank, mm, col0)
            pool.release(bidx, ev)
            ev_out = ev
            if tc == ntc - 1:
                wbig_free = mm
                wf_free = [mm, mm]
    return ev_out


def emit_attention(ctx, QT, KT, Vtok, OT, kv_of_head, vcols, mode, esb, ready,
                   tag="a0", mid_cb=None):
    nc, s, pool = ctx.nc, ctx.s, ctx.pool
    onescol = ctx.consts["onescol"]
    onesrow = ctx.consts["onesrow"]
    masks = ctx.consts["masks"]
    ident = ctx.consts["ident"]
    den = [ctx.sbuf(f"{tag}_den{i}", [1, 512], F32) for i in range(2)]
    recb = [ctx.sbuf(f"{tag}_recb{i}", [1, 512], BF16) for i in range(2)]
    bcs = [ctx.sbuf(f"{tag}_bcs{i}", [128, 512], F32) for i in range(2)]
    esb_free = [None] * 32
    den_free = [None, None]
    recb_free = [None, None]
    bcs_free = [None, None]
    scale = np.float32(1.0 / math.sqrt(DH))
    pairs = [(h, Q) for Q in range(4) for h in range(NH)]
    state = {}

    def stage1(i):
        h, Q = pairs[i]
        par = i % 2
        kv = kv_of_head(h)
        jmax = 4 * Q + 4
        eoff = 16 * (i % 2)
        bU_idx, bU, bU_free = pool.acquire()
        bD_idx, bD, bD_free = pool.acquire()
        mmu = mmden = None

        def emit_S(j):
            bS_idx, bS, bS_free = pool.acquire()
            delta = j - 4 * Q
            mm = s.op("tensor", lambda e, h=h, kv=kv, Q=Q, j=j, bS=bS: e.matmul(
                bS[:, :], KT[:, kv * S + j * 128: kv * S + j * 128 + 128],
                QT[:, h * S + Q * 512: h * S + Q * 512 + 512],
                start=True, stop=(delta < 0)), deps=[bS_free] + ready)
            if delta >= 0:
                mm = s.op("tensor", lambda e, bS=bS, delta=delta: e.matmul(
                    bS[:, :], ident[:, :],
                    masks[:, delta * 512:delta * 512 + 512],
                    start=False, stop=True), deps=[mm])
            return (bS_idx, bS, mm)

        spend = [emit_S(jj) for jj in range(min(2, jmax))]
        for j in range(jmax):
            bS_idx, bS, mm = spend.pop(0)
            if j + 2 < jmax:
                spend.append(emit_S(j + 2))
            delta = j - 4 * Q
            je = j + eoff
            ef = esb_free[je] or []
            if mode == "softmax":
                ev_e = s.op("scalar", lambda e, bS=bS, je=je: e.activation(
                    esb[je][:, :], bS[:, :], AF.Exp, scale=scale),
                    deps=[mm], strict=ef)
            else:
                if delta >= 0:
                    ev_e = s.op("scalar", lambda e, bS=bS, je=je: e.activation(
                        esb[je][:, :], bS[:, :], AF.Relu),
                        deps=[mm], strict=ef)
                else:
                    ev_e = s.op("scalar", lambda e, bS=bS, je=je: e.copy(
                        esb[je][:, :], bS[:, :]), deps=[mm], strict=ef)
            pool.release(bS_idx, ev_e)
            first = j == 0
            mmden = s.op("tensor", lambda e, je=je, bD=bD, first=first, jmax=jmax, j=j: e.matmul(
                bD[0:1, :], onescol[:, 0:1], esb[je][:, :],
                start=first, stop=(j == jmax - 1)),
                deps=[ev_e] + ([bD_free] if first else []))
            mmu = s.op("tensor", lambda e, j=j, je=je, kv=kv, bU=bU, first=first, jmax=jmax, vcols=vcols: e.matmul(
                bU[:, :], Vtok[:, j * vcols + kv * 128: j * vcols + kv * 128 + 128],
                esb[je][:, :], start=first, stop=(j == jmax - 1)),
                deps=[ev_e] + ([bU_free] if first else []))
            esb_free[je] = [mmu]
        state[i] = (h, Q, par, bU_idx, bU, mmu, bD_idx, bD, mmden)

    def stage2(i):
        h, Q, par, bU_idx, bU, mmu, bD_idx, bD, mmden = state.pop(i)
        bias = np.float32(EPS_L1) if mode == "linear" else np.float32(0.0)
        e1 = s.op("scalar", lambda e, par=par, bD=bD, bias=bias: e.activation(
            den[par][0:1, :], bD[0:1, :], AF.Ln, bias=bias),
            deps=[mmden], strict=[den_free[par]])
        pool.release(bD_idx, e1)
        e3 = s.op("scalar", lambda e, par=par: e.activation(
            recb[par][0:1, :], den[par][0:1, :], AF.Exp, scale=np.float32(-1.0)),
            deps=[recb_free[par]], strict=[e1])
        den_free[par] = e3
        bB_idx, bB, bB_free = pool.acquire()
        e4 = s.op("tensor", lambda e, par=par, bB=bB: e.matmul(
            bB[:, :], onesrow[0:1, :], recb[par][0:1, :], start=True, stop=True),
            deps=[e3, bB_free])
        recb_free[par] = e4
        e5 = s.op("scalar", lambda e, par=par, bB=bB: e.copy(
            bcs[par][:, :], bB[:, :]), deps=[e4, bcs_free[par]])
        pool.release(bB_idx, e5)
        ev = s.op("vector", lambda e, h=h, Q=Q, par=par, bU=bU: e.scalar_tensor_tensor(
            OT[:, h * S + Q * 512: h * S + Q * 512 + 512], bU[:, :], 1.0,
            bcs[par][:, :], ALU.bypass, ALU.mult), deps=[mmu, e5])
        bcs_free[par] = ev
        pool.release(bU_idx, ev)

    gsz = len(pairs) // 4
    for g in range(4):
        for i in range(g * gsz, (g + 1) * gsz):
            stage1(i)
            if i > g * gsz:
                stage2(i - 1)
        stage2((g + 1) * gsz - 1)
        if mid_cb is not None and g < 3:
            mid_cb(s.now("vector"))
    return s.now("vector")



def build_l0attn():
    nc = bass.Bass("TRN2", target_bir_lowering=False)
    stack = ExitStack()
    xfull = nc.dram_tensor("xfull", [S, D], F32, kind="ExternalInput")
    wq = nc.dram_tensor("wq", [D, 1024], F32, kind="ExternalInput")
    wk = nc.dram_tensor("wk", [D, 1024], F32, kind="ExternalInput")
    wv = nc.dram_tensor("wv", [D, 1024], F32, kind="ExternalInput")
    bqc = nc.dram_tensor("bqc", [128, 8], F32, kind="ExternalInput")
    bkc = nc.dram_tensor("bkc", [128, 8], F32, kind="ExternalInput")
    bvr = nc.dram_tensor("bvr", [1, 1024], F32, kind="ExternalInput")
    oat = nc.dram_tensor("oat", [1024, S], BF16, kind="ExternalOutput")

    with stack:
        s = Sched(nc, stack)
        pool = Pool(nc, stack)
        ctx = Ctx(nc, s, pool, stack)
        stage_consts(ctx, masks=False)
        bq_sb = ctx.sbuf("bq_sb", [128, 8], F32)
        bk_sb = ctx.sbuf("bk_sb", [128, 8], F32)
        bvf = ctx.sbuf("bvf", [1, 1024], F32)
        bvb = ctx.sbuf("bvb", [1, 1024], BF16)
        s.dma("sync", bq_sb[:, :], bqc[:, :], sem="dC")
        s.dma("sync", bk_sb[:, :], bkc[:, :], sem="dC")
        evb = s.dma("sync", bvf[:, :], bvr[:, :], sem="dC")
        s.op("scalar", lambda e: e.copy(bvb[:, :], bvf[:, :]), deps=[evb])
        ctx.consts["ready"] = ("dC", s.cnt["dC"])

        QT = ctx.sbuf("QT", [128, NH * S], BF16)
        KT = ctx.sbuf("KT", [128, NH * S], BF16)
        VT = ctx.sbuf("VT", [128, (S // 128) * 1024], BF16)

        with (nc.sbuf_tensor("lnt1", [128, 16 * S], BF16) as lnt1,):
            with (nc.sbuf_tensor("xt0", [128, D], F32) as xt0,
                  nc.sbuf_tensor("xt1", [128, D], F32) as xt1,
                  nc.sbuf_tensor("lb0", [128, D], BF16) as lb0,
                  nc.sbuf_tensor("lb1", [128, D], BF16) as lb1,
                  nc.sbuf_tensor("lb2", [128, D], BF16) as lb2,
                  nc.sbuf_tensor("lb3", [128, D], BF16) as lb3,
                  nc.sbuf_tensor("stat", [128, 16], F32) as stat,
                  nc.sbuf_tensor("sqb0", [128, D], BF16) as sqb0,
                  nc.sbuf_tensor("sqb1", [128, D], BF16) as sqb1):
                emit_ln_t(ctx, xfull, S // 128, lnt1, S, [xt0, xt1],
                          [lb0, lb1, lb2, lb3], stat, [sqb0, sqb1])
            ln_done = [s.now("tensor"), s.now("vector"), s.now("scalar")]
            with (nc.sbuf_tensor("vwf0", [128, 8 * 512], F32) as vwf0,
                  nc.sbuf_tensor("vwb", [128, 16 * 512], BF16) as vwb):
                ev_v = emit_proj_tm(ctx, lnt1, S, S, wv, 1024, VT, 1024,
                                    [vwf0], vwb, bias_row=bvb,
                                    ready=ln_done, tag="v")
            v_done = [s.now("tensor")]
            with (nc.sbuf_tensor("qwf0", [128, 16 * 128], F32) as qwf0,
                  nc.sbuf_tensor("qwf1", [128, 16 * 128], F32) as qwf1,
                  nc.sbuf_tensor("qwb0", [128, 16 * 128], BF16) as qwb0,
                  nc.sbuf_tensor("qwb1", [128, 16 * 128], BF16) as qwb1):
                emit_proj_fm(ctx, lnt1, S, S, wq, 1024, QT, S,
                             [qwf0, qwf1], [qwb0, qwb1], bias_cols=bq_sb,
                             ready=ln_done + v_done, tag="q")
                emit_proj_fm(ctx, lnt1, S, S, wk, 1024, KT, S,
                             [qwf0, qwf1], [qwb0, qwb1], bias_cols=bk_sb,
                             ready=ln_done + [s.now("tensor")], tag="k")

        md = nc.dram_tensor("c_masks", [128, 4 * 512], BF16, kind="ExternalInput")
        masks_sb = ctx.sbuf("masks_sb", [128, 4 * 512], BF16)
        evm = s.dma("sync", masks_sb[:, :], md[:, :],
                    deps=[s.now("tensor")], sem="dC")
        ctx.consts["masks"] = masks_sb
        OT = ctx.sbuf("OT", [128, NH * S], BF16)
        esb = [ctx.sbuf(f"esb{j}", [128, 512], BF16) for j in range(32)]
        ready = [s.now("scalar"), s.now("vector"), evm]
        ev_a = emit_attention(ctx, QT, KT, VT, OT, lambda h: h, 1024, "softmax",
                              esb, ready)
        ev = s.dma("scalar", oat[:, :].rearrange("(h p) t -> p h t", p=128),
                   OT[:, :].rearrange("p (h t) -> p h t", h=NH),
                   deps=[ev_a], sem="dO")
        s.op("sync", lambda e: None, deps=[ev])
        s.finalize()
    return nc


def build_wo(layer):
    nc = bass.Bass("TRN2", target_bir_lowering=False)
    stack = ExitStack()
    otf = nc.dram_tensor("otf", [D, TOWN], BF16, kind="ExternalInput")
    wo = nc.dram_tensor("wo", [D, D], F32, kind="ExternalInput")
    xres = nc.dram_tensor("xres", [TOWN, D], F32, kind="ExternalInput")
    xout = nc.dram_tensor("xout", [TOWN, D], F32, kind="ExternalOutput")
    use_bias = layer == 0
    if use_bias:
        bor = nc.dram_tensor("bor", [1, D], F32, kind="ExternalInput")

    with stack:
        s = Sched(nc, stack)
        pool = Pool(nc, stack)
        ctx = Ctx(nc, s, pool, stack)
        stage_consts(ctx, masks=False)
        bias_row = None
        if use_bias:
            brf = ctx.sbuf("brf", [1, D], F32)
            brb = ctx.sbuf("brb", [1, D], BF16)
            e1 = s.dma("sync", brf[:, :], bor[:, :], sem="dC")
            s.op("scalar", lambda e: e.copy(brb[:, :], brf[:, :]), deps=[e1])
            bias_row = brb
        ctx.consts["ready"] = ("dC", s.cnt["dC"])
        bias_ev = s.now("scalar")
        onesrow = ctx.consts["onesrow"]

        otf_sb = ctx.sbuf("otf_sb", [128, 16 * TOWN], BF16)
        ev_o = s.dma("sync", otf_sb[:, :].rearrange("p (f t) -> p f t", f=16),
                     otf[:, :].rearrange("(f p) t -> p f t", p=128), sem="dX")
        wob = ctx.sbuf("wob", [128, 16 * D], BF16)
        wf0 = ctx.sbuf("wf0", [128, D], F32)
        wf1 = ctx.sbuf("wf1", [128, D], F32)
        wfs = [wf0, wf1]
        wfree = [None, None]
        ev_conv = []
        for k in range(16):
            wi = k % 2
            evd = s.dma("sync", wfs[wi][:, :], wo[k * 128:(k + 1) * 128, :],
                        deps=[wfree[wi]], sem="dW")
            if wi:
                evc = s.op("vector", lambda e, k=k, wi=wi: e.tensor_copy(
                    wob[:, k * D:(k + 1) * D], wfs[wi][:, :]), deps=[evd])
            else:
                evc = s.op("scalar", lambda e, k=k, wi=wi: e.copy(
                    wob[:, k * D:(k + 1) * D], wfs[wi][:, :]), deps=[evd])
            wfree[wi] = evc
            ev_conv.append(evc)
        w_ready = [("dX", s.cnt["dX"])] + ev_conv[-2:]

        xr0 = ctx.sbuf("xr0", [128, D], F32)
        xr1 = ctx.sbuf("xr1", [128, D], F32)
        xrs = [xr0, xr1]
        xo0 = ctx.sbuf("xo0", [128, D], F32)
        xo1 = ctx.sbuf("xo1", [128, D], F32)
        xos = [xo0, xo1]
        xr_free = [None, None]
        xo_free = [None, None]
        onesrow = ctx.consts["onesrow"]
        for tc in range(TOWN // 128):
            xi = tc % 2
            evx = s.dma("scalar", xrs[xi][:, :], xres[tc * 128:(tc + 1) * 128, :],
                        deps=[xr_free[xi]], sem="dX")
            evs_os = []
            for os_ in range(4):
                bidx, bank, bfree = pool.acquire()
                mm = None
                for k in range(16):
                    mm = s.op("tensor", lambda e, k=k, tc=tc, os_=os_, bank=bank: e.matmul(
                        bank[:, :], otf_sb[:, k * TOWN + tc * 128: k * TOWN + tc * 128 + 128],
                        wob[:, k * D + os_ * 512: k * D + os_ * 512 + 512],
                        start=(k == 0), stop=(k == 15 and bias_row is None)),
                        deps=(w_ready + [bfree]) if k == 0 else [])
                if bias_row is not None:
                    mm = s.op("tensor", lambda e, bank=bank, os_=os_: e.matmul(
                        bank[:, :], onesrow[0:1, :], bias_row[0:1, os_ * 512:os_ * 512 + 512],
                        start=False, stop=True), deps=[mm, bias_ev])
                ev = s.op("vector", lambda e, bank=bank, os_=os_, xi=xi: e.tensor_add(
                    xos[xi][:, os_ * 512:os_ * 512 + 512], bank[:, :],
                    xrs[xi][:, os_ * 512:os_ * 512 + 512]),
                    deps=[mm, evx, xo_free[xi]])
                pool.release(bidx, ev)
                evs_os.append(ev)
            evd = s.dma("scalar", xout[tc * 128:(tc + 1) * 128, :], xos[xi][:, :],
                        deps=evs_os, sem="dO")
            xo_free[xi] = evd
            xr_free[xi] = evd
        s.op("sync", lambda e: None, deps=[("dO", s.cnt["dO"])])
        s.finalize()
    return nc


def build_l0ffn():
    nc = bass.Bass("TRN2", target_bir_lowering=False)
    stack = ExitStack()
    x1d = nc.dram_tensor("x1d", [TOWN, D], F32, kind="ExternalInput")
    gwt = nc.dram_tensor("gwt", [D, INTER], F32, kind="ExternalInput")
    uwt = nc.dram_tensor("uwt", [D, INTER], F32, kind="ExternalInput")
    dwt = nc.dram_tensor("dwt", [INTER, D], F32, kind="ExternalInput")
    gsc = nc.dram_tensor("gsc", [128, NI], F32, kind="ExternalInput")
    usc = nc.dram_tensor("usc", [128, NI], F32, kind="ExternalInput")
    dsb = nc.dram_tensor("dsb", [128, D], F32, kind="ExternalInput")
    xout = nc.dram_tensor("xout", [TOWN, D], F32, kind="ExternalOutput")

    with stack:
        s = Sched(nc, stack)
        pool = Pool(nc, stack)
        ctx = Ctx(nc, s, pool, stack)
        stage_consts(ctx, masks=False)
        gs_sb = ctx.sbuf("gs_sb", [128, NI], F32)
        us_sb = ctx.sbuf("us_sb", [128, NI], F32)
        ds_sb = ctx.sbuf("ds_sb", [128, D], F32)
        s.dma("sync", gs_sb[:, :], gsc[:, :], sem="dC")
        s.dma("sync", us_sb[:, :], usc[:, :], sem="dC")
        s.dma("sync", ds_sb[:, :], dsb[:, :], sem="dC")
        ctx.consts["ready"] = ("dC", s.cnt["dC"])
        c_ready = ctx.consts["ready"]

        hdn = ctx.sbuf("hdn", [128, NI * TOWN], BF16)
        nk = 16
        with (nc.sbuf_tensor("lnt2", [128, 16 * TOWN], BF16) as lnt2,):
            with (nc.sbuf_tensor("xt0", [128, D], F32) as xt0,
                  nc.sbuf_tensor("xt1", [128, D], F32) as xt1,
                  nc.sbuf_tensor("lb0", [128, D], BF16) as lb0,
                  nc.sbuf_tensor("lb1", [128, D], BF16) as lb1,
                  nc.sbuf_tensor("lb2", [128, D], BF16) as lb2,
                  nc.sbuf_tensor("lb3", [128, D], BF16) as lb3,
                  nc.sbuf_tensor("stat", [128, 16], F32) as stat,
                  nc.sbuf_tensor("sqb0", [128, D], BF16) as sqb0,
                  nc.sbuf_tensor("sqb1", [128, D], BF16) as sqb1):
                emit_ln_t(ctx, x1d, TOWN // 128, lnt2, TOWN, [xt0, xt1],
                          [lb0, lb1, lb2, lb3], stat, [sqb0, sqb1])
            ln_done = [s.now("tensor"), s.now("vector"), s.now("scalar")]
            with (nc.sbuf_tensor("wg_f", [128, nk * 128], F32) as wg_f,
                  nc.sbuf_tensor("wg_b", [128, nk * 128], BF16) as wg_b,
                  nc.sbuf_tensor("wu_f", [128, nk * 128], F32) as wu_f,
                  nc.sbuf_tensor("wu_b", [128, nk * 128], BF16) as wu_b,
                  nc.sbuf_tensor("sg", [128, TOWN], F32) as sg):
                wg_free = wu_free = None
                sg_free = None
                ev_sg = None
                for ic in range(NI):
                    ev_dg = gather_w(ctx, gwt, ic * 128, 128, wg_f,
                                     [wg_free] + (ln_done if ic == 0 else []))
                    ev_wg = conv_w(ctx, wg_f, wg_b, nk * 128, "scalar", [ev_dg])
                    ev_du = gather_w(ctx, uwt, ic * 128, 128, wu_f, [wu_free])
                    ev_wu = conv_w(ctx, wu_f, wu_b, nk * 128, "vector", [ev_du])
                    sg_evs = []
                    for ts in range(TOWN // 512):
                        bidx, bank, bfree = pool.acquire()
                        mm = None
                        for k in range(nk):
                            mm = s.op("tensor", lambda e, k=k, ts=ts, bank=bank: e.matmul(
                                bank[:, :], wg_b[:, k * 128:(k + 1) * 128],
                                lnt2[:, k * TOWN + ts * 512: k * TOWN + ts * 512 + 512],
                                start=(k == 0), stop=(k == nk - 1)),
                                deps=([ev_wg, bfree] + ln_done) if k == 0 else [])
                        ev_sg = s.op("scalar", lambda e, ic=ic, ts=ts, bank=bank: e.activation(
                            sg[:, ts * 512:ts * 512 + 512], bank[:, :], AF.Silu,
                            scale=gs_sb[:, ic:ic + 1]), deps=[mm, sg_free, c_ready])
                        pool.release(bidx, ev_sg)
                        sg_evs.append(ev_sg)
                        wg_free = mm
                    hv = []
                    for ts in range(TOWN // 512):
                        bidx, bank, bfree = pool.acquire()
                        mm = None
                        for k in range(nk):
                            mm = s.op("tensor", lambda e, k=k, ts=ts, bank=bank: e.matmul(
                                bank[:, :], wu_b[:, k * 128:(k + 1) * 128],
                                lnt2[:, k * TOWN + ts * 512: k * TOWN + ts * 512 + 512],
                                start=(k == 0), stop=(k == nk - 1)),
                                deps=[ev_wu, bfree] if k == 0 else [])
                        col0 = ic * TOWN + ts * 512
                        ev_h = s.op("vector", lambda e, ic=ic, ts=ts, bank=bank, col0=col0: e.scalar_tensor_tensor(
                            hdn[:, col0:col0 + 512], bank[:, :], us_sb[:, ic:ic + 1],
                            sg[:, ts * 512:ts * 512 + 512], ALU.mult, ALU.mult),
                            deps=[mm, sg_evs[ts], c_ready])
                        pool.release(bidx, ev_h)
                        hv.append(ev_h)
                        wu_free = mm
                    sg_free = hv[-1]
        hdn_ready = [s.now("vector")]

        with (nc.sbuf_tensor("dw_f0", [128, 8 * 512], F32) as dw_f0,
              nc.sbuf_tensor("dw_f1", [128, 8 * 512], F32) as dw_f1,
              nc.sbuf_tensor("dbig", [128, NI * 512], BF16) as dbig,
              nc.sbuf_tensor("xr0", [128, 512], F32) as xr0,
              nc.sbuf_tensor("xr1", [128, 512], F32) as xr1,
              nc.sbuf_tensor("xo0", [128, 512], F32) as xo0,
              nc.sbuf_tensor("xo1", [128, 512], F32) as xo1):
            dview = dwt[:, :].rearrange("(k p) t -> p k t", p=128)
            dwf = [dw_f0, dw_f1]
            xrs, xos = [xr0, xr1], [xo0, xo1]
            gu_done = s.now("tensor")
            dwf_free = [gu_done, gu_done]
            dbig_free = gu_done
            xr_free = [gu_done, gu_done]
            xo_free = [gu_done, gu_done]
            for os_ in range(4):
                convs = []
                for bi, kb0 in enumerate(range(0, NI, 8)):
                    nkb = min(8, NI - kb0)
                    wi = bi % 2
                    evd = s.dma("sync", dwf[wi][:, 0:nkb * 512].rearrange("p (k c) -> p k c", k=nkb),
                                dview[:, kb0:kb0 + nkb, os_ * 512:os_ * 512 + 512],
                                deps=[dwf_free[wi]], sem="dW")
                    evc = s.op("vector" if wi else "scalar",
                               (lambda e, wi=wi, kb0=kb0, nkb=nkb: e.tensor_copy(
                                   dbig[:, kb0 * 512:(kb0 + nkb) * 512],
                                   dwf[wi][:, 0:nkb * 512])) if wi else
                               (lambda e, wi=wi, kb0=kb0, nkb=nkb: e.copy(
                                   dbig[:, kb0 * 512:(kb0 + nkb) * 512],
                                   dwf[wi][:, 0:nkb * 512])),
                               deps=[evd, dbig_free])
                    convs.append(evc)
                    dwf_free[wi] = evc
                for tc in range(TOWN // 128):
                    xi = tc % 2
                    evx = s.dma("scalar", xrs[xi][:, :],
                                x1d[tc * 128:(tc + 1) * 128, os_ * 512:os_ * 512 + 512],
                                deps=[xr_free[xi]], sem="dX")
                    bidx, bank, bfree = pool.acquire()
                    mm = None
                    for k in range(NI):
                        mm = s.op("tensor", lambda e, k=k, tc=tc, bank=bank: e.matmul(
                            bank[:, :], hdn[:, k * TOWN + tc * 128: k * TOWN + tc * 128 + 128],
                            dbig[:, k * 512:(k + 1) * 512],
                            start=(k == 0), stop=(k == NI - 1)),
                            deps=(convs + hdn_ready + [bfree]) if k == 0 else [])
                    if tc == TOWN // 128 - 1:
                        dbig_free = mm
                    e1 = s.op("vector", lambda e, bank=bank, os_=os_, xi=xi: e.scalar_tensor_tensor(
                        xos[xi][:, :], bank[:, :], 1.0,
                        ds_sb[:, os_ * 512:os_ * 512 + 512], ALU.bypass, ALU.mult),
                        deps=[mm, xo_free[xi], c_ready])
                    e2 = s.op("vector", lambda e, xi=xi: e.tensor_add(
                        xos[xi][:, :], xos[xi][:, :], xrs[xi][:, :]),
                        deps=[evx], strict=[e1])
                    pool.release(bidx, e2)
                    evd2 = s.dma("scalar", xout[tc * 128:(tc + 1) * 128, os_ * 512:os_ * 512 + 512],
                                 xos[xi][:, :], deps=[e2], sem="dO")
                    xo_free[xi] = evd2
                    xr_free[xi] = evd2
        s.op("sync", lambda e: None, deps=[("dO", s.cnt["dO"])])
        s.finalize()
    return nc


def build_l1attn():
    nc = bass.Bass("TRN2", target_bir_lowering=False)
    stack = ExitStack()
    x2f = nc.dram_tensor("x2f", [S, D], F32, kind="ExternalInput")
    wq1 = nc.dram_tensor("wq1", [D, 1024], F32, kind="ExternalInput")
    wk1 = nc.dram_tensor("wk1", [D, 256], F32, kind="ExternalInput")
    wv1 = nc.dram_tensor("wv1", [D, 256], F32, kind="ExternalInput")
    o1t = nc.dram_tensor("o1t", [1024, S], BF16, kind="ExternalOutput")

    with stack:
        s = Sched(nc, stack)
        pool = Pool(nc, stack)
        ctx = Ctx(nc, s, pool, stack)
        stage_consts(ctx, rope=True, masks=False)

        QT = ctx.sbuf("QT1", [128, NH * S], BF16)
        KT = ctx.sbuf("KT1", [128, 2 * S], BF16)
        VT = ctx.sbuf("VT1", [128, (S // 128) * 256], BF16)
        with (nc.sbuf_tensor("lnt3", [128, 16 * S], BF16) as lnt3,):
            with (nc.sbuf_tensor("xt0", [128, D], F32) as xt0,
                  nc.sbuf_tensor("xt1", [128, D], F32) as xt1,
                  nc.sbuf_tensor("lb0", [128, D], BF16) as lb0,
                  nc.sbuf_tensor("lb1", [128, D], BF16) as lb1,
                  nc.sbuf_tensor("lb2", [128, D], BF16) as lb2,
                  nc.sbuf_tensor("lb3", [128, D], BF16) as lb3,
                  nc.sbuf_tensor("stat", [128, 16], F32) as stat,
                  nc.sbuf_tensor("sqb0", [128, D], BF16) as sqb0,
                  nc.sbuf_tensor("sqb1", [128, D], BF16) as sqb1):
                emit_ln_t(ctx, x2f, S // 128, lnt3, S, [xt0, xt1],
                          [lb0, lb1, lb2, lb3], stat, [sqb0, sqb1])
            ln_done = [s.now("tensor"), s.now("vector"), s.now("scalar")]
            with (nc.sbuf_tensor("vwf0", [128, 8 * 256], F32) as vwf0,
                  nc.sbuf_tensor("vwb", [128, 16 * 256], BF16) as vwb):
                emit_proj_tm(ctx, lnt3, S, S, wv1, 256, VT, 256,
                             [vwf0], vwb, ready=ln_done, tag="v1")
            with (nc.sbuf_tensor("qwf0", [128, 16 * 128], F32) as qwf0,
                  nc.sbuf_tensor("qwf1", [128, 16 * 128], F32) as qwf1,
                  nc.sbuf_tensor("qwb0", [128, 16 * 128], BF16) as qwb0,
                  nc.sbuf_tensor("qwb1", [128, 16 * 128], BF16) as qwb1,
                  nc.sbuf_tensor("rp0", [128, 512], F32) as rp0,
                  nc.sbuf_tensor("rp1", [128, 512], BF16) as rp1,
                  nc.sbuf_tensor("rp2", [128, 512], F32) as rp2,
                  nc.sbuf_tensor("rp3", [128, 512], F32) as rp3):
                rope = {"bufs": (rp0, rp1, rp2, rp3)}
                v_done = [s.now("tensor")]
                emit_proj_fm(ctx, lnt3, S, S, wq1, 1024, QT, S,
                             [qwf0, qwf1], [qwb0, qwb1], rope=rope,
                             ready=ln_done + v_done, tag="q1")
                emit_proj_fm(ctx, lnt3, S, S, wk1, 256, KT, S,
                             [qwf0, qwf1], [qwb0, qwb1], rope=rope,
                             ready=ln_done + [s.now("tensor")], tag="k1")

        md = nc.dram_tensor("c_masks", [128, 4 * 512], BF16, kind="ExternalInput")
        masks_sb = ctx.sbuf("masks_sb", [128, 4 * 512], BF16)
        evm = s.dma("sync", masks_sb[:, :], md[:, :],
                    deps=[s.now("tensor")], sem="dC")
        ctx.consts["masks"] = masks_sb
        OT = ctx.sbuf("OT1", [128, NH * S], BF16)
        esb = [ctx.sbuf(f"esb{j}", [128, 512], BF16) for j in range(32)]
        ready = [s.now("scalar"), s.now("vector"), evm]
        ev_a = emit_attention(ctx, QT, KT, VT, OT, lambda h: h // 4, 256,
                              "linear", esb, ready, tag="a1")
        ev = s.dma("scalar", o1t[:, :].rearrange("(h p) t -> p h t", p=128),
                   OT[:, :].rearrange("p (h t) -> p h t", h=NH),
                   deps=[ev_a], sem="dO")
        s.op("sync", lambda e: None, deps=[ev])
        s.finalize()
    return nc


def build_l1ffn():
    nc = bass.Bass("TRN2", target_bir_lowering=False)
    stack = ExitStack()
    x3d = nc.dram_tensor("x3d", [TOWN, D], F32, kind="ExternalInput")
    w1t = nc.dram_tensor("w1t", [D, D], F32, kind="ExternalInput")
    b1b = nc.dram_tensor("b1b", [128, D], F32, kind="ExternalInput")
    xout = nc.dram_tensor("xout", [TOWN, D], F32, kind="ExternalOutput")

    with stack:
        s = Sched(nc, stack)
        pool = Pool(nc, stack)
        ctx = Ctx(nc, s, pool, stack)
        stage_consts(ctx, masks=False)
        onescol = ctx.consts["onescol"]
        onesrow = ctx.consts["onesrow"]
        b1_sb = ctx.sbuf("b1_sb", [128, D], F32)
        ev_b1 = s.dma("sync", b1_sb[:, :], b1b[:, :], sem="dC")
        ctx.consts["ready"] = ("dC", s.cnt["dC"])
        cready = ctx.consts["ready"]

        signT = ctx.sbuf("signT", [128, 16 * D], BF16)
        alphab = ctx.sbuf("alphab", [128, D], F32)
        lnt4 = ctx.sbuf("lnt4", [128, 16 * TOWN], BF16)

        with (nc.sbuf_tensor("xt0", [128, D], F32) as xt0,
              nc.sbuf_tensor("xt1", [128, D], F32) as xt1,
              nc.sbuf_tensor("lb0", [128, D], BF16) as lb0,
              nc.sbuf_tensor("lb1", [128, D], BF16) as lb1,
              nc.sbuf_tensor("lb2", [128, D], BF16) as lb2,
              nc.sbuf_tensor("lb3", [128, D], BF16) as lb3,
              nc.sbuf_tensor("stat", [128, 16], F32) as stat,
                  nc.sbuf_tensor("sqb0", [128, D], BF16) as sqb0,
                  nc.sbuf_tensor("sqb1", [128, D], BF16) as sqb1):
            emit_ln_t(ctx, x3d, TOWN // 128, lnt4, TOWN, [xt0, xt1],
                      [lb0, lb1, lb2, lb3], stat, [sqb0, sqb1])

        with (nc.sbuf_tensor("wsl", [128, D], F32) as wsl,
              nc.sbuf_tensor("absl", [128, D], BF16) as absl,
              nc.sbuf_tensor("alr_f", [1, D], F32) as alr_f,
              nc.sbuf_tensor("alr_b", [1, D], BF16) as alr_b):
            ln_guard = s.op("sync", lambda e: None, deps=[
                s.now("tensor"), s.now("vector"), s.now("scalar")])
            wsl_free = ln_guard
            absl_free = None
            apb = [pool.acquire() for _ in range(4)]
            mm_a = [None] * 4
            for k in range(16):
                evw = s.dma("sync", wsl[:, :], w1t[k * 128:(k + 1) * 128, :],
                            deps=[wsl_free], sem="dW")
                evs = s.op("scalar", lambda e, k=k: e.activation(
                    signT[:, k * D:(k + 1) * D], wsl[:, :], AF.Sign), deps=[evw])
                eva0 = s.op("scalar", lambda e: e.activation(
                    absl[:, 0:1024], wsl[:, 0:1024], AF.Abs), deps=[evw, absl_free])
                eva = s.op("scalar", lambda e: e.activation(
                    absl[:, 1024:2048], wsl[:, 1024:2048], AF.Abs), deps=[evw])
                wsl_free = eva
                for q4 in range(4):
                    bidx, bank, bfree = apb[q4]
                    deps = [eva, cready]
                    if k == 0:
                        deps.append(bfree)
                    mm_a[q4] = s.op("tensor", lambda e, q4=q4, bank=bank, k=k: e.matmul(
                        bank[0:1, :], onescol[:, 0:1],
                        absl[:, q4 * 512:(q4 + 1) * 512],
                        start=(k == 0), stop=(k == 15)), deps=deps)
                absl_free = mm_a[3]
            for q4 in range(4):
                bidx, bank, bfree = apb[q4]
                e1 = s.op("vector", lambda e, q4=q4, bank=bank: e.tensor_scalar_mul(
                    alr_f[0:1, q4 * 512:(q4 + 1) * 512], bank[0:1, :], 1.0 / D),
                    deps=[mm_a[q4]])
                pool.release(bidx, e1)
            e2 = s.op("scalar", lambda e: e.copy(alr_b[0:1, :], alr_f[0:1, :]),
                      deps=[s.now("vector")])
            for q4 in range(4):
                bidx, bank, bfree = pool.acquire()
                e3 = s.op("tensor", lambda e, q4=q4, bank=bank: e.matmul(
                    bank[:, :], onesrow[0:1, :], alr_b[0:1, q4 * 512:(q4 + 1) * 512],
                    start=True, stop=True), deps=[e2, bfree, cready])
                e4 = s.op("scalar", lambda e, q4=q4, bank=bank: e.copy(
                    alphab[:, q4 * 512:(q4 + 1) * 512], bank[:, :]), deps=[e3])
                pool.release(bidx, e4)
        alpha_ready = s.now("scalar")
        sign_ready = s.now("scalar")
        ready_mm = [sign_ready, s.now("vector")]

        with (nc.sbuf_tensor("xo0", [128, 512], F32) as xo0,
              nc.sbuf_tensor("xo1", [128, 512], F32) as xo1,
              nc.sbuf_tensor("xr0", [128, 512], F32) as xr0,
              nc.sbuf_tensor("xr1", [128, 512], F32) as xr1):
            xos = [xo0, xo1]
            xrs = [xr0, xr1]
            xo_free = [None, None]
            xr_free = [ready_mm[1], ready_mm[1]]
            nn = 0
            for os_ in range(4):
                for tc in range(TOWN // 128):
                    bidx, bank, bfree = pool.acquire()
                    mm = None
                    for k in range(16):
                        mm = s.op("tensor", lambda e, k=k, tc=tc, os_=os_, bank=bank: e.matmul(
                            bank[:, :], lnt4[:, k * TOWN + tc * 128: k * TOWN + tc * 128 + 128],
                            signT[:, k * D + os_ * 512: k * D + os_ * 512 + 512],
                            start=(k == 0), stop=(k == 15)),
                            deps=(ready_mm + [bfree]) if k == 0 else [])
                    xi = nn % 2
                    nn += 1
                    evx = s.dma("sync", xrs[xi][:, :],
                                x3d[tc * 128:(tc + 1) * 128, os_ * 512:os_ * 512 + 512],
                                deps=[xr_free[xi]], sem="dX")
                    e1 = s.op("vector", lambda e, bank=bank, os_=os_, xi=xi: e.scalar_tensor_tensor(
                        xos[xi][:, :], bank[:, :], 1.0,
                        alphab[:, os_ * 512:os_ * 512 + 512], ALU.bypass, ALU.mult),
                        deps=[mm, xo_free[xi], alpha_ready])
                    e2 = s.op("vector", lambda e, xi=xi: e.tensor_add(
                        xos[xi][:, :], xos[xi][:, :], xrs[xi][:, :]),
                        deps=[evx], strict=[e1])
                    e3 = s.op("vector", lambda e, os_=os_, xi=xi: e.tensor_add(
                        xos[xi][:, :], xos[xi][:, :],
                        b1_sb[:, os_ * 512:os_ * 512 + 512]),
                        deps=[e2, ev_b1])
                    pool.release(bidx, e3)
                    evd = s.dma("scalar", xout[tc * 128:(tc + 1) * 128, os_ * 512:os_ * 512 + 512],
                                xos[xi][:, :], deps=[e3], sem="dO")
                    xo_free[xi] = evd
                    xr_free[xi] = e2
        s.op("sync", lambda e: None, deps=[("dO", s.cnt["dO"])])
        s.finalize()
    return nc



def _consts(rope=False):
    ident = np.eye(128, dtype=bfloat16)
    masks = np.zeros((128, 4 * 512), dtype=np.float32)
    kk = np.arange(128)[:, None]
    qq = np.arange(512)[None, :]
    for d in range(4):
        masks[:, d * 512:(d + 1) * 512] = (kk <= qq - 128 * d).astype(np.float32)
    maskneg = np.where(masks > 0, np.float32(0.0), np.float32(-1e30))
    out = dict(
        c_ident=ident,
        c_masks=masks.astype(bfloat16),
        c_maskneg=maskneg.astype(bfloat16),
        c_onescol=np.ones((128, 1), dtype=bfloat16),
        c_onesrow=np.ones((1, 128), dtype=bfloat16),
    )
    if rope:
        inv = 1.0 / (10000.0 ** (np.arange(0, DH, 2, dtype=np.float64) / DH))
        t = np.arange(S, dtype=np.float64)
        fr = np.outer(t, inv)
        out["c_prot"] = np.ascontiguousarray(np.roll(np.eye(128), 64, axis=0).astype(bfloat16))
        out["c_cos"] = np.ascontiguousarray(
            np.concatenate([np.cos(fr), np.cos(fr)], 1).T.astype(np.float32))
        out["c_srot"] = np.ascontiguousarray(
            np.concatenate([np.sin(fr), -np.sin(fr)], 1).T.astype(np.float32))
    return out


def _run(nc, in_maps):
    res = run_bass_kernel_spmd(nc, in_maps, core_ids=list(range(NCORES)))
    return res.results


def _pc(fn):
    return [fn(c // 2, c % 2) for c in range(NCORES)]


def kernel_unfused(**inputs):
    ii = {k: np.asarray(v) for k, v in inputs.items()}
    X = np.ascontiguousarray(ii["L0_attn_X"])
    cA = _consts()
    cR = _consts(rope=True)
    ca_l = {k: v for k, v in cA.items() if k != "c_masks"}

    nc1 = build_l0attn()
    maps = _pc(lambda b, r: dict(
        xfull=X[b],
        wq=np.ascontiguousarray(ii["L0_attn_Wq"][:, r * 1024:(r + 1) * 1024]),
        wk=np.ascontiguousarray(ii["L0_attn_Wk"][:, r * 1024:(r + 1) * 1024]),
        wv=np.ascontiguousarray(ii["L0_attn_Wv"][:, r * 1024:(r + 1) * 1024]),
        bqc=np.ascontiguousarray(ii["L0_attn_bq"][r * 1024:(r + 1) * 1024].reshape(8, 128).T),
        bkc=np.ascontiguousarray(ii["L0_attn_bk"][r * 1024:(r + 1) * 1024].reshape(8, 128).T),
        bvr=np.ascontiguousarray(ii["L0_attn_bv"][r * 1024:(r + 1) * 1024].reshape(1, 1024)),
        **cA))
    r1 = _run(nc1, maps)

    nc2 = build_wo(0)
    maps = _pc(lambda b, r: dict(
        otf=np.ascontiguousarray(np.concatenate(
            [r1[2 * b]["oat"], r1[2 * b + 1]["oat"]], 0)[:, r * 1024:(r + 1) * 1024]),
        wo=ii["L0_attn_Wo"],
        xres=np.ascontiguousarray(X[b, r * 1024:(r + 1) * 1024]),
        bor=np.ascontiguousarray(ii["L0_attn_bo"].reshape(1, D)),
        **ca_l))
    r2 = _run(nc2, maps)

    nc3 = build_l0ffn()
    gwt = np.ascontiguousarray(ii["L0_ffn_gate_w"].T)
    uwt = np.ascontiguousarray(ii["L0_ffn_up_w"].T)
    dwt = np.ascontiguousarray(ii["L0_ffn_down_w"].T)
    gsc = np.ascontiguousarray(ii["L0_ffn_gate_s"].reshape(NI, 128).T)
    usc = np.ascontiguousarray(ii["L0_ffn_up_s"].reshape(NI, 128).T)
    dsb = np.ascontiguousarray(np.broadcast_to(ii["L0_ffn_down_s"][None, :], (128, D)))
    maps = _pc(lambda b, r: dict(
        x1d=r2[2 * b + r]["xout"], gwt=gwt, uwt=uwt, dwt=dwt,
        gsc=gsc, usc=usc, dsb=dsb, **ca_l))
    r3 = _run(nc3, maps)

    nc4 = build_l1attn()
    wq1 = np.ascontiguousarray(ii["L1_attn_Wq"].T)
    wk1 = np.ascontiguousarray(ii["L1_attn_Wk"].T)
    wv1 = np.ascontiguousarray(ii["L1_attn_Wv"].T)
    maps = _pc(lambda b, r: dict(
        x2f=np.ascontiguousarray(np.concatenate(
            [r3[2 * b]["xout"], r3[2 * b + 1]["xout"]], 0)),
        wq1=np.ascontiguousarray(wq1[:, r * 1024:(r + 1) * 1024]),
        wk1=np.ascontiguousarray(wk1[:, r * 256:(r + 1) * 256]),
        wv1=np.ascontiguousarray(wv1[:, r * 256:(r + 1) * 256]),
        **cR))
    r4 = _run(nc4, maps)

    nc5 = build_wo(1)
    wo1t = np.ascontiguousarray(ii["L1_attn_Wo"].T)
    maps = _pc(lambda b, r: dict(
        otf=np.ascontiguousarray(np.concatenate(
            [r4[2 * b]["o1t"], r4[2 * b + 1]["o1t"]], 0)[:, r * 1024:(r + 1) * 1024]),
        wo=wo1t,
        xres=r3[2 * b + r]["xout"],
        **ca_l))
    r5 = _run(nc5, maps)

    nc6 = build_l1ffn()
    w1t = np.ascontiguousarray(ii["L1_ffn_weight"].T)
    b1b = np.ascontiguousarray(np.broadcast_to(ii["L1_ffn_bias"][None, :], (128, D)))
    maps = _pc(lambda b, r: dict(
        x3d=r5[2 * b + r]["xout"], w1t=w1t, b1b=b1b, **ca_l))
    r6 = _run(nc6, maps)

    out = np.empty((B, S, D), dtype=np.float32)
    for b in range(B):
        out[b, 0:1024] = r6[2 * b]["xout"]
        out[b, 1024:2048] = r6[2 * b + 1]["xout"]
    return out



def build_fused(dbg=False):
    nc = bass.Bass("TRN2", target_bir_lowering=False)
    stack = ExitStack()
    xfull = nc.dram_tensor("xfull", [S, D], F32, kind="ExternalInput")
    xres = nc.dram_tensor("xres", [TOWN, D], F32, kind="ExternalInput")
    toff = nc.dram_tensor("toff", [1, 1], mybir.dt.int32, kind="ExternalInput")
    wq = nc.dram_tensor("wq", [D, 1024], F32, kind="ExternalInput")
    wk = nc.dram_tensor("wk", [D, 1024], F32, kind="ExternalInput")
    wv = nc.dram_tensor("wv", [D, 1024], F32, kind="ExternalInput")
    bqc = nc.dram_tensor("bqc", [128, 8], F32, kind="ExternalInput")
    bkc = nc.dram_tensor("bkc", [128, 8], F32, kind="ExternalInput")
    bvr = nc.dram_tensor("bvr", [1, 1024], F32, kind="ExternalInput")
    wo = nc.dram_tensor("wo", [D, D], F32, kind="ExternalInput")
    bor = nc.dram_tensor("bor", [1, D], F32, kind="ExternalInput")
    gwt = nc.dram_tensor("gwt", [D, INTER], F32, kind="ExternalInput")
    uwt = nc.dram_tensor("uwt", [D, INTER], F32, kind="ExternalInput")
    dwt = nc.dram_tensor("dwt", [INTER, D], F32, kind="ExternalInput")
    gsc = nc.dram_tensor("gsc", [128, NI], F32, kind="ExternalInput")
    usc = nc.dram_tensor("usc", [128, NI], F32, kind="ExternalInput")
    dsb = nc.dram_tensor("dsb", [128, D], F32, kind="ExternalInput")
    wq1 = nc.dram_tensor("wq1", [D, 1024], F32, kind="ExternalInput")
    wk1 = nc.dram_tensor("wk1", [D, 256], F32, kind="ExternalInput")
    wv1 = nc.dram_tensor("wv1", [D, 256], F32, kind="ExternalInput")
    wo1 = nc.dram_tensor("wo1", [D, D], F32, kind="ExternalInput")
    w1t = nc.dram_tensor("w1t", [D, D], F32, kind="ExternalInput")
    b1b = nc.dram_tensor("b1b", [128, D], F32, kind="ExternalInput")
    c_ident = nc.dram_tensor("c_ident", [128, 128], BF16, kind="ExternalInput")
    c_onescol = nc.dram_tensor("c_onescol", [128, 1], BF16, kind="ExternalInput")
    c_onesrow = nc.dram_tensor("c_onesrow", [1, 128], BF16, kind="ExternalInput")
    c_masks = nc.dram_tensor("c_masks", [128, 4 * 512], BF16, kind="ExternalInput")
    c_maskneg = nc.dram_tensor("c_maskneg", [128, 4 * 512], BF16, kind="ExternalInput")
    c_cos = nc.dram_tensor("c_cos", [128, S], F32, kind="ExternalInput")
    c_srot = nc.dram_tensor("c_srot", [128, S], F32, kind="ExternalInput")
    c_prot = nc.dram_tensor("c_prot", [128, 128], BF16, kind="ExternalInput")
    xout = nc.dram_tensor("xout", [TOWN, D], F32, kind="ExternalOutput")
    KD = "ExternalOutput" if dbg else "Internal"
    x1_d = nc.dram_tensor("x1_d", [TOWN, D], F32, kind=KD)
    x2_d = nc.dram_tensor("x2_d", [TOWN, D], F32, kind=KD)
    x3_d = nc.dram_tensor("x3_d", [TOWN, D], F32, kind=KD)
    e1_in = nc.dram_tensor("e1_in", [4, 1024, 512], BF16)
    e1_out = nc.dram_tensor("e1_out", [4, 2048, 512], BF16)
    e2_in = nc.dram_tensor("e2_in", [4, D, 256], BF16)
    e2_out = nc.dram_tensor("e2_out", [4, 2 * D, 256], BF16)
    e3_in = nc.dram_tensor("e3_in", [4, 1024, 512], BF16)
    e3_out = nc.dram_tensor("e3_out", [4, 2048, 512], BF16)
    if dbg:
        dbg_e1i = nc.dram_tensor("dbg_e1i", [1024, S], BF16, kind="ExternalOutput")
        dbg_e1o = nc.dram_tensor("dbg_e1o", [2048, S], BF16, kind="ExternalOutput")
        dbg_e2i = nc.dram_tensor("dbg_e2i", [D, 1024], BF16, kind="ExternalOutput")
        dbg_e3i = nc.dram_tensor("dbg_e3i", [1024, S], BF16, kind="ExternalOutput")
        dbg_lnt2 = nc.dram_tensor("dbg_lnt2", [128, 16 * TOWN], BF16, kind="ExternalOutput")
        dbg_hdn = nc.dram_tensor("dbg_hdn", [128, 4 * TOWN], BF16, kind="ExternalOutput")
        dbg_lnt4 = nc.dram_tensor("dbg_lnt4", [128, 16 * TOWN], BF16, kind="ExternalOutput")
        dbg_sgn = nc.dram_tensor("dbg_sgn", [128, 2 * D], BF16, kind="ExternalOutput")
        dbg_alp = nc.dram_tensor("dbg_alp", [128, D], F32, kind="ExternalOutput")
        dbg_absl = nc.dram_tensor("dbg_absl", [128, D], BF16, kind="ExternalOutput")
        dbg_alrf = nc.dram_tensor("dbg_alrf", [1, D], F32, kind="ExternalOutput")
        dbg_alrb = nc.dram_tensor("dbg_alrb", [1, D], BF16, kind="ExternalOutput")
        dbg_otf = nc.dram_tensor("dbg_otf", [128, 16 * TOWN], BF16, kind="ExternalOutput")
    RG = [[0, 1], [2, 3], [4, 5], [6, 7]]

    with stack:
        s = Sched(nc, stack)
        pool = Pool(nc, stack)
        ctx = Ctx(nc, s, pool, stack)
        ident_sb = ctx.sbuf("ident_sb", [128, 128], BF16)
        onescol_sb = ctx.sbuf("onescol_sb", [128, 1], BF16)
        onesrow_sb = ctx.sbuf("onesrow_sb", [1, 128], BF16)
        toff_sb = ctx.sbuf("toff_sb", [1, 1], mybir.dt.int32)
        s.dma("sync", ident_sb[:, :], c_ident[:, :], sem="dC")
        s.dma("sync", onescol_sb[:, :], c_onescol[:, :], sem="dC")
        s.dma("sync", onesrow_sb[:, :], c_onesrow[:, :], sem="dC")
        ev_toff = s.dma("sync", toff_sb[:, :], toff[:, :], sem="dT")
        ctx.consts.update(ident=ident_sb, onescol=onescol_sb, onesrow=onesrow_sb)
        ctx.consts["ready"] = ("dC", s.cnt["dC"])
        onesrow = onesrow_sb

        def wo_phase(otf_src_dram, cc_ev, w_dram, bias_dram, res_dram, out_dram,
                     px, guard=(), dbg_otf=None):
            bias_row = None
            if bias_dram is not None:
                brf = ctx.sbuf(f"{px}brf", [1, D], F32)
                brb = ctx.sbuf(f"{px}brb", [1, D], BF16)
                e1b = s.dma("sync", brf[:, :], bias_dram[:, :],
                            deps=list(guard), sem="dC")
                ebb = s.op("scalar", lambda e: e.copy(brb[:, :], brf[:, :]), deps=[e1b])
                bias_row = brb
                bias_ev = ebb
            else:
                bias_ev = None
            otf_sb = ctx.sbuf(f"{px}otf", [128, 16 * TOWN], BF16)

            def dynload(q):
                def fn(e):
                    with e.register(f"{px}ro{q}") as ro:
                        e.reg_load(ro, toff_sb[0:1, 0:1])
                        off = e.snap(ro, min_val=0, max_val=1)
                        return e.dma_start(
                            out=otf_sb[:, :].rearrange(
                                "p (f q t) -> p q f t", q=2, t=512)[:, q],
                            in_=otf_src_dram[:, :, :].rearrange(
                                "s (f p) t -> s p f t", p=128)
                                [bass.ds(off * 2 + q, 1), :, :, :].squeeze(0))
                return fn

            ev_o0 = s.dyn_dma("gpsimd", dynload(0),
                              deps=[cc_ev[0], cc_ev[2], ev_toff], sem="dY")
            ev_o = s.dyn_dma("gpsimd", dynload(1),
                             deps=[cc_ev[1], cc_ev[3]], sem="dY")
            ev_os = [ev_o0, ev_o]
            if dbg_otf is not None:
                s.dma("scalar", dbg_otf[:, :], otf_sb[:, :], deps=[ev_o], sem="dO")
            wob = ctx.sbuf(f"{px}wob", [128, 16 * D], BF16)
            wf0 = ctx.sbuf(f"{px}wf0", [128, 2 * D], F32)
            wf1 = ctx.sbuf(f"{px}wf1", [128, 2 * D], F32)
            wfs = [wf0, wf1]
            wfree = [None, None]
            ev_conv = []
            for kb in range(8):
                wi = kb % 2
                evd = s.dma("sync" if wi == 0 else "scalar",
                            wfs[wi][:, :].rearrange("p (a c) -> p a c", a=2),
                            w_dram[kb * 256:(kb + 1) * 256, :]
                            .rearrange("(a p) c -> p a c", p=128),
                            deps=[wfree[wi]] + (list(guard) if kb < 2 else []),
                            sem="wA" if wi == 0 else "wB")
                if wi:
                    evc = s.op("vector", lambda e, kb=kb, wi=wi: e.tensor_copy(
                        wob[:, kb * 2 * D:(kb + 1) * 2 * D], wfs[wi][:, :]),
                        deps=[evd] + (list(guard) if kb < 2 else []))
                else:
                    evc = s.op("scalar", lambda e, kb=kb, wi=wi: e.copy(
                        wob[:, kb * 2 * D:(kb + 1) * 2 * D], wfs[wi][:, :]),
                        deps=[evd] + (list(guard) if kb < 2 else []))
                wfree[wi] = evc
                ev_conv.append(evc)
                ev_conv.append(evc)
            w_ready0 = [bias_ev] if bias_ev else []
            xr0 = ctx.sbuf(f"{px}xr0", [128, D], F32)
            xr1 = ctx.sbuf(f"{px}xr1", [128, D], F32)
            xo0 = ctx.sbuf(f"{px}xo0", [128, D], F32)
            xo1 = ctx.sbuf(f"{px}xo1", [128, D], F32)
            xrs, xos = [xr0, xr1], [xo0, xo1]
            xr_free = [None, None]
            xo_free = [None, None]
            for tc in range(TOWN // 128):
                xi = tc % 2
                evx = s.dma("scalar", xrs[xi][:, :], res_dram[tc * 128:(tc + 1) * 128, :],
                            deps=[xr_free[xi]] + (list(guard) if tc < 2 else []),
                            sem="xA" if xi == 0 else "xB")
                evs_os = []
                for os_ in range(4):
                    bidx, bank, bfree = pool.acquire()
                    mm = None
                    for k in range(16):
                        kdeps = [ev_conv[k]]
                        if k == 0:
                            kdeps += w_ready0 + [ev_os[tc // 4], bfree]
                        mm = s.op("tensor", lambda e, k=k, tc=tc, os_=os_, bank=bank: e.matmul(
                            bank[:, :], otf_sb[:, k * TOWN + tc * 128: k * TOWN + tc * 128 + 128],
                            wob[:, k * D + os_ * 512: k * D + os_ * 512 + 512],
                            start=(k == 0), stop=(k == 15 and bias_row is None)),
                            deps=kdeps)
                    if bias_row is not None:
                        mm = s.op("tensor", lambda e, bank=bank, os_=os_: e.matmul(
                            bank[:, :], onesrow[0:1, :],
                            bias_row[0:1, os_ * 512:os_ * 512 + 512],
                            start=False, stop=True), deps=[mm])
                    ev = s.op("vector", lambda e, bank=bank, os_=os_, xi=xi: e.tensor_add(
                        xos[xi][:, os_ * 512:os_ * 512 + 512], bank[:, :],
                        xrs[xi][:, os_ * 512:os_ * 512 + 512]),
                        deps=[mm, evx, xo_free[xi]])
                    pool.release(bidx, ev)
                    evs_os.append(ev)
                evd = s.dma("scalar", out_dram[tc * 128:(tc + 1) * 128, :],
                            xos[xi][:, :], deps=evs_os,
                            sem="sA" if xi == 0 else "sB")
                xo_free[xi] = evd
                xr_free[xi] = evd
            return [("sA", s.cnt["sA"]), ("sB", s.cnt["sB"])]

        stA = ExitStack()
        ctx.stack = stA
        bq_sb = ctx.sbuf("bq_sb", [128, 8], F32)
        bk_sb = ctx.sbuf("bk_sb", [128, 8], F32)
        bvf = ctx.sbuf("bvf", [1, 1024], F32)
        bvb = ctx.sbuf("bvb", [1, 1024], BF16)
        s.dma("sync", bq_sb[:, :], bqc[:, :], sem="dC")
        s.dma("sync", bk_sb[:, :], bkc[:, :], sem="dC")
        evb = s.dma("sync", bvf[:, :], bvr[:, :], sem="dC")
        s.op("scalar", lambda e: e.copy(bvb[:, :], bvf[:, :]), deps=[evb])
        ctx.consts["ready"] = ("dC", s.cnt["dC"])
        QT = ctx.sbuf("QT", [128, NH * S], BF16)
        KT = ctx.sbuf("KT", [128, NH * S], BF16)
        VT = ctx.sbuf("VT", [128, (S // 128) * 1024], BF16)
        stA1 = ExitStack()
        ctx.stack = stA1
        lnt1 = ctx.sbuf("lnt1", [128, 16 * S], BF16)
        stA1a = ExitStack()
        ctx.stack = stA1a
        xt0 = ctx.sbuf("xt0", [128, D], F32)
        xt1 = ctx.sbuf("xt1", [128, D], F32)
        lbs = [ctx.sbuf(f"lb{i}", [128, D], BF16) for i in range(4)]
        statb = ctx.sbuf("statb", [128, 16], F32)
        sqb = [ctx.sbuf("sqbAa", [128, D], BF16), ctx.sbuf("sqbAb", [128, D], BF16)]
        emit_ln_t(ctx, xfull, S // 128, lnt1, S, [xt0, xt1], lbs, statb, sqb)
        stA1a.close()
        ln_done = [s.now("tensor"), s.now("vector"), s.now("scalar")]
        stA1b = ExitStack()
        ctx.stack = stA1b
        vwf0 = ctx.sbuf("vwf0", [128, 8 * 512], F32)
        vwb = ctx.sbuf("vwb", [128, 16 * 512], BF16)
        emit_proj_tm(ctx, lnt1, S, S, wv, 1024, VT, 1024, [vwf0], vwb,
                     bias_row=bvb, ready=ln_done, tag="v")
        stA1b.close()
        v_done = [s.now("tensor")]
        stA1c = ExitStack()
        ctx.stack = stA1c
        qwf0 = ctx.sbuf("qwf0", [128, 16 * 128], F32)
        qwf1 = ctx.sbuf("qwf1", [128, 16 * 128], F32)
        qwb0 = ctx.sbuf("qwb0", [128, 16 * 128], BF16)
        qwb1 = ctx.sbuf("qwb1", [128, 16 * 128], BF16)
        emit_proj_fm(ctx, lnt1, S, S, wq, 1024, QT, S, [qwf0, qwf1], [qwb0, qwb1],
                     bias_cols=bq_sb, ready=ln_done + v_done, tag="q")
        emit_proj_fm(ctx, lnt1, S, S, wk, 1024, KT, S, [qwf0, qwf1], [qwb0, qwb1],
                     bias_cols=bk_sb, ready=ln_done + [s.now("tensor")], tag="k")
        stA1c.close()
        stA1.close()
        sched_dS_prev = [None]

        def dS_dma(eng, out, in_, deps=(), sem="dS"):
            ev = s.dma(eng, out, in_, deps=list(deps) + [sched_dS_prev[0]],
                       sem=sem)
            sched_dS_prev[0] = ev
            return ev

        stA2 = ExitStack()
        ctx.stack = stA2
        masks_sb = ctx.sbuf("masksA", [128, 4 * 512], BF16)
        evm = s.dma("sync", masks_sb[:, :], c_maskneg[:, :], deps=[s.now("tensor")],
                    sem="dC")
        ctx.consts["masks"] = masks_sb
        OT = ctx.sbuf("OT", [128, NH * S], BF16)
        esb = [ctx.sbuf(f"esb{j}", [128, 512], BF16) for j in range(32)]
        ready = [s.now("scalar"), s.now("vector"), evm]
        ccs1 = []
        sts1 = []

        def mid1(ev_half):
            g = len(ccs1)
            st = dS_dma("scalar",
                        e1_in[g, :, :].rearrange("(h p) t -> p h t", p=128),
                        OT[:, :].rearrange("p (h t) -> p h t", h=NH)
                        [:, :, g * 512:(g + 1) * 512],
                        deps=[ev_half])
            sts1.append(st)
            ccs1.append(s.cc(lambda e, g=g: e.collective_compute(
                "AllGather", ALU.bypass, replica_groups=RG,
                ins=[e1_in[g, :, :]], outs=[e1_out[g, :, :]]),
                f"cc1_{g}", deps=[st]))

        ev_a = emit_attention(ctx, QT, KT, VT, OT, lambda h: h, 1024, "softmax",
                              esb, ready, mid_cb=mid1)
        mid1(ev_a)
        ev_st1, cc1 = sts1[-1], ccs1[-1]
        stA2.close()
        stA.close()
        guardA = [ev_st1, s.now("tensor"), s.now("vector"), s.now("scalar")]

        stB_out = ExitStack()
        ctx.stack = stB_out
        lnt2 = ctx.sbuf("lnt2", [128, 16 * TOWN], BF16)
        stB = ExitStack()
        ctx.stack = stB
        sp1 = wo_phase(e1_out, ccs1, wo, bor, xres, x1_d, "b", guard=guardA,
                       dbg_otf=dbg_otf if dbg else None)
        stB.close()
        guardB = sp1 + [s.now("tensor"), s.now("vector"), s.now("scalar")]
        ctx.stack = stB_out
        hdn = ctx.sbuf("hdn", [128, NI * TOWN], BF16)
        gs_sb = ctx.sbuf("gs_sb", [128, NI], F32)
        us_sb = ctx.sbuf("us_sb", [128, NI], F32)
        ds_sb = ctx.sbuf("ds_sb", [128, D], F32)
        s.dma("sync", gs_sb[:, :], gsc[:, :], deps=guardB, sem="dC")
        s.dma("sync", us_sb[:, :], usc[:, :], sem="dC")
        s.dma("sync", ds_sb[:, :], dsb[:, :], sem="dC")
        c_ready_f = ("dC", s.cnt["dC"])
        stB2 = ExitStack()
        ctx.stack = stB2
        xt0 = ctx.sbuf("xtB0", [128, D], F32)
        xt1 = ctx.sbuf("xtB1", [128, D], F32)
        lbs = [ctx.sbuf(f"lbB{i}", [128, D], BF16) for i in range(4)]
        statb = ctx.sbuf("statB", [128, 16], F32)
        sqb = [ctx.sbuf("sqbBa", [128, D], BF16), ctx.sbuf("sqbBb", [128, D], BF16)]
        emit_ln_t(ctx, x1_d, TOWN // 128, lnt2, TOWN, [xt0, xt1], lbs, statb, sqb,
                  start_deps=sp1 + [s.now("tensor")])
        stB2.close()

        ln2_done = [s.now("tensor"), s.now("vector"), s.now("scalar")]
        if dbg:
            s.dma("scalar", dbg_lnt2[:, :], lnt2[:, :], deps=ln2_done, sem="dO")
        stC1 = ExitStack()
        ctx.stack = stC1
        nk = 16
        wg_f = ctx.sbuf("wg_f", [128, nk * 128], F32)
        wg_b = ctx.sbuf("wg_b", [128, nk * 128], BF16)
        wu_f = ctx.sbuf("wu_f", [128, nk * 128], F32)
        wu_b = ctx.sbuf("wu_b", [128, nk * 128], BF16)
        sg = ctx.sbuf("sg", [128, TOWN], F32)
        wg_free = wu_free = None
        sg_free = None
        for ic in range(NI):
            ev_dg = gather_w(ctx, gwt, ic * 128, 128, wg_f,
                             [wg_free] + (ln2_done if ic == 0 else []), sem="wA")
            ev_wg = conv_w(ctx, wg_f, wg_b, nk * 128, "scalar",
                           [ev_dg] + (ln2_done if ic == 0 else []))
            ev_du = gather_w(ctx, uwt, ic * 128, 128, wu_f,
                             [wu_free] + (ln2_done if ic == 0 else []), sem="uA")
            ev_wu = conv_w(ctx, wu_f, wu_b, nk * 128, "vector",
                           [ev_du] + (ln2_done if ic == 0 else []))
            sg_evs = []
            for ts_ in range(TOWN // 512):
                bidx, bank, bfree = pool.acquire()
                mm = None
                for k in range(nk):
                    mm = s.op("tensor", lambda e, k=k, ts_=ts_, bank=bank: e.matmul(
                        bank[:, :], wg_b[:, k * 128:(k + 1) * 128],
                        lnt2[:, k * TOWN + ts_ * 512: k * TOWN + ts_ * 512 + 512],
                        start=(k == 0), stop=(k == nk - 1)),
                        deps=([ev_wg, bfree] + ln2_done) if k == 0 else [])
                ev_sg = s.op("scalar", lambda e, ic=ic, ts_=ts_, bank=bank: e.activation(
                    sg[:, ts_ * 512:ts_ * 512 + 512], bank[:, :], AF.Silu,
                    scale=gs_sb[:, ic:ic + 1]), deps=[mm, sg_free, c_ready_f])
                pool.release(bidx, ev_sg)
                sg_evs.append(ev_sg)
                wg_free = mm
            hv = []
            for ts_ in range(TOWN // 512):
                bidx, bank, bfree = pool.acquire()
                mm = None
                for k in range(nk):
                    mm = s.op("tensor", lambda e, k=k, ts_=ts_, bank=bank: e.matmul(
                        bank[:, :], wu_b[:, k * 128:(k + 1) * 128],
                        lnt2[:, k * TOWN + ts_ * 512: k * TOWN + ts_ * 512 + 512],
                        start=(k == 0), stop=(k == nk - 1)),
                        deps=[ev_wu, bfree] if k == 0 else [])
                col0 = ic * TOWN + ts_ * 512
                ev_h = s.op("vector", lambda e, ic=ic, ts_=ts_, bank=bank, col0=col0: e.scalar_tensor_tensor(
                    hdn[:, col0:col0 + 512], bank[:, :], us_sb[:, ic:ic + 1],
                    sg[:, ts_ * 512:ts_ * 512 + 512], ALU.mult, ALU.mult),
                    deps=[mm, sg_evs[ts_], c_ready_f] + (guardB if ic == 0 else []))
                pool.release(bidx, ev_h)
                hv.append(ev_h)
                wu_free = mm
            sg_free = hv[-1]
        stC1.close()
        hdn_ready = [s.now("vector")]
        if dbg:
            s.dma("scalar", dbg_hdn[:, :], hdn[:, 0:4 * TOWN], deps=hdn_ready, sem="dO")
        stC2 = ExitStack()
        ctx.stack = stC2
        dw_f0 = ctx.sbuf("dw_f0", [128, 4 * 512], F32)
        dw_f1 = ctx.sbuf("dw_f1", [128, 4 * 512], F32)
        dbig = ctx.sbuf("dbig", [128, NI * 512], BF16)
        xr0 = ctx.sbuf("xrC0", [128, 512], F32)
        xr1 = ctx.sbuf("xrC1", [128, 512], F32)
        xo0 = ctx.sbuf("xoC0", [128, 512], F32)
        xo1 = ctx.sbuf("xoC1", [128, 512], F32)
        dview = dwt[:, :].rearrange("(k p) t -> p k t", p=128)
        dwf = [dw_f0, dw_f1]
        xrs, xos = [xr0, xr1], [xo0, xo1]
        gu_done = s.op("sync", lambda e: None, deps=[
            s.now("tensor"), s.now("vector"), s.now("scalar")])
        dwf_free = [gu_done, gu_done]
        dbig_free = gu_done
        xr_free = [gu_done, gu_done]
        xo_free = [gu_done, gu_done]
        for os_ in range(4):
            convs = []
            for bi, kb0 in enumerate(range(0, NI, 4)):
                nkb = min(4, NI - kb0)
                wi = bi % 2
                evd = s.dma("sync", dwf[wi][:, 0:nkb * 512].rearrange("p (k c) -> p k c", k=nkb),
                            dview[:, kb0:kb0 + nkb, os_ * 512:os_ * 512 + 512],
                            deps=[dwf_free[wi]], sem="wA" if wi == 0 else "wB")
                evc = s.op("vector" if wi else "scalar",
                           (lambda e, wi=wi, kb0=kb0, nkb=nkb: e.tensor_copy(
                               dbig[:, kb0 * 512:(kb0 + nkb) * 512],
                               dwf[wi][:, 0:nkb * 512])) if wi else
                           (lambda e, wi=wi, kb0=kb0, nkb=nkb: e.copy(
                               dbig[:, kb0 * 512:(kb0 + nkb) * 512],
                               dwf[wi][:, 0:nkb * 512])),
                           deps=[evd, dbig_free])
                convs.append(evc)
                dwf_free[wi] = evc
            for tc in range(TOWN // 128):
                xi = tc % 2
                evx = s.dma("scalar", xrs[xi][:, :],
                            x1_d[tc * 128:(tc + 1) * 128, os_ * 512:os_ * 512 + 512],
                            deps=[xr_free[xi]], sem="xA" if xi == 0 else "xB")
                bidx, bank, bfree = pool.acquire()
                mm = None
                for k in range(NI):
                    kdeps = [convs[k // 4]]
                    if k == 0:
                        kdeps += hdn_ready + [bfree]
                    mm = s.op("tensor", lambda e, k=k, tc=tc, bank=bank: e.matmul(
                        bank[:, :], hdn[:, k * TOWN + tc * 128: k * TOWN + tc * 128 + 128],
                        dbig[:, k * 512:(k + 1) * 512],
                        start=(k == 0), stop=(k == NI - 1)), deps=kdeps)
                if tc == TOWN // 128 - 1:
                    dbig_free = mm
                e1_ = s.op("vector", lambda e, bank=bank, os_=os_, xi=xi, xos=xos: e.scalar_tensor_tensor(
                    xos[xi][:, :], bank[:, :], 1.0,
                    ds_sb[:, os_ * 512:os_ * 512 + 512], ALU.bypass, ALU.mult),
                    deps=[mm, xo_free[xi], c_ready_f])
                e2_ = s.op("vector", lambda e, xi=xi, xos=xos, xrs=xrs: e.tensor_add(
                    xos[xi][:, :], xos[xi][:, :], xrs[xi][:, :]),
                    deps=[evx], strict=[e1_])
                pool.release(bidx, e2_)
                evd2 = s.dma("scalar", x2_d[tc * 128:(tc + 1) * 128, os_ * 512:os_ * 512 + 512],
                             xos[xi][:, :], deps=[e2_],
                             sem="sA" if xi == 0 else "sB")
                xo_free[xi] = evd2
                xr_free[xi] = evd2
        stC2.close()
        stB_out.close()
        guardC = [("sA", s.cnt["sA"]), ("sB", s.cnt["sB"]),
                  s.now("tensor"), s.now("vector"), s.now("scalar")]

        stD = ExitStack()
        ctx.stack = stD
        cos_sb = ctx.sbuf("cos_sb", [128, S], F32)
        srot_sb = ctx.sbuf("srot_sb", [128, S], F32)
        prot_sb = ctx.sbuf("prot_sb", [128, 128], BF16)
        s.dma("sync", cos_sb[:, :], c_cos[:, :], deps=guardC, sem="dC")
        s.dma("sync", srot_sb[:, :], c_srot[:, :], sem="dC")
        s.dma("sync", prot_sb[:, :], c_prot[:, :], sem="dC")
        ctx.consts.update(cos=cos_sb, srot=srot_sb, prot=prot_sb)
        ctx.consts["ready"] = ("dC", s.cnt["dC"])
        QT1 = ctx.sbuf("QT1", [128, NH * S], BF16)
        KT1 = ctx.sbuf("KT1", [128, 2 * S], BF16)
        VT1 = ctx.sbuf("VT1", [128, (S // 128) * 256], BF16)
        stD1 = ExitStack()
        ctx.stack = stD1
        lnt3 = ctx.sbuf("lnt3", [128, 16 * S], BF16)
        stD1a = ExitStack()
        ctx.stack = stD1a
        lnt3hs = [ctx.sbuf(f"lnt3h{g}", [128, 16 * 256], BF16) for g in range(4)]
        xt0 = ctx.sbuf("xtD0", [128, D], F32)
        xt1 = ctx.sbuf("xtD1", [128, D], F32)
        lbs = [ctx.sbuf(f"lbD{i}", [128, D], BF16) for i in range(4)]
        statb = ctx.sbuf("statD", [128, 16], F32)
        sqb = [ctx.sbuf("sqbDa", [128, D], BF16), ctx.sbuf("sqbDb", [128, D], BF16)]
        ccs2 = []
        for g in range(4):
            lnow = guardC if g == 0 else [s.now("vector"), s.now("scalar")]
            emit_ln_t(ctx, x2_d[g * 256:(g + 1) * 256, :], 2, lnt3hs[g], 256,
                      [xt0, xt1], lbs, statb, sqb,
                      start_deps=lnow)
            lng = [s.now("tensor"), s.now("vector"), s.now("scalar")]
            stg = dS_dma("scalar",
                         e2_in[g, :, :].rearrange("(f p) t -> p f t", p=128),
                         lnt3hs[g][:, :].rearrange("p (f t) -> p f t", f=16),
                         deps=lng)
            ccs2.append(s.cc(lambda e, g=g: e.collective_compute(
                "AllGather", ALU.bypass, replica_groups=RG,
                ins=[e2_in[g, :, :]], outs=[e2_out[g, :, :]]),
                f"cc2_{g}", deps=[stg]))
        ev_st2 = s.now("scalar")
        cc2 = ccs2[-1]
        stD1a.close()
        rbs = []
        for g in range(4):
            rbs.append(s.dma(
                "sync",
                lnt3[:, :].rearrange("p (f t) -> p f t", f=16)
                [:, :, g * 256:(g + 1) * 256],
                e2_out[g, 0:D, :].rearrange("(f p) t -> p f t", p=128),
                deps=[ccs2[g], s.now("scalar"), s.now("vector"),
                      rbs[-1] if rbs else None], sem="dR"))
            rbs.append(s.dma(
                "sync",
                lnt3[:, :].rearrange("p (f t) -> p f t", f=16)
                [:, :, 1024 + g * 256:1024 + (g + 1) * 256],
                e2_out[g, D:2 * D, :].rearrange("(f p) t -> p f t", p=128),
                deps=[rbs[-1]], sem="dR"))
        ln3f_done = rbs[-2:]
        rb1, rb2 = rbs[0], rbs[1]
        stD1b = ExitStack()
        ctx.stack = stD1b
        vwf0 = ctx.sbuf("vwfD", [128, 8 * 256], F32)
        vwb = ctx.sbuf("vwbD", [128, 16 * 256], BF16)
        emit_proj_tm(ctx, lnt3, S, S, wv1, 256, VT1, 256, [vwf0], vwb,
                     ready=[ev_st2, s.now("tensor"), s.now("vector")],
                     ready_mm=ln3f_done, tag="v1")
        stD1b.close()
        stD1c = ExitStack()
        ctx.stack = stD1c
        qwf0 = ctx.sbuf("qwfD0", [128, 16 * 128], F32)
        qwf1 = ctx.sbuf("qwfD1", [128, 16 * 128], F32)
        qwb0 = ctx.sbuf("qwbD0", [128, 16 * 128], BF16)
        qwb1 = ctx.sbuf("qwbD1", [128, 16 * 128], BF16)
        rp0 = ctx.sbuf("rp0", [128, 512], F32)
        rp1 = ctx.sbuf("rp1", [128, 512], BF16)
        rp2 = ctx.sbuf("rp2", [128, 512], F32)
        rp3 = ctx.sbuf("rp3", [128, 512], F32)
        rope = {"bufs": (rp0, rp1, rp2, rp3)}
        v1_done = [s.now("tensor")]
        emit_proj_fm(ctx, lnt3, S, S, wq1, 1024, QT1, S, [qwf0, qwf1], [qwb0, qwb1],
                     rope=rope, ready=ln3f_done + v1_done, tag="q1")
        emit_proj_fm(ctx, lnt3, S, S, wk1, 256, KT1, S, [qwf0, qwf1], [qwb0, qwb1],
                     rope=rope, ready=ln3f_done + [s.now("tensor")], tag="k1")
        stD1c.close()
        stD1.close()
        stD2 = ExitStack()
        ctx.stack = stD2
        masksD = ctx.sbuf("masksD", [128, 4 * 512], BF16)
        evmD = s.dma("sync", masksD[:, :], c_maskneg[:, :], deps=[s.now("tensor")],
                     sem="dC")
        ctx.consts["masks"] = masksD
        OT1 = ctx.sbuf("OT1", [128, NH * S], BF16)
        esb1 = [ctx.sbuf(f"esbD{j}", [128, 512], BF16) for j in range(32)]
        ready1 = [s.now("scalar"), s.now("vector"), evmD]
        ccs3 = []
        sts3 = []

        def mid3(ev_half):
            g = len(ccs3)
            st = dS_dma("scalar",
                        e3_in[g, :, :].rearrange("(h p) t -> p h t", p=128),
                        OT1[:, :].rearrange("p (h t) -> p h t", h=NH)
                        [:, :, g * 512:(g + 1) * 512],
                        deps=[ev_half])
            sts3.append(st)
            ccs3.append(s.cc(lambda e, g=g: e.collective_compute(
                "AllGather", ALU.bypass, replica_groups=RG,
                ins=[e3_in[g, :, :]], outs=[e3_out[g, :, :]]),
                f"cc3_{g}", deps=[st]))

        ev_a1 = emit_attention(ctx, QT1, KT1, VT1, OT1, lambda h: h // 4, 256,
                               "linear", esb1, ready1, tag="a1", mid_cb=mid3)
        mid3(ev_a1)
        ev_st3, cc3 = sts3[-1], ccs3[-1]
        stD2.close()
        stD.close()

        guardD = [ev_st3, s.now("tensor"), s.now("vector"), s.now("scalar")]
        stE = ExitStack()
        ctx.stack = stE
        sp3 = wo_phase(e3_out, ccs3, wo1, None, x2_d, x3_d, "e", guard=guardD)
        stE.close()
        guardE = sp3 + [s.now("tensor"), s.now("vector"), s.now("scalar")]

        stF = ExitStack()
        ctx.stack = stF
        b1_sb = ctx.sbuf("b1_sb", [128, D], F32)
        ev_b1 = s.dma("sync", b1_sb[:, :], b1b[:, :], deps=guardE, sem="dC")
        lnt4 = ctx.sbuf("lnt4", [128, 16 * TOWN], BF16)
        signT = ctx.sbuf("signT", [128, 16 * D], BF16)
        alphab = ctx.sbuf("alphab", [128, D], F32)
        stF1 = ExitStack()
        ctx.stack = stF1
        xt0 = ctx.sbuf("xtF0", [128, D], F32)
        xt1 = ctx.sbuf("xtF1", [128, D], F32)
        lbs = [ctx.sbuf(f"lbF{i}", [128, D], BF16) for i in range(4)]
        statb = ctx.sbuf("statF", [128, 16], F32)
        sqb = [ctx.sbuf("sqbFa", [128, D], BF16), ctx.sbuf("sqbFb", [128, D], BF16)]
        wsl = [ctx.sbuf("wsl0", [128, D], F32), ctx.sbuf("wsl1", [128, D], F32)]
        absl = [ctx.sbuf("absl0", [128, D], BF16), ctx.sbuf("absl1", [128, D], BF16)]
        alr_f = ctx.sbuf("alr_f", [1, D], F32)
        alr_b = ctx.sbuf("alr_b", [1, D], BF16)
        onescol = ctx.consts["onescol"]
        emit_ln_t(ctx, x3_d, TOWN // 128, lnt4, TOWN, [xt0, xt1], lbs, statb, sqb,
                  start_deps=guardE)
        ln_done = [s.now("vector"), s.now("scalar")]
        ev_zero = s.op("vector", lambda e: e.memset(alr_f[0:1, :], 0.0),
                       deps=guardE)
        acc_prev = ev_zero
        sign_evs = []
        wsl_free = [[], []]
        absl_free = [None, None]
        for k in range(16):
            sl = k % 2
            evw = s.dma("sync" if sl == 0 else "scalar",
                        wsl[sl][:, :], w1t[k * 128:(k + 1) * 128, :],
                        deps=wsl_free[sl] + (guardE if k < 2 else []),
                        sem="wA" if sl == 0 else "wC")
            evsg = s.op("scalar", lambda e, k=k, sl=sl: e.activation(
                signT[:, k * D:(k + 1) * D], wsl[sl][:, :], AF.Sign), deps=[evw])
            sign_evs.append(evsg)
            eva = s.op("vector", lambda e, k=k, sl=sl: e.tensor_tensor(
                absl[sl][:, :], wsl[sl][:, :], signT[:, k * D:(k + 1) * D],
                ALU.mult), deps=[evsg, absl_free[sl]])
            wsl_free[sl] = [evsg, eva]
            mm_a = None
            for q4 in range(4):
                bidx, bank, bfree = pool.acquire()
                mm_a = s.op("tensor", lambda e, bank=bank, sl=sl, q4=q4: e.matmul(
                    bank[0:1, :], onescol[:, 0:1],
                    absl[sl][:, q4 * 512:(q4 + 1) * 512],
                    start=True, stop=True), deps=[eva, bfree])
                e_acc = s.op("vector", lambda e, bank=bank, q4=q4: e.tensor_add(
                    alr_f[0:1, q4 * 512:(q4 + 1) * 512],
                    alr_f[0:1, q4 * 512:(q4 + 1) * 512], bank[0:1, :]),
                    deps=[mm_a], strict=[acc_prev])
                acc_prev = e_acc
                pool.release(bidx, e_acc)
            absl_free[sl] = mm_a
        e1a = s.op("vector", lambda e: e.tensor_scalar_mul(
            alr_f[0:1, :], alr_f[0:1, :], 1.0 / D), strict=[acc_prev])
        e2a = s.op("scalar", lambda e: e.copy(alr_b[0:1, :], alr_f[0:1, :]),
                   deps=[e1a])
        for q4 in range(4):
            bidx, bank, bfree = pool.acquire()
            e3a = s.op("tensor", lambda e, q4=q4, bank=bank: e.matmul(
                bank[:, :], onesrow[0:1, :], alr_b[0:1, q4 * 512:(q4 + 1) * 512],
                start=True, stop=True), deps=[e2a, bfree])
            e4a = s.op("scalar", lambda e, q4=q4, bank=bank: e.copy(
                alphab[:, q4 * 512:(q4 + 1) * 512], bank[:, :]), deps=[e3a])
            pool.release(bidx, e4a)
        f2_done = s.op("sync", lambda e: None, deps=[
            s.now("tensor"), s.now("vector"), s.now("scalar"), s.now("gpsimd")])
        stF1.close()
        alpha_ready = s.now("scalar")
        ready_mm = ln_done
        stF3 = ExitStack()
        ctx.stack = stF3
        xo0 = ctx.sbuf("xoF0", [128, 512], F32)
        xo1 = ctx.sbuf("xoF1", [128, 512], F32)
        xr0 = ctx.sbuf("xrF0", [128, 512], F32)
        xr1 = ctx.sbuf("xrF1", [128, 512], F32)
        xosF, xrsF = [xo0, xo1], [xr0, xr1]
        xo_free = [None, None]
        xr_free = [f2_done, f2_done]
        nn = 0
        for os_ in range(4):
            for tc in range(TOWN // 128):
                bidx, bank, bfree = pool.acquire()
                mm = None
                for k in range(16):
                    kdeps = [sign_evs[k]]
                    if k == 0:
                        kdeps += ready_mm + [bfree]
                    mm = s.op("tensor", lambda e, k=k, tc=tc, os_=os_, bank=bank: e.matmul(
                        bank[:, :], lnt4[:, k * TOWN + tc * 128: k * TOWN + tc * 128 + 128],
                        signT[:, k * D + os_ * 512: k * D + os_ * 512 + 512],
                        start=(k == 0), stop=(k == 15)), deps=kdeps)
                xi = nn % 2
                nn += 1
                evx = s.dma("sync", xrsF[xi][:, :],
                            x3_d[tc * 128:(tc + 1) * 128, os_ * 512:os_ * 512 + 512],
                            deps=[xr_free[xi]], sem="xA" if xi == 0 else "xB")
                e1_ = s.op("vector", lambda e, bank=bank, os_=os_, xi=xi, xosF=xosF: e.scalar_tensor_tensor(
                    xosF[xi][:, :], bank[:, :], 1.0,
                    alphab[:, os_ * 512:os_ * 512 + 512], ALU.bypass, ALU.mult),
                    deps=[mm, xo_free[xi], alpha_ready])
                e2_ = s.op("vector", lambda e, xi=xi, xosF=xosF, xrsF=xrsF: e.tensor_add(
                    xosF[xi][:, :], xosF[xi][:, :], xrsF[xi][:, :]),
                    deps=[evx], strict=[e1_])
                e3_ = s.op("vector", lambda e, os_=os_, xi=xi, xosF=xosF: e.tensor_add(
                    xosF[xi][:, :], xosF[xi][:, :],
                    b1_sb[:, os_ * 512:os_ * 512 + 512]),
                    deps=[ev_b1], strict=[e2_])
                pool.release(bidx, e3_)
                evd = s.dma("scalar", xout[tc * 128:(tc + 1) * 128, os_ * 512:os_ * 512 + 512],
                            xosF[xi][:, :], deps=[e3_],
                            sem="sA" if xi == 0 else "sB")
                xo_free[xi] = evd
                xr_free[xi] = e2_
        stF3.close()
        stF.close()
        if dbg:
            pass
        s.op("sync", lambda e: None, deps=[
            ("dO", s.cnt["dO"]), ("sA", s.cnt["sA"]), ("sB", s.cnt["sB"])])
        s.finalize()
    return nc


def kernel(**inputs):
    ii = {k: np.asarray(v) for k, v in inputs.items()}
    X = np.ascontiguousarray(ii["L0_attn_X"])
    cR = _consts(rope=True)
    nc = build_fused()
    gwt = np.ascontiguousarray(ii["L0_ffn_gate_w"].T)
    uwt = np.ascontiguousarray(ii["L0_ffn_up_w"].T)
    dwt = np.ascontiguousarray(ii["L0_ffn_down_w"].T)
    gsc = np.ascontiguousarray(ii["L0_ffn_gate_s"].reshape(NI, 128).T)
    usc = np.ascontiguousarray(ii["L0_ffn_up_s"].reshape(NI, 128).T)
    dsb = np.ascontiguousarray(np.broadcast_to(ii["L0_ffn_down_s"][None, :], (128, D)))
    wq1 = np.ascontiguousarray(ii["L1_attn_Wq"].T)
    wk1 = np.ascontiguousarray(ii["L1_attn_Wk"].T)
    wv1 = np.ascontiguousarray(ii["L1_attn_Wv"].T)
    wo1 = np.ascontiguousarray(ii["L1_attn_Wo"].T)
    w1t = np.ascontiguousarray(ii["L1_ffn_weight"].T)
    b1b = np.ascontiguousarray(np.broadcast_to(ii["L1_ffn_bias"][None, :], (128, D)))
    bor = np.ascontiguousarray(ii["L0_attn_bo"].reshape(1, D))
    maps = _pc(lambda b, r: dict(
        xfull=X[b],
        xres=np.ascontiguousarray(X[b, r * 1024:(r + 1) * 1024]),
        toff=np.array([[r]], dtype=np.int32),
        wq=np.ascontiguousarray(ii["L0_attn_Wq"][:, r * 1024:(r + 1) * 1024]),
        wk=np.ascontiguousarray(ii["L0_attn_Wk"][:, r * 1024:(r + 1) * 1024]),
        wv=np.ascontiguousarray(ii["L0_attn_Wv"][:, r * 1024:(r + 1) * 1024]),
        bqc=np.ascontiguousarray(ii["L0_attn_bq"][r * 1024:(r + 1) * 1024].reshape(8, 128).T),
        bkc=np.ascontiguousarray(ii["L0_attn_bk"][r * 1024:(r + 1) * 1024].reshape(8, 128).T),
        bvr=np.ascontiguousarray(ii["L0_attn_bv"][r * 1024:(r + 1) * 1024].reshape(1, 1024)),
        wo=ii["L0_attn_Wo"], bor=bor,
        gwt=gwt, uwt=uwt, dwt=dwt, gsc=gsc, usc=usc, dsb=dsb,
        wq1=np.ascontiguousarray(wq1[:, r * 1024:(r + 1) * 1024]),
        wk1=np.ascontiguousarray(wk1[:, r * 256:(r + 1) * 256]),
        wv1=np.ascontiguousarray(wv1[:, r * 256:(r + 1) * 256]),
        wo1=wo1, w1t=w1t, b1b=b1b,
        **cR))
    res = _run(nc, maps)
    out = np.empty((B, S, D), dtype=np.float32)
    for b in range(B):
        out[b, 0:1024] = res[2 * b]["xout"]
        out[b, 1024:2048] = res[2 * b + 1]["xout"]
    return out
